# version 01
# baseline (speedup 1.0000x reference)
import numpy as np
import ml_dtypes
import concourse.bass as bass
import concourse.mybir as mybir
from concourse.bass_utils import run_bass_kernel_spmd
from contextlib import ExitStack

F32 = mybir.dt.float32
BF16 = mybir.dt.bfloat16
I32 = mybir.dt.int32
AF = mybir.ActivationFunctionType
ALU = mybir.AluOpType
AX = mybir.AxisListType

NEG = -1.0e38


class CFG:
    def __init__(self, nslot=64, depth=6, qg=5, mb=4, nbatch=4, ncores=8):
        self.NSLOT = nslot
        self.DEPTH = depth
        self.NSWEEP = (nslot + 6 * depth) // 2
        assert 2 * self.NSWEEP == nslot + 6 * depth
        self.ROT = nslot - self.NSWEEP
        assert self.ROT % 2 == 0 and self.ROT > 0
        self.QG = qg
        assert self.NSWEEP % qg == 0 and qg <= 8
        self.MB = mb
        assert nslot % mb == 0 and mb <= 4
        self.T = nslot * 128
        self.D = 1024
        self.H = 16
        self.DH = 64
        self.FFH = 2730
        self.FF2 = 5460
        self.NB = nbatch
        self.NC = ncores



def _rope_tables(T, DH, WINDOW=256):
    half = np.arange(0, DH, 2, dtype=np.float64)
    inv_freq = 1.0 / 10000.0 ** (half / DH)
    t = np.arange(T, dtype=np.float64)
    freqs = t[:, None] * inv_freq[None, :]
    freqs = np.concatenate([freqs, freqs], -1)
    base = (half + 0.4 * DH) / (1.4 * DH)
    bs = np.concatenate([base, base])
    p = (t - 256.0 * np.floor(t / 256.0)) / (WINDOW // 2)
    scale = bs[None, :] ** p[:, None]
    cos, sin = np.cos(freqs), np.sin(freqs)
    return (np.concatenate([cos * scale, sin * scale, cos / scale, sin / scale],
                           axis=1).astype(np.float32),
            (bs ** 2.0).astype(np.float32))


def _masks(cfg, perm):
    n = cfg.NSWEEP
    out = np.zeros((n, 128, 384), dtype=np.float32)
    q = np.arange(128)[:, None]
    k = np.arange(128)[None, :]
    up = (k < q)
    lo = (k > q)
    for j in range(n):
        a = perm[j]
        m = out[j]
        if a - 2 < 0:
            m[:, 0:128] = NEG
        else:
            m[:, 0:128][up] = NEG
        if a - 1 < 0:
            m[:, 128:256] = NEG
        m[:, 256:384][lo] = NEG
    return out


def _kt(w, k=128):
    K, N = w.shape
    nk = (K + k - 1) // k
    pad = nk * k - K
    if pad:
        w = np.concatenate([w, np.zeros((pad, N), w.dtype)], 0)
    return np.ascontiguousarray(
        w.reshape(nk, k, N).transpose(1, 0, 2).reshape(k, nk * N))


def stage_core(inputs, cfg, b, h):
    c = cfg
    rot = 0 if h == 0 else c.ROT
    perm = (np.arange(c.NSLOT) + rot) % c.NSLOT
    x = np.asarray(inputs["L0_attn_x"][b], dtype=np.float32)
    xp = np.ascontiguousarray(x.reshape(c.NSLOT, 128, c.D)[perm].reshape(c.T, c.D))
    rope, b2 = _rope_tables(c.T, c.DH)
    rope_p = rope.reshape(c.NSLOT, 128, 256)[perm].reshape(c.T, 256)
    d = {
        "x": xp,
        "rope": rope_p,
        "b2col": np.ascontiguousarray(b2.reshape(c.DH, 1)),
        "masks": _masks(c, perm).reshape(c.NSWEEP * 128, 384).astype(ml_dtypes.bfloat16),
        "ident": np.eye(128, dtype=ml_dtypes.bfloat16),
        "wspinT": _kt(np.ascontiguousarray(
            np.asarray(inputs["L0_attn_W_spin"], np.float32).T)),
        "wjT": _kt(np.ascontiguousarray(
            np.asarray(inputs["L0_attn_W_J"], np.float32).T)),
        "wgT": _kt(np.ascontiguousarray(
            np.asarray(inputs["L0_attn_W_global"], np.float32).T)),
        "bspin": np.broadcast_to(
            np.asarray(inputs["L0_attn_b_spin"], np.float32), (128, c.D)).copy(),
        "beta": np.broadcast_to(
            np.asarray(inputs["L0_attn_beta"], np.float32).reshape(1, 1), (128, 1)).copy(),
    }
    for l in range(c.DEPTH):
        d[f"wqkv{l}"] = _kt(np.asarray(inputs["L0_ffn_Wqkv"][l], np.float32))
        d[f"wo{l}"] = _kt(np.asarray(inputs["L0_ffn_Wo"][l], np.float32))
        W1 = np.asarray(inputs["L0_ffn_W1"][l], np.float32)
        d[f"w1a{l}"] = _kt(np.ascontiguousarray(
            np.concatenate([W1[:, 0:1536], W1[:, 2730:4266]], 1)))
        d[f"w1b{l}"] = _kt(np.ascontiguousarray(
            np.concatenate([W1[:, 1536:2730], W1[:, 4266:5460]], 1)))
        w2 = np.asarray(inputs["L0_ffn_W2"][l], np.float32)
        b2r = np.asarray(inputs["L0_ffn_b2"][l], np.float32)[None, :]
        d[f"w2{l}"] = _kt(np.concatenate([w2, b2r], 0))
        d[f"b1_{l}"] = np.ascontiguousarray(
            np.asarray(inputs["L0_ffn_b1"][l], np.float32)[None, :])
        d[f"lng{l}"] = np.ascontiguousarray(
            np.asarray(inputs["L0_ffn_ln_g"][l], np.float32).reshape(8, 128).T)
        d[f"lnb{l}"] = np.ascontiguousarray(
            np.asarray(inputs["L0_ffn_ln_b"][l], np.float32).reshape(8, 128).T)
        qsl = np.asarray(inputs["L0_ffn_q_scale"][l], np.float32)
        ksl = np.asarray(inputs["L0_ffn_k_scale"][l], np.float32)
        d[f"qsc{l}"] = np.broadcast_to(qsl, (128, c.DH)).copy()
        d[f"ksc{l}"] = np.broadcast_to(ksl, (128, c.DH)).copy()
        d[f"qscr{l}"] = np.broadcast_to(np.roll(qsl, -32), (128, c.DH)).copy()
        d[f"kscr{l}"] = np.broadcast_to(np.roll(ksl, -32), (128, c.DH)).copy()
    return d



class Chan:
    __slots__ = ("name", "n", "step", "sem")

    def __init__(self, name, step=1):
        self.name, self.n, self.step, self.sem = name, 0, step, None

    def push(self):
        self.n += self.step
        return self.n


class Planner:
    ENGINES = ("pe", "dve", "act", "sp", "gp")

    def __init__(self):
        self.streams = {e: [] for e in self.ENGINES}
        self.chans = {}

    def chan(self, name, step=1):
        if name not in self.chans:
            self.chans[name] = Chan(name, step)
        return self.chans[name]

    def op(self, eng, fn, waits=(), inc=None, drain=False):
        v = inc.push() if inc is not None else None
        ws = [(ch, val) for (ch, val) in waits if val > 0]
        self.streams[eng].append((fn, ws, inc, drain))
        return (inc, v) if inc is not None else None

    def emit(self, nc, stack):
        for ch in self.chans.values():
            ch.sem = stack.enter_context(nc.semaphore(ch.name))
        block = stack.enter_context(nc.Block())

        def run(ename):
            def body(e):
                for fn, waits, inc, drain in self.streams[ename]:
                    for (ch, val) in waits:
                        e.wait_ge(ch.sem, val)
                    if drain:
                        e.drain()
                    ins = fn(e)
                    if inc is not None:
                        if ins is None:
                            ins = e.nop()
                        ins.then_inc(inc.sem, inc.step)
            return body

        block.tensor(run("pe"))
        block.vector(run("dve"))
        block.scalar(run("act"))
        block.sync(run("sp"))
        block.gpsimd(run("gp"))


def brd(ap, reps, inner):
    return ap.rearrange("p (a d) -> p a d", a=1).broadcast_to([128, reps, inner])



def build_program(cfg):
    c = cfg
    D, H, DH, FFH, FF2 = c.D, c.H, c.DH, c.FFH, c.FF2
    NS, NW, QG, MB, DEPTH = c.NSLOT, c.NSWEEP, c.QG, c.MB, c.DEPTH
    QW, MW = QG * 128, MB * 128

    nc = bass.Bass("TRN2", target_bir_lowering=False)

    dx = nc.dram_tensor("x", [c.T, D], F32, kind="ExternalInput").ap()
    drope = nc.dram_tensor("rope", [c.T, 256], F32, kind="ExternalInput").ap()
    db2col = nc.dram_tensor("b2col", [DH, 1], F32, kind="ExternalInput").ap()
    dmask = nc.dram_tensor("masks", [NW * 128, 384], BF16, kind="ExternalInput").ap()
    dident = nc.dram_tensor("ident", [128, 128], BF16, kind="ExternalInput").ap()
    dwspinT = nc.dram_tensor("wspinT", [128, 8 * D], F32, kind="ExternalInput").ap()
    dwjT = nc.dram_tensor("wjT", [128, 8 * D], F32, kind="ExternalInput").ap()
    dwgT = nc.dram_tensor("wgT", [128, 8 * D], F32, kind="ExternalInput").ap()
    dbspin = nc.dram_tensor("bspin", [128, D], F32, kind="ExternalInput").ap()
    dbeta = nc.dram_tensor("beta", [128, 1], F32, kind="ExternalInput").ap()
    dwqkv, dwo, dw1a, dw1b, dw2 = [], [], [], [], []
    db1, dlng, dlnb, dqsc, dksc = [], [], [], [], []
    dqscr, dkscr = [], []
    for l in range(DEPTH):
        dwqkv.append(nc.dram_tensor(f"wqkv{l}", [128, 8 * 3 * D], F32, kind="ExternalInput").ap())
        dwo.append(nc.dram_tensor(f"wo{l}", [128, 8 * D], F32, kind="ExternalInput").ap())
        dw1a.append(nc.dram_tensor(f"w1a{l}", [128, 8 * 3072], F32, kind="ExternalInput").ap())
        dw1b.append(nc.dram_tensor(f"w1b{l}", [128, 8 * 2388], F32, kind="ExternalInput").ap())
        dw2.append(nc.dram_tensor(f"w2{l}", [128, 22 * D], F32, kind="ExternalInput").ap())
        db1.append(nc.dram_tensor(f"b1_{l}", [1, FF2], F32, kind="ExternalInput").ap())
        dlng.append(nc.dram_tensor(f"lng{l}", [128, 8], F32, kind="ExternalInput").ap())
        dlnb.append(nc.dram_tensor(f"lnb{l}", [128, 8], F32, kind="ExternalInput").ap())
        dqsc.append(nc.dram_tensor(f"qsc{l}", [128, DH], F32, kind="ExternalInput").ap())
        dksc.append(nc.dram_tensor(f"ksc{l}", [128, DH], F32, kind="ExternalInput").ap())
        dqscr.append(nc.dram_tensor(f"qscr{l}", [128, DH], F32, kind="ExternalInput").ap())
        dkscr.append(nc.dram_tensor(f"kscr{l}", [128, DH], F32, kind="ExternalInput").ap())
    kd = "ExternalOutput" if getattr(cfg, "dbg", False) else "Internal"
    dxA = nc.dram_tensor("xA", [(1 + NS) * 128, D], F32, kind=kd).ap()
    dxB = nc.dram_tensor("xB", [(1 + NS) * 128, D], F32, kind=kd).ap()
    dx1 = nc.dram_tensor("x1s", [NW * 128, D], F32, kind=kd).ap()
    ds = nc.dram_tensor("s_hbm", [NS * 128, D], BF16, kind=kd).ap()
    dxnT = nc.dram_tensor("xnT", [NW * 128, D], BF16, kind=kd).ap()
    dm = nc.dram_tensor("mscr", [NW * 128, FFH], BF16, kind=kd).ap()
    dout = nc.dram_tensor("out", [NW * 128, D], F32, kind="ExternalOutput").ap()
    datt = (nc.dram_tensor("attdbg", [NW * 128, D], BF16, kind="ExternalOutput").ap()
            if getattr(cfg, "dbg", False) else None)

    P = Planner()
    cPE = P.chan("cPE")
    cDVE = P.chan("cDVE")
    cACT = P.chan("cACT")
    cLDs = P.chan("cLDs", 16)
    cGP = P.chan("cGP", 16)

    def pe(fn, waits=(), drain=False):
        return P.op("pe", fn, waits, cPE, drain)

    def dve(fn, waits=(), drain=False):
        return P.op("dve", fn, waits, cDVE, drain)

    def act(fn, waits=(), drain=False):
        return P.op("act", fn, waits, cACT, drain)

    def sp(fn, ch="misc_s", waits=()):
        return P.op("sp", fn, waits, P.chan("d_" + ch, 16))

    def gp(fn, ch="misc_g", waits=()):
        return P.op("gp", fn, waits, P.chan("d_" + ch, 16))

    def tokl(x):
        if x is None:
            return []
        if isinstance(x, list):
            return list(x)
        return [x]

    def snap(*names):
        out = []
        for n in names:
            ch = P.chan("d_" + n, 16)
            if ch.n > 0:
                out.append((ch, ch.n))
        return out

    stack = ExitStack()

    def sb(name, p, f, dt):
        return stack.enter_context(nc.sbuf_tensor("sb_" + name, [p, f], dt))

    def ps(name, p, f, dt=F32):
        return stack.enter_context(nc.psum_tensor("ps_" + name, [p, f], dt))

    AR = sb("arena", 128, 57344, BF16)
    AR2 = sb("arena2", 128, 20480, BF16)

    def a_(off, n):
        return AR.ap()[:, off:off + n]

    def a32(off, n):
        return AR.ap().bitcast(F32)[:, off // 2: off // 2 + n]

    def b_(off, n, p=128):
        return AR2.ap()[0:p, off:off + n]

    def b32(off, n, p=128):
        return AR2.ap().bitcast(F32)[0:p, off // 2: off // 2 + n]

    wspin = a_(0, 8192)
    wj = a_(8192, 8192)
    wg = a_(16384, 8192)
    bspin = a32(49152, D)
    sqT = a_(16384, 8 * QW)
    ET = a_(24576, 8 * QW)
    O_acc = a32(32768, QG * D)
    SST = 49152
    sstr = [a_(SST, MB * D), a_(SST + MB * D, MB * D)]
    wqkv = a_(0, 24576)
    w1b = a_(0, 8 * 2388)
    w1a = a_(24576, 8 * 3072)
    w2 = a_(24576, 22 * D)
    wo = a_(49152, 8192)
    b1b = a_(49152, FF2)

    NLD = 2
    xbf = [b_(0 + 1024 * i, 1024) for i in range(2)]
    xbsh = [b_(2048 + 512 * i, 512) for i in range(2)]
    ropet = [b32(3072 + 512 * i, 256) for i in range(2)]
    maskt = [b_(4096 + 384 * i, 384) for i in range(2)]
    lnT = b_(4864, 1024)
    q_sb = b_(5888, 1024)
    k_sb = b_(6912, 1024)
    rotb = b_(7936, 1024)
    qp = b_(8960, 1024)
    kp = b_(9984, 1024)
    qpT = b_(11008, 2048, p=64)
    qppT = b_(13056, 2048, p=64)
    P_sb = b_(15104, 1536)
    PT_sb = b_(16640, 768)
    at_sb = b_(17408, 1024)
    aT_sb = b_(18432, 1024)
    xshf = [b32(0 + 1024 * i, 512) for i in range(2)]
    gl_sb = b_(6912, 512)
    aa_sb = b_(7424, 512)
    m_sb = b_(8960, FFH)
    mload = b_(0, FFH)
    mT_sb = b_(11008, 22 * 128)
    Pa_sb = b_(0, MW)
    PT_a = b_(512, MW)
    mrow = b32(1536, 2 * QG)
    lrow = b32(1600, 2 * QG)
    s_q = b_(4864, 1024)
    stile = b_(6912, 1024)
    lnTT = b_(5888, 1024)
    mf_sb = AR2.ap().bitcast(F32)[0:1, 1024:2048]
    gi_sb = AR2.ap()[0:1, 4096:5120]
    mfb = b_(5220, 8)

    ident = sb("ident", 128, 128, BF16)
    onesc = sb("onesc", 128, 1, BF16)
    onesr = sb("onesr", 1, 128, BF16)
    b2col = sb("b2col", 64, 1, F32)
    beta = sb("beta", 128, 1, F32)
    gb = sb("gb", 128, D, BF16)
    lng = sb("lng", 128, 8 * DEPTH, F32)
    lnb = sb("lnb", 128, 8 * DEPTH, F32)
    qsc = sb("qsc", 128, DH * DEPTH, F32)
    ksc = sb("ksc", 128, DH * DEPTH, F32)
    qscr = sb("qscr", 128, DH * DEPTH, F32)
    kscr = sb("kscr", 128, DH * DEPTH, F32)
    xin = sb("xin", 128, NLD * D, F32)
    xo_sb = sb("xo_sb", 128, NLD * D, F32)
    scrA = sb("scrA", 128, D, F32)
    scrB = sb("scrB", 128, D, F32)
    scrD = sb("scrD", 128, D, BF16)
    scrC = sb("scrC", 128, 256, F32)
    small = sb("small", 128, 128, F32)
    ismall = sb("ismall", 128, 16, I32)
    kTr = sb("kTr", 64, 3 * H * 128, BF16)
    vring = sb("vring", 128, 3 * D, BF16)
    lsum = sb("lsum", 128, H, F32)
    rinvb = sb("rinvb", 128, H, F32)
    rq8 = sb("rq8", 128, H, F32)

    pA = ps("pA", 128, 512)
    pB = ps("pB", 128, 512)
    pS = ps("pS", 128, 2048)
    pD = ps("pD", 128, 1024)
    pT = pS.ap().bitcast(BF16)[:, 2048:3072]
    pMF = pD.ap()[0:1, 0:1024]

    guard = {"pT": None}

    def newton_rsqrt(dst, src, width=1):
        iv = ismall[:, 0:width]
        t2 = small[:, 72:72 + width]
        u0 = small[:, 88:88 + width]
        dve(lambda e: e.tensor_scalar(iv, src.bitcast(I32), 1, None,
                                      ALU.logical_shift_right), drain=True)
        dve(lambda e: e.tensor_scalar(iv, iv, -1, None, ALU.mult), drain=True)
        dve(lambda e: e.tensor_scalar(iv, iv, 0x5f3759df, None, ALU.add), drain=True)
        dve(lambda e: e.tensor_copy(dst, iv.bitcast(F32)), drain=True)
        dve(lambda e: e.tensor_scalar(t2, src, 0.5, None, ALU.mult), drain=True)
        for _ in range(4):
            dve(lambda e: e.tensor_tensor(u0, dst, dst, ALU.mult), drain=True)
            dve(lambda e: e.tensor_tensor(u0, u0, t2, ALU.mult), drain=True)
            dve(lambda e: e.tensor_scalar(u0, u0, -1.0, 1.5, ALU.mult, ALU.add), drain=True)
            dve(lambda e: e.tensor_tensor(dst, dst, u0, ALU.mult), drain=True)

    def ln_full(xa, act_waits):
        dve(lambda e: e.reduce_sum(small[:, 0:1], xa, AX.X), waits=act_waits,
            drain=True)
        a1 = act(lambda e: e.activation(scrD[:, :], xa, AF.Square,
                                        accum_out=small[:, 1:2]), waits=act_waits)
        dve(lambda e: e.tensor_scalar(small[:, 2:3], small[:, 0:1], 1.0 / D,
                                      None, ALU.mult), waits=[a1], drain=True)
        dve(lambda e: e.tensor_tensor(small[:, 3:4], small[:, 2:3],
                                      small[:, 2:3], ALU.mult), drain=True)
        dve(lambda e: e.tensor_scalar(small[:, 3:4], small[:, 3:4], -1.0, None,
                                      ALU.mult), drain=True)
        dve(lambda e: e.scalar_tensor_tensor(small[:, 3:4], small[:, 1:2],
                                             1.0 / D, small[:, 3:4], ALU.mult,
                                             ALU.add), drain=True)
        dve(lambda e: e.tensor_scalar(small[:, 3:4], small[:, 3:4], 1e-5, None,
                                      ALU.add), drain=True)
        newton_rsqrt(small[:, 8:9], small[:, 3:4])
        dve(lambda e: e.tensor_tensor(small[:, 9:10], small[:, 2:3],
                                      small[:, 8:9], ALU.mult), drain=True)
        return dve(lambda e: e.tensor_scalar(small[:, 9:10], small[:, 9:10],
                                             -1.0, None, ALU.mult), drain=True)

    def transposes(blocks, pewaits=()):
        ev = None
        first = True
        for blk in blocks:
            (srcf, dstf, rows) = blk[:3]
            pin = blk[3] if len(blk) > 3 else 128
            w = list(pewaits) if first else []
            if guard["pT"] is not None:
                w.append(guard["pT"])
            first = False
            tk = pe(lambda e, srcf=srcf, rows=rows, pin=pin: e.transpose(
                pT[0:rows, 0:pin], srcf(), ident[0:pin, 0:pin]), waits=w)
            ev = dve(lambda e, dstf=dstf, rows=rows, pin=pin: e.tensor_copy(
                dstf(), pT[0:rows, 0:pin]), waits=[tk])
            guard["pT"] = ev
        return ev

    def transposes_sc(blocks, pewaits=()):
        ev = None
        first = True
        for (srcf, dstf, s1f, s2f) in blocks:
            w = list(pewaits) if first else []
            if guard["pT"] is not None:
                w.append(guard["pT"])
            first = False
            tk = pe(lambda e, srcf=srcf: e.transpose(
                pT[:, 0:128], srcf(), ident[:, :]), waits=w)
            ev = dve(lambda e, dstf=dstf, s1f=s1f, s2f=s2f: e.tensor_scalar(
                dstf(), pT[:, 0:128], s1f(), s2f(), ALU.mult, ALU.add),
                waits=[tk])
            guard["pT"] = ev
        return ev

    for f in [lambda e: e.dma_start(ident[:, :], dident),
              lambda e: e.dma_start(b2col[:, :], db2col),
              lambda e: e.dma_start(beta[:, :], dbeta)]:
        sp(f, "const")
    for l in range(DEPTH):
        sp(lambda e, l=l: e.dma_start(lng[:, 8 * l:8 * l + 8], dlng[l]), "const")
        sp(lambda e, l=l: e.dma_start(lnb[:, 8 * l:8 * l + 8], dlnb[l]), "const")
        sp(lambda e, l=l: e.dma_start(qsc[:, DH * l:DH * l + DH], dqsc[l]), "const")
        sp(lambda e, l=l: e.dma_start(ksc[:, DH * l:DH * l + DH], dksc[l]), "const")
        sp(lambda e, l=l: e.dma_start(qscr[:, DH * l:DH * l + DH], dqscr[l]), "const")
        sp(lambda e, l=l: e.dma_start(kscr[:, DH * l:DH * l + DH], dkscr[l]), "const")
    tok_const = (P.chan("d_const", 16), P.chan("d_const", 16).n)
    gp(lambda e: e.dma_start(wspin, dwspinT), "w0")
    gp(lambda e: e.dma_start(wj, dwjT), "w0")
    gp(lambda e: e.dma_start(wg, dwgT), "w0")
    gp(lambda e: e.dma_start(bspin, dbspin), "w0")
    tok_w0 = (P.chan("d_w0", 16), P.chan("d_w0", 16).n)

    d0 = dve(lambda e: e.memset(onesc[:, :], 1.0), waits=[tok_const, tok_w0])
    dve(lambda e: e.memset(onesr[:, :], 1.0))
    dve(lambda e: e.memset(kTr[:, :], 0.0))
    dve(lambda e: e.memset(vring[:, :], 0.0))
    dve(lambda e: e.memset(O_acc, 0.0))
    z = dve(lambda e: e.memset(scrA[:, :], 0.0))
    gp(lambda e: e.dma_start(dxA[0:128, :], scrA[:, :]), "pad", waits=[z])
    zz = gp(lambda e: e.dma_start(dxB[0:128, :], scrA[:, :]), "pad")
    act(lambda e: e.activation(small[:, 120:121], beta[:, 0:1], AF.Copy),
        waits=[tok_const, zz])

    war_xin = [None, None]
    s_done = None
    st_s_prev = None
    for j in range(NS):
        sl = j % NLD
        lw = [war_xin[sl]] if war_xin[sl] else []
        ld = sp(lambda e, j=j, sl=sl: e.dma_start(
            xin[:, sl * D:(sl + 1) * D], dx[j * 128:(j + 1) * 128, :]),
            "xin%d" % sl, waits=lw)
        xa = xin[:, sl * D:(sl + 1) * D]
        nm = ln_full(xa, [ld])
        lnt = act(lambda e, xa=xa: e.activation(lnT, xa, AF.Identity,
                                                bias=small[:, 9:10],
                                                scale=small[:, 8:9]), waits=[nm])
        war_xin[sl] = lnt
        tv = transposes(
            [(lambda i=i: lnT[:, 128 * i:128 * (i + 1)],
              lambda i=i: lnTT[:, 128 * i:128 * (i + 1)], 128) for i in range(8)],
            pewaits=[lnt])
        sv = None
        for ch in range(2):
            pp = pA if ch == 0 else pB
            for k in range(8):
                mmi = pe(lambda e, pp=pp, k=k, ch=ch: e.matmul(
                    pp[:, :], lnTT[:, 128 * k:128 * (k + 1)],
                    wspin[:, D * k + 512 * ch: D * k + 512 * (ch + 1)],
                    start=(k == 0), stop=(k == 7)),
                    waits=[tv] if k == 0 else [])
            sv = dve(lambda e, pp=pp, ch=ch: e.tensor_tensor(
                stile[:, 512 * ch:512 * (ch + 1)], pp[:, :],
                bspin[:, 512 * ch:512 * (ch + 1)], ALU.add),
                waits=[mmi] + ([st_s_prev] if (ch == 0 and st_s_prev) else []),
                drain=True)
            s_done = pe(lambda e, ch=ch, j=j: e.matmul(
                pMF[:, 512 * ch:512 * (ch + 1)], onesc[:, :],
                stile[:, 512 * ch:512 * (ch + 1)],
                start=(j == 0), stop=(j == NS - 1), skip_group_check=True),
                waits=[sv])
        st_s_prev = gp(lambda e, j=j: e.dma_start(
            ds[j * 128:(j + 1) * 128, :], stile[:, :]), "st_s%d" % (j % 2), waits=[s_done])
    tok_slast = None

    mfev = dve(lambda e: e.tensor_scalar(mf_sb, pMF[0:1, :], 1.0 / c.T, None,
                                         ALU.mult), waits=[s_done])
    cast = dve(lambda e: e.tensor_copy(gi_sb, mf_sb), waits=[mfev], drain=True)
    tv = transposes(
        [(lambda k=k: gi_sb[0:1, 128 * k:128 * (k + 1)],
          lambda k=k: mfb[:, k:k + 1], 128, 1) for k in range(8)],
        pewaits=[cast])
    gv = None
    for ch in range(2):
        pp = pA if ch == 0 else pB
        for k in range(8):
            gmm = pe(lambda e, pp=pp, k=k, ch=ch: e.matmul(
                pp[0:1, :], mfb[:, k:k + 1],
                wg[:, D * k + 512 * ch: D * k + 512 * (ch + 1)],
                start=(k == 0), stop=(k == 7)),
                waits=[tv] if (ch == 0 and k == 0) else [])
        gv = dve(lambda e, pp=pp, ch=ch: e.tensor_copy(
            gi_sb[0:1, 512 * ch:512 * (ch + 1)], pp[0:1, :]),
            waits=[gmm, cast], drain=True)
    gbv = None
    for ch in range(2):
        pp = pB if ch == 0 else pA
        bmm = pe(lambda e, pp=pp, ch=ch: e.matmul(
            pp[:, :], onesr[0:1, :], gi_sb[0:1, 512 * ch:512 * (ch + 1)],
            start=True, stop=True), waits=[gv] if ch == 0 else [])
        gbv = dve(lambda e, pp=pp, ch=ch: e.tensor_copy(
            gb[:, 512 * ch:512 * (ch + 1)], pp[:, :]), waits=[bmm])

    INV32 = 1.0 / 32.0
    epi_store = None
    war_scrB = gbv
    war_sst = [None, None]
    war_sq = None
    war_sqT = None
    for g in range(NW // QG):
        for qt in range(QG):
            sl = qt % 2
            lw = [war_sst[sl]] if war_sst[sl] else []
            if g == 0 and qt < 2:
                lw.extend(snap("st_s0", "st_s1"))
            ldq = sp(lambda e, g=g, qt=qt, sl=sl: e.dma_start(
                sstr[sl][:, 0:D], ds[(g * QG + qt) * 128:(g * QG + qt + 1) * 128, :]),
                "sst%d" % sl, waits=lw)
            ev = transposes(
                [(lambda i=i, sl=sl: sstr[sl][:, 128 * i:128 * (i + 1)],
                  lambda i=i, qt=qt: sqT[:, QW * i + 128 * qt: QW * i + 128 * (qt + 1)],
                  128) for i in range(8)],
                pewaits=[ldq] + ([war_sqT] if war_sqT else []))
            war_sst[sl] = ev
        nch = (QW + 511) // 512
        eev = None
        for o in range(8):
            for ch in range(nch):
                w0 = 512 * ch
                w1c = min(512, QW - w0)
                pp = pA if (o * nch + ch) % 2 == 0 else pB
                for k in range(8):
                    emm = pe(lambda e, pp=pp, o=o, k=k, w0=w0, w1c=w1c: e.matmul(
                        pp[:, 0:w1c], wj[:, D * k + 128 * o: D * k + 128 * (o + 1)],
                        sqT[:, QW * k + w0: QW * k + w0 + w1c],
                        start=(k == 0), stop=(k == 7)),
                        waits=[ev] if (o == 0 and ch == 0 and k == 0) else [])
                eev = dve(lambda e, pp=pp, o=o, w0=w0, w1c=w1c: e.tensor_copy(
                    ET[:, QW * o + w0: QW * o + w0 + w1c], pp[:, 0:w1c]),
                    waits=[emm])
                pe(lambda e: e.nop(), waits=[eev])
        war_sqT = eev
        dve(lambda e: e.memset(mrow[:, 0:QG], NEG), waits=[eev])
        im = dve(lambda e: e.memset(lrow[:, 0:QG], 0.0))
        prev_pact = {0: None, 1: None}
        prev_ptk = None
        prev_oupd = None
        step = 0
        for mb in range(NS // MB):
            par = mb % 2
            mold = mrow[:, QG * par: QG * (par + 1)]
            mnew = mrow[:, QG * (1 - par): QG * (2 - par)]
            lold = lrow[:, QG * par: QG * (par + 1)]
            lnew = lrow[:, QG * (1 - par): QG * (2 - par)]
            ssl = mb % 2
            lw = [war_sst[ssl]] if war_sst[ssl] else []
            ldb = sp(lambda e, mb=mb, ssl=ssl: e.dma_start(
                sstr[ssl].rearrange("p (b d) -> p b d", b=MB),
                ds[MB * mb * 128:(MB * mb + MB) * 128, :].rearrange(
                    "(b p) d -> p b d", p=128)), "sst%d" % ssl, waits=lw)
            sTmv = a_(0, 8 * MW)
            blks = []
            for sl4 in range(MB):
                for i in range(8):
                    blks.append((
                        lambda ssl=ssl, sl4=sl4, i=i: sstr[ssl][:, D * sl4 + 128 * i:
                                                                D * sl4 + 128 * (i + 1)],
                        lambda sl4=sl4, i=i: sTmv[:, MW * i + 128 * sl4:
                                                  MW * i + 128 * (sl4 + 1)],
                        128))
            stv = transposes(blks, pewaits=[ldb] + ([im] if mb == 0 else []))
            war_sst[ssl] = stv
            for qt in range(QG):
                par2 = step % 2
                pp = pA if par2 == 0 else pB
                sw = [stv] if qt == 0 else []
                if prev_pact[par2] is not None:
                    sw.append(prev_pact[par2])
                for k in range(8):
                    smm = pe(lambda e, pp=pp, k=k, qt=qt: e.matmul(
                        pp[:, 0:MW], ET[:, QW * k + 128 * qt: QW * k + 128 * (qt + 1)],
                        sTmv[:, MW * k: MW * (k + 1)],
                        start=(k == 0), stop=(k == 7)), waits=sw if k == 0 else [])
                tmx = small[:, 16:17]
                dve(lambda e, pp=pp: e.reduce_max(tmx, pp[:, 0:MW], AX.X),
                    waits=[smm], drain=True)
                dve(lambda e, qt=qt, mold=mold, mnew=mnew: e.tensor_tensor(
                    mnew[:, qt:qt + 1], mold[:, qt:qt + 1], tmx, ALU.max),
                    drain=True)
                ng = dve(lambda e, mnew=mnew, qt=qt: e.tensor_scalar(
                    small[:, 17:18], mnew[:, qt:qt + 1], -INV32, None, ALU.mult),
                    drain=True)
                aw = [ng]
                if prev_ptk is not None:
                    aw.append(prev_ptk)
                pact = act(lambda e, pp=pp: e.activation(
                    Pa_sb, pp[:, 0:MW], AF.Exp, bias=small[:, 17:18],
                    scale=INV32, accum_out=small[:, 18:19]), waits=aw)
                prev_pact[par2] = pact
                aact = act(lambda e, qt=qt, mold=mold: e.activation(
                    small[:, 19:20], mold[:, qt:qt + 1], AF.Exp,
                    bias=small[:, 17:18], scale=INV32))
                lup = dve(lambda e, qt=qt, lold=lold, lnew=lnew: e.scalar_tensor_tensor(
                    lnew[:, qt:qt + 1], lold[:, qt:qt + 1], small[:, 19:20],
                    small[:, 18:19], ALU.mult, ALU.add), waits=[pact, aact],
                    drain=True)
                blksP = [(lambda ck=ck: Pa_sb[:, 128 * ck:128 * (ck + 1)],
                          lambda ck=ck: PT_a[:, 128 * ck:128 * (ck + 1)], 128)
                         for ck in range(MB)]
                ptk = transposes(blksP, pewaits=[pact])
                prev_ptk = ptk
                pvw = [ptk]
                if prev_oupd is not None:
                    pvw.append(prev_oupd)
                for ck in range(MB):
                    for ch2 in range(2):
                        pvm = pe(lambda e, ck=ck, ch2=ch2, ssl=ssl: e.matmul(
                            pD[:, 512 * ch2:512 * (ch2 + 1)],
                            PT_a[:, 128 * ck:128 * (ck + 1)],
                            sstr[ssl][:, D * ck + 512 * ch2: D * ck + 512 * (ch2 + 1)],
                            start=(ck == 0), stop=(ck == MB - 1)),
                            waits=pvw if (ck == 0 and ch2 == 0) else [])
                prev_oupd = dve(lambda e, qt=qt: e.scalar_tensor_tensor(
                    O_acc[:, D * qt: D * (qt + 1)], O_acc[:, D * qt: D * (qt + 1)],
                    small[:, 19:20], pD[:, :], ALU.mult, ALU.add),
                    waits=[pvm, lup], drain=True)
                step += 1
            war_sst[ssl] = pvm
        fpar = (NS // MB) % 2
        lfin = lrow[:, QG * fpar: QG * (fpar + 1)]
        for qt in range(QG):
            q = g * QG + qt
            sl = q % NLD
            lw = [war_xin[sl]] if war_xin[sl] else []
            ldx = sp(lambda e, q=q, sl=sl: e.dma_start(
                xin[:, sl * D:(sl + 1) * D], dx[q * 128:(q + 1) * 128, :]),
                "xin%d" % sl, waits=lw)
            lws = [war_sq] if war_sq else []
            ldsq = sp(lambda e, q=q: e.dma_start(
                s_q[:, :], ds[q * 128:(q + 1) * 128, :]), "sq", waits=lws)
            rv = small[:, 20:21]
            dve(lambda e, qt=qt, lfin=lfin: e.reciprocal(rv, lfin[:, qt:qt + 1]),
                waits=[prev_oupd] if qt == 0 else [], drain=True)
            dve(lambda e, qt=qt: e.tensor_scalar(
                scrA[:, :], O_acc[:, D * qt: D * (qt + 1)], rv, None, ALU.mult),
                drain=True)
            war_sq = dve(lambda e: e.tensor_tensor(scrA[:, :], scrA[:, :],
                                                   s_q[:, :], ALU.add),
                         waits=[ldsq], drain=True)
            tt = dve(lambda e: e.tensor_tensor(scrA[:, :], scrA[:, :], gb[:, :],
                                               ALU.add), drain=True)
            taw = [tt]
            if war_scrB is not None:
                taw.append(war_scrB)
            ta = act(lambda e: e.activation(scrB[:, :], scrA[:, :], AF.Tanh,
                                            scale=beta[:, 0:1]), waits=taw)
            lnd = ln_full(scrB[:, :], [ta])
            lno = act(lambda e: e.activation(scrA[:, :], scrB[:, :], AF.Identity,
                                             bias=small[:, 9:10],
                                             scale=small[:, 8:9]), waits=[lnd])
            x1v = dve(lambda e, sl=sl: e.tensor_tensor(
                xo_sb[:, 0:D], scrA[:, :], xin[:, sl * D:(sl + 1) * D], ALU.add),
                waits=[lno, ldx], drain=True)
            war_xin[sl] = x1v
            st1 = gp(lambda e, q=q: e.dma_start(
                dx1[q * 128:(q + 1) * 128, :], xo_sb[:, 0:D]), "st_xone%d" % (q % 2), waits=[x1v])
            lnd2 = ln_full(xo_sb[:, 0:D], [st1])
            fv = act(lambda e: e.activation(scrB[:, :], xo_sb[:, 0:D], AF.Identity,
                                            bias=small[:, 9:10],
                                            scale=small[:, 8:9]), waits=[lnd2])
            st2 = gp(lambda e, q=q: e.dma_start(
                dxA[(1 + q) * 128:(2 + q) * 128, :], scrB[:, :]), "st_f%d" % (q % 2), waits=[fv])
            war_scrB = st2
            epi_store = st2

    def load_weights_attn(l, waits):
        for ch in range(8):
            t = gp(lambda e, l=l, ch=ch: e.dma_start(
                wqkv[:, 3 * D * ch:3 * D * (ch + 1)],
                dwqkv[l][:, 3 * D * ch:3 * D * (ch + 1)]),
                "wq", waits=waits if ch == 0 else [])
        for ch in range(2):
            t = gp(lambda e, l=l, ch=ch: e.dma_start(
                wo[:, 4 * D * ch:4 * D * (ch + 1)],
                dwo[l][:, 4 * D * ch:4 * D * (ch + 1)]), "wq")
        return t

    def load_weights_w1a(l, waits):
        for ch in range(8):
            t = gp(lambda e, l=l, ch=ch: e.dma_start(
                w1a[:, 3072 * ch:3072 * (ch + 1)],
                dw1a[l][:, 3072 * ch:3072 * (ch + 1)]),
                "w1a", waits=waits if ch == 0 else [])
        return t

    def load_weights_w1b(l, waits):
        for ch in range(8):
            t = gp(lambda e, l=l, ch=ch: e.dma_start(
                w1b[:, 2388 * ch:2388 * (ch + 1)],
                dw1b[l][:, 2388 * ch:2388 * (ch + 1)]),
                "w1b", waits=waits if ch == 0 else [])
        return t

    def load_weights_w2(l, waits):
        for ch in range(11):
            t = gp(lambda e, l=l, ch=ch: e.dma_start(
                w2[:, 2 * D * ch:2 * D * (ch + 1)],
                dw2[l][:, 2 * D * ch:2 * D * (ch + 1)]),
                "w2", waits=waits if ch == 0 else [])
        return t

    def build_b1b(l, waits):
        cb = gp(lambda e, l=l: e.dma_start(b1b[0:1, 0:FF2], db1[l]), "b1", waits=waits)
        ev = None
        for ch in range((FF2 + 511) // 512):
            w0 = 512 * ch
            w1c = min(512, FF2 - w0)
            pp = pA if ch % 2 == 0 else pB
            bm = pe(lambda e, pp=pp, w0=w0, w1c=w1c: e.matmul(
                pp[:, 0:w1c], onesr[0:1, :], b1b[0:1, w0:w0 + w1c],
                start=True, stop=True), waits=[cb] if ch == 0 else [])
            ev = dve(lambda e, pp=pp, w0=w0, w1c=w1c: e.tensor_copy(
                b1b[:, w0:w0 + w1c], pp[:, 0:w1c]), waits=[bm])
            pe(lambda e: e.nop(), waits=[ev])
        return ev

    war = {"xin": war_xin, "xbf": [None, None], "xbsh": [None, None],
           "mask": [None, None], "rope": [None, None],
           "xo": [None, None], "m_sb": None, "mload": None, "xsh": [None, None],
           "diag": None, "lnTf1b": None, "lnTst": None, "rq8": None,
           "lsum": [None, None, None, None]}

    def attn_tile(l, j, src, dst, extra_w):
        sl = j % NLD
        r0 = (1 + j) * 128

        def lw(key):
            return tokl(war[key][sl])

        ld1 = sp(lambda e, sl=sl, r0=r0, src=src: e.dma_start(
            xin[:, sl * D:(sl + 1) * D], src[r0:r0 + 128, :]),
            "xin%d" % sl, waits=lw("xin"))
        ld2 = gp(lambda e, sl=sl, r0=r0, src=src: e.dma_start(
            xbf[sl], src[r0:r0 + 128, :]), "xbf%d" % sl, waits=lw("xbf"))
        ld3 = gp(lambda e, sl=sl, r0=r0, src=src: e.dma_start(
            xbsh[sl], src[r0 - 1:r0 + 127, 512:1024]),
            "xbsh%d" % sl, waits=lw("xbsh"))
        ld4 = sp(lambda e, sl=sl, j=j: e.dma_start(
            maskt[sl], dmask[j * 128:(j + 1) * 128, :]),
            "mask%d" % sl, waits=lw("mask"))
        ld5 = sp(lambda e, sl=sl, j=j: e.dma_start(
            ropet[sl], drope[j * 128:(j + 1) * 128, :]),
            "rope%d" % sl, waits=lw("rope"))

        def xs_blk(i, sl=sl):
            if i < 4:
                return xbf[sl][:, 128 * i:128 * (i + 1)]
            return xbsh[sl][:, 128 * (i - 4):128 * (i - 3)]
        tv = transposes(
            [(lambda i=i: xs_blk(i), lambda i=i: lnT[:, 128 * i:128 * (i + 1)], 128)
             for i in range(8)], pewaits=[ld2, ld3] + extra_w)
        war["xbf"][sl] = tv
        war["xbsh"][sl] = tv
        evs = []
        for ch in range(6):
            pp = pA if ch % 2 == 0 else pB
            for k in range(8):
                mm = pe(lambda e, pp=pp, ch=ch, k=k: e.matmul(
                    pp[:, :], lnT[:, 128 * k:128 * (k + 1)],
                    wqkv[:, 3 * D * k + 512 * ch: 3 * D * k + 512 * (ch + 1)],
                    start=(k == 0), stop=(k == 7)),
                    waits=[tv] if (ch == 0 and k == 0) else [])
            if ch < 2:
                ev = dve(lambda e, pp=pp, ch=ch: e.tensor_copy(
                    q_sb[:, 512 * ch:512 * (ch + 1)], pp[:, :]), waits=[mm])
            elif ch < 4:
                ev = dve(lambda e, pp=pp, ch=ch: e.tensor_copy(
                    k_sb[:, 512 * (ch - 2):512 * (ch - 1)], pp[:, :]), waits=[mm])
            else:
                ev = dve(lambda e, pp=pp, ch=ch, j=j: e.tensor_copy(
                    vring[:, D * (j % 3) + 512 * (ch - 4): D * (j % 3) + 512 * (ch - 3)],
                    pp[:, :]), waits=[mm])
            evs.append(ev)
            pe(lambda e: e.nop(), waits=[ev])
        qev, kev = evs[1], evs[3]
        ro = ropet[sl]
        sqa = act(lambda e: e.activation(scrD[:, :], q_sb[:, :], AF.Square),
                  waits=[qev])
        qred = dve(lambda e: e.tensor_reduce(
            small[:, 24:24 + H], scrD.ap().rearrange("p (h d) -> p h d", h=H),
            AX.X, ALU.add), waits=[sqa], drain=True)
        dve(lambda e: e.tensor_scalar(small[:, 24:24 + H], small[:, 24:24 + H],
                                      1e-24, None, ALU.add), drain=True)
        newton_rsqrt(small[:, 56:56 + H], small[:, 24:24 + H], width=H)
        rq = small[:, 56:56 + H]
        qr8 = dve(lambda e: e.tensor_scalar(rq8[:, :], rq, 8.0, None, ALU.mult),
                  waits=[war["rq8"]] if war["rq8"] else [], drain=True)
        ska = act(lambda e: e.activation(scrD[:, :], k_sb[:, :], AF.Square),
                  waits=[kev, qred])
        dve(lambda e: e.tensor_reduce(
            small[:, 24:24 + H], scrD.ap().rearrange("p (h d) -> p h d", h=H),
            AX.X, ALU.add), waits=[ska], drain=True)
        dve(lambda e: e.tensor_scalar(small[:, 24:24 + H], small[:, 24:24 + H],
                                      1e-24, None, ALU.add), drain=True)
        newton_rsqrt(small[:, 40:40 + H], small[:, 24:24 + H], width=H)
        rk = small[:, 40:40 + H]
        qsl = qsc[:, DH * l:DH * (l + 1)]
        ksl = ksc[:, DH * l:DH * (l + 1)]
        qslr = qscr[:, DH * l:DH * (l + 1)]
        kslr = kscr[:, DH * l:DH * (l + 1)]
        t1 = dve(lambda e: e.tensor_tensor(scrC[:, 0:64], qsl, ro[:, 0:64],
                                           ALU.mult), waits=[ld5], drain=True)
        t2 = dve(lambda e: e.tensor_tensor(scrC[:, 64:128], qslr, ro[:, 64:128],
                                           ALU.mult), drain=True)
        t3 = dve(lambda e: e.tensor_tensor(scrC[:, 128:192], ksl, ro[:, 128:192],
                                           ALU.mult), drain=True)
        t4 = dve(lambda e: e.tensor_tensor(scrC[:, 192:256], kslr, ro[:, 192:256],
                                           ALU.mult), drain=True)
        war["rope"][sl] = t4

        def reh(x):
            return x.rearrange("p (h d) -> p h d", h=H)

        def reh2(x):
            return x.rearrange("p (h t d) -> p h t d", h=H, t=2)

        def reA(x):
            return x.rearrange("p (h d) -> p h d", h=H)
        dve(lambda e: e.tensor_scalar(reh2(rotb)[:, :, 0, :],
                                      reh2(q_sb)[:, :, 1, :], -1.0, None,
                                      ALU.mult), drain=True)
        dve(lambda e: e.tensor_copy(reh2(rotb)[:, :, 1, :],
                                    reh2(q_sb)[:, :, 0, :]), drain=True)
        dve(lambda e: e.tensor_tensor(reA(scrA.ap()), reh(q_sb),
                                      brd(scrC[:, 0:64], H, 64), ALU.mult),
            waits=[t1], drain=True)
        dve(lambda e: e.tensor_tensor(reA(scrB.ap()), reh(rotb),
                                      brd(scrC[:, 64:128], H, 64), ALU.mult),
            waits=[t2], drain=True)
        qpv = dve(lambda e: e.tensor_tensor(qp[:, :], scrA[:, :], scrB[:, :],
                                            ALU.add), drain=True)
        dve(lambda e: e.tensor_scalar(reh2(rotb)[:, :, 0, :],
                                      reh2(k_sb)[:, :, 1, :], -1.0, None,
                                      ALU.mult), drain=True)
        dve(lambda e: e.tensor_copy(reh2(rotb)[:, :, 1, :],
                                    reh2(k_sb)[:, :, 0, :]), drain=True)
        dve(lambda e: e.tensor_tensor(reA(scrA.ap()), reh(k_sb),
                                      brd(scrC[:, 128:192], H, 64), ALU.mult),
            waits=[t3], drain=True)
        dve(lambda e: e.tensor_tensor(reA(scrB.ap()), reh(rotb),
                                      brd(scrC[:, 192:256], H, 64), ALU.mult),
            waits=[t4], drain=True)
        dve(lambda e: e.tensor_tensor(scrA[:, :], scrA[:, :], scrB[:, :],
                                      ALU.add), drain=True)
        kpv = dve(lambda e: e.tensor_tensor(
            reA(kp), reA(scrA.ap()),
            rk.rearrange("p (h d) -> p h d", d=1).broadcast_to([128, H, DH]),
            ALU.mult), drain=True)
        blks = []
        for hh in range(H):
            blks.append((lambda hh=hh: qp[:, 64 * hh:64 * (hh + 1)],
                         lambda hh=hh: qpT[0:64, 128 * hh:128 * (hh + 1)], 64))
            blks.append((lambda hh=hh: kp[:, 64 * hh:64 * (hh + 1)],
                         lambda hh=hh, j=j: kTr[0:64, 2048 * (j % 3) + 128 * hh:
                                                2048 * (j % 3) + 128 * (hh + 1)], 64))
        tq0 = transposes(blks, pewaits=[qpv, kpv])
        tq = dve(lambda e: e.tensor_scalar(qppT[0:64, :], qpT[0:64, :],
                                           b2col[:, 0:1], None, ALU.mult),
                 waits=[tq0], drain=True)
        even = (j % 2 == 0)
        mk = maskt[sl]
        prev_pacts = None
        prev_ptm = None
        pvm = None
        for grp in range(4):
            sw0 = [tq] if grp == 0 else list(prev_pacts)
            smm = None
            for ck in range(3):
                slot = (j - 2 + ck) % 3
                usepp = (ck < 2) if even else (ck < 1)
                src_qT = qppT if usepp else qpT
                for hh in range(4):
                    h0 = grp * 4 + hh
                    smm = pe(lambda e, hh=hh, h0=h0, ck=ck, slot=slot, src_qT=src_qT: e.matmul(
                        pS[:, 512 * hh + 128 * ck: 512 * hh + 128 * (ck + 1)],
                        src_qT[0:64, 128 * h0:128 * (h0 + 1)],
                        kTr[0:64, 2048 * slot + 128 * h0: 2048 * slot + 128 * (h0 + 1)],
                        start=True, stop=True),
                        waits=sw0 if (ck == 0 and hh == 0) else [])

            def rps():
                return pS.ap().rearrange("p (h w) -> p h w", h=4)
            dve(lambda e, mk=mk: e.tensor_tensor(
                rps()[:, :, 0:256], rps()[:, :, 0:256],
                brd(mk[:, 0:256], 4, 256), ALU.add), waits=[smm, ld4], drain=True)
            ms2 = dve(lambda e, mk=mk: e.tensor_tensor(
                rps()[:, :, 256:384], rps()[:, :, 256:384],
                brd(mk[:, 256:384], 4, 128), ALU.add), drain=True)
            war["mask"][sl] = ms2
            mx4 = small[:, 104:108]
            dve(lambda e: e.reduce_max(mx4, rps()[:, :, 0:384], AX.X),
                waits=[ms2], drain=True)
            bw = [qr8]
            if prev_pacts is not None:
                bw.append(prev_pacts[-1])
            bi4 = dve(lambda e, grp=grp: e.scalar_tensor_tensor(
                small[:, 108:112], mx4, -8.0, rq[:, 4 * grp:4 * (grp + 1)],
                ALU.mult, ALU.mult), waits=bw, drain=True)
            pacts = []
            for hh in range(4):
                h0 = grp * 4 + hh
                aw = [bi4] if hh == 0 else []
                if hh == 0 and prev_ptm is not None:
                    aw.append(prev_ptm)
                if hh == 0 and war["lsum"][grp] is not None:
                    aw.append(war["lsum"][grp])
                pacts.append(act(lambda e, hh=hh, h0=h0: e.activation(
                    P_sb[:, 384 * hh:384 * (hh + 1)],
                    pS[:, 512 * hh:512 * hh + 384], AF.Exp,
                    bias=small[:, 108 + hh:109 + hh], scale=rq8[:, h0:h0 + 1],
                    accum_out=lsum[:, h0:h0 + 1]), waits=aw))
            prev_pacts = pacts
            war["rq8"] = pacts[-1]
            rcp = dve(lambda e, grp=grp: e.reciprocal(
                rinvb[:, 4 * grp:4 * (grp + 1)], lsum[:, 4 * grp:4 * (grp + 1)]),
                waits=list(pacts), drain=True)
            war["lsum"][grp] = rcp
            for hh in range(4):
                h0 = grp * 4 + hh
                ppt = (pA if hh % 2 == 0 else pB).ap().bitcast(BF16)
                for ck in range(3):
                    ptm = pe(lambda e, ppt=ppt, hh=hh, ck=ck: e.transpose(
                        ppt[:, 128 * ck:128 * (ck + 1)],
                        P_sb[:, 384 * hh + 128 * ck: 384 * hh + 128 * (ck + 1)],
                        ident[:, :]),
                        waits=[pacts[hh]] if ck == 0 else [])
                prev_ptm = ptm
                pev = dve(lambda e, ppt=ppt, hh=hh: e.tensor_copy(
                    PT_sb[:, 384 * (hh % 2):384 * (hh % 2 + 1)], ppt[:, 0:384]),
                    waits=[ptm])
                pcol = 64 * (8 * (h0 % 2) + h0 // 2)
                for ck in range(3):
                    slot = (j - 2 + ck) % 3
                    pvm = pe(lambda e, hh=hh, h0=h0, ck=ck, slot=slot, pcol=pcol: e.matmul(
                        pD[:, pcol:pcol + 64],
                        PT_sb[:, 384 * (hh % 2) + 128 * ck: 384 * (hh % 2) + 128 * (ck + 1)],
                        vring[:, D * slot + 64 * h0: D * slot + 64 * (h0 + 1)],
                        start=(ck == 0), stop=(ck == 2)),
                        waits=[pev] if ck == 0 else [])
            war["diag"] = prev_ptm
        rinv4d = rinvb.ap().rearrange("p (a b) -> p a b", b=2) \
            .rearrange("p a (b u) -> p a b u", u=1).broadcast_to([128, 8, 2, DH])
        atv = dve(lambda e: e.tensor_tensor(
            at_sb.rearrange("p (a b d) -> p a b d", a=8, b=2),
            pD.ap().rearrange("p (b a d) -> p a b d", b=2, a=8),
            rinv4d, ALU.mult), waits=[pvm], drain=True)
        tv2 = transposes(
            [(lambda i=i: at_sb[:, 128 * i:128 * (i + 1)],
              lambda i=i: aT_sb[:, 128 * i:128 * (i + 1)], 128) for i in range(8)],
            pewaits=[atv])
        for ch in range(2):
            for k in range(8):
                wm = pe(lambda e, ch=ch, k=k: e.matmul(
                    pD[:, 512 * ch:512 * (ch + 1)], aT_sb[:, 128 * k:128 * (k + 1)],
                    wo[:, D * k + 512 * ch: D * k + 512 * (ch + 1)],
                    start=(k == 0), stop=(k == 7)),
                    waits=[tv2, atv] if (ch == 0 and k == 0) else [])
        xo = xo_sb[:, sl * D:(sl + 1) * D]
        xov = dve(lambda e, xo=xo, sl=sl: e.tensor_tensor(
            xo, pD[:, :], xin[:, sl * D:(sl + 1) * D], ALU.add),
            waits=[wm, ld1] + lw("xo"), drain=True)
        war["xin"][sl] = xov
        if datt is not None and l == 0:
            gp(lambda e, j=j: e.dma_start(datt[j * 128:(j + 1) * 128, :],
                                          at_sb[:, :]), "st_att", waits=[atv])
        stv = gp(lambda e, xo=xo, r0=r0, dst=dst: e.dma_start(
            dst[r0:r0 + 128, :], xo), "st_x%d" % sl, waits=[xov])
        war["xo"][sl] = stv
        return stv

    def f1a_tile(l, j, src, extra_w):
        sl = j % NLD
        r0 = (1 + j) * 128
        ld1 = sp(lambda e, sl=sl, r0=r0, src=src: e.dma_start(
            xin[:, sl * D:(sl + 1) * D], src[r0:r0 + 128, :]),
            "xin%d" % sl, waits=tokl(war["xin"][sl]))
        ld2 = sp(lambda e, sl=sl, r0=r0, src=src: e.dma_start(
            xshf[sl], src[r0 - 1:r0 + 127, 512:1024]),
            "xsh%d" % sl, waits=tokl(war["xsh"][sl]))
        xa = xin[:, sl * D:(sl + 1) * D]
        xb = xshf[sl]
        dve(lambda e, xa=xa: e.reduce_sum(small[:, 0:1], xa[:, 0:512], AX.X),
            waits=[ld1], drain=True)
        dve(lambda e, xb=xb: e.reduce_sum(small[:, 4:5], xb, AX.X),
            waits=[ld2], drain=True)
        dve(lambda e: e.tensor_tensor(small[:, 0:1], small[:, 0:1],
                                      small[:, 4:5], ALU.add), drain=True)
        act(lambda e, xa=xa: e.activation(scrD[:, 0:512], xa[:, 0:512],
                                          AF.Square, accum_out=small[:, 1:2]),
            waits=[ld1])
        a1 = act(lambda e, xb=xb: e.activation(scrD[:, 512:1024], xb, AF.Square,
                                               accum_out=small[:, 5:6]),
                 waits=[ld2])
        dve(lambda e: e.tensor_tensor(small[:, 1:2], small[:, 1:2],
                                      small[:, 5:6], ALU.add), waits=[a1],
            drain=True)
        dve(lambda e: e.tensor_scalar(small[:, 2:3], small[:, 0:1], 1.0 / D,
                                      None, ALU.mult), drain=True)
        dve(lambda e: e.tensor_tensor(small[:, 3:4], small[:, 2:3],
                                      small[:, 2:3], ALU.mult), drain=True)
        dve(lambda e: e.tensor_scalar(small[:, 3:4], small[:, 3:4], -1.0, None,
                                      ALU.mult), drain=True)
        dve(lambda e: e.scalar_tensor_tensor(small[:, 3:4], small[:, 1:2],
                                             1.0 / D, small[:, 3:4], ALU.mult,
                                             ALU.add), drain=True)
        dve(lambda e: e.tensor_scalar(small[:, 3:4], small[:, 3:4], 1e-5, None,
                                      ALU.add), drain=True)
        newton_rsqrt(small[:, 8:9], small[:, 3:4])
        dve(lambda e: e.tensor_tensor(small[:, 9:10], small[:, 2:3],
                                      small[:, 8:9], ALU.mult), drain=True)
        nm = dve(lambda e: e.tensor_scalar(small[:, 9:10], small[:, 9:10], -1.0,
                                           None, ALU.mult), drain=True)
        na = act(lambda e, xa=xa: e.activation(q_sb[:, 0:512], xa[:, 0:512],
                                               AF.Identity, bias=small[:, 9:10],
                                               scale=small[:, 8:9]), waits=[nm])
        nb = act(lambda e, xb=xb: e.activation(q_sb[:, 512:1024], xb, AF.Identity,
                                               bias=small[:, 9:10],
                                               scale=small[:, 8:9]))
        war["xin"][sl] = nb
        war["xsh"][sl] = nb
        tvv = transposes_sc(
            [(lambda i=i: q_sb[:, 128 * i:128 * (i + 1)],
              lambda i=i: lnT[:, 128 * i:128 * (i + 1)],
              lambda i=i: lng[:, 8 * l + i:8 * l + i + 1],
              lambda i=i: lnb[:, 8 * l + i:8 * l + i + 1]) for i in range(8)],
            pewaits=[na, nb] + extra_w +
            ([war["lnTst"]] if war["lnTst"] else []))
        stx = gp(lambda e, j=j: e.dma_start(
            dxnT[j * 128:(j + 1) * 128, :], lnT[:, :]), "st_xnT%d" % (j % 2), waits=[tvv])
        war["lnTst"] = stx
        mv = _f1_chunks(j, sl, [tvv], w1a, 3072, 0, 1536,
                        [(0, 512), (512, 512), (1024, 512)])
        stv = gp(lambda e, sl=sl, j=j: e.dma_start(
            dm[j * 128:(j + 1) * 128, 0:1536], m_sb[:, 0:1536]), "st_m%d" % (j % 2), waits=[mv])
        war["m_sb"] = stv
        return stv

    def _f1_chunks(j, sl, lnw, wbuf, wstride, a_off, g_off, chunks):
        mv = None
        for ci, (cw0, wc) in enumerate(chunks):
            def bank(i):
                if i == 0:
                    return pA[:, :]
                if i == 1:
                    return pB[:, :]
                if i < 4:
                    return pS[:, 512 * (i - 2):512 * (i - 1)]
                return pD[:, 512 * (i - 4):512 * (i - 3)]
            ppa, ppg = bank((2 * ci) % 6), bank((2 * ci + 1) % 6)
            for k in range(8):
                mma = pe(lambda e, ppa=ppa, k=k, cw0=cw0, wc=wc: e.matmul(
                    ppa[:, 0:wc], lnT[:, 128 * k:128 * (k + 1)],
                    wbuf[:, wstride * k + cw0: wstride * k + cw0 + wc],
                    start=(k == 0), stop=(k == 7)),
                    waits=lnw if (ci == 0 and k == 0) else [])
            for k in range(8):
                mmg = pe(lambda e, ppg=ppg, k=k, cw0=cw0, wc=wc: e.matmul(
                    ppg[:, 0:wc], lnT[:, 128 * k:128 * (k + 1)],
                    wbuf[:, wstride * k + g_off + cw0: wstride * k + g_off + cw0 + wc],
                    start=(k == 0), stop=(k == 7)))
            bcol = a_off + cw0
            ga = dve(lambda e, ppg=ppg, bcol=bcol, wc=wc: e.tensor_tensor(
                scrA[:, 0:wc], ppg[:, 0:wc], b1b[:, FFH + bcol: FFH + bcol + wc],
                ALU.add), waits=[mmg], drain=True)
            gl = act(lambda e, wc=wc: e.activation(gl_sb[:, 0:wc], scrA[:, 0:wc],
                                                   AF.Gelu), waits=[ga])
            aa = dve(lambda e, ppa=ppa, bcol=bcol, wc=wc: e.tensor_tensor(
                aa_sb[:, 0:wc], ppa[:, 0:wc], b1b[:, bcol:bcol + wc], ALU.add),
                waits=[mma], drain=True)
            mw = [gl, aa]
            if ci == 0 and war["m_sb"]:
                mw.append(war["m_sb"])
            mv = dve(lambda e, bcol=bcol, wc=wc: e.tensor_tensor(
                m_sb[:, bcol:bcol + wc],
                aa_sb[:, 0:wc], gl_sb[:, 0:wc], ALU.mult), waits=mw, drain=True)
            pe(lambda e: e.nop(), waits=[mv])
        return mv

    def f1b_tile(l, j, extra_w):
        sl = j % NLD
        lnw0 = tokl(war["lnTf1b"])
        ldn = sp(lambda e, j=j: e.dma_start(
            lnT[:, :], dxnT[j * 128:(j + 1) * 128, :]), "lnTf", waits=lnw0)
        mv = _f1_chunks(j, sl, [ldn] + extra_w, w1b, 2388, 1536, 1194,
                        [(0, 512), (512, 512), (1024, 170)])
        war["lnTf1b"] = mv
        stv = gp(lambda e, j=j: e.dma_start(
            dm[j * 128:(j + 1) * 128, 1536:2730], m_sb[:, 1536:2730]),
            "st_m%d" % (j % 2), waits=[mv])
        war["m_sb"] = stv
        return stv

    def f2_tile(l, j, src, dst, extra_w, final):
        sl = j % NLD
        r0 = (1 + j) * 128
        ld1 = sp(lambda e, j=j: e.dma_start(
            mload, dm[j * 128:(j + 1) * 128, :]),
            "mload", waits=tokl(war["mload"]))
        ld2 = sp(lambda e, sl=sl, r0=r0, src=src: e.dma_start(
            xin[:, sl * D:(sl + 1) * D], src[r0:r0 + 128, :]),
            "xin%d" % sl, waits=tokl(war["xin"][sl]))
        msone = dve(lambda e: e.memset(mT_sb[:, 21 * 128:22 * 128], 1.0)) \
            if j == 0 else None
        blks = []
        for k in range(22):
            rows = 128 if k < 21 else FFH - 21 * 128
            blks.append((
                lambda k=k, rows=rows: mload[:, 128 * k:128 * k + rows],
                lambda k=k, rows=rows: mT_sb[0:rows, 128 * k:128 * (k + 1)],
                rows))
        ev = transposes(blks, pewaits=[ld1] + extra_w +
                        ([msone] if msone else []))
        war["mload"] = ev
        for ch in range(2):
            for k in range(22):
                rows = 128 if k < 21 else (FFH - 21 * 128 + 1)
                mm = pe(lambda e, ch=ch, k=k, rows=rows: e.matmul(
                    pD[:, 512 * ch:512 * (ch + 1)],
                    mT_sb[0:rows, 128 * k:128 * k + 128],
                    w2[0:rows, D * k + 512 * ch: D * k + 512 * (ch + 1)],
                    start=(k == 0), stop=(k == 21)),
                    waits=[ev] if (ch == 0 and k == 0) else [])
        xo = xo_sb[:, sl * D:(sl + 1) * D]
        if not final:
            xov = dve(lambda e, xo=xo, sl=sl: e.tensor_tensor(
                xo, pD[:, :], xin[:, sl * D:(sl + 1) * D], ALU.add),
                waits=[mm, ld2] + ([war["xo"][sl]] if war["xo"][sl] else []),
                drain=True)
            war["xin"][sl] = xov
            stv = gp(lambda e, xo=xo, r0=r0, dst=dst: e.dma_start(
                dst[r0:r0 + 128, :], xo), "st_x%d" % sl, waits=[xov])
        else:
            x12 = dve(lambda e, sl=sl: e.tensor_tensor(
                scrB[:, :], pD[:, :], xin[:, sl * D:(sl + 1) * D], ALU.add),
                waits=[mm, ld2], drain=True)
            war["xin"][sl] = x12
            ld3 = sp(lambda e, j=j: e.dma_start(
                scrA[:, :], dx1[j * 128:(j + 1) * 128, :]), "x1ld", waits=[x12])
            lnd = ln_full(scrB[:, :], [x12])
            lno = act(lambda e: e.activation(scrB[:, :], scrB[:, :], AF.Identity,
                                             bias=small[:, 9:10],
                                             scale=small[:, 8:9]), waits=[lnd])
            xov = dve(lambda e, xo=xo: e.tensor_tensor(
                xo, scrB[:, :], scrA[:, :], ALU.add),
                waits=[lno, ld3] + ([war["xo"][sl]] if war["xo"][sl] else []),
                drain=True)
            stv = gp(lambda e, xo=xo, j=j: e.dma_start(
                dout[j * 128:(j + 1) * 128, :], xo), "st_out%d" % (j % 2), waits=[xov])
        war["xo"][sl] = stv
        return stv

    SLOTTED = {"xin", "xbf", "xbsh", "mask", "rope", "xsh", "xo", "lsum"}

    def reset_war(keys, toks):
        for k in keys:
            if k in SLOTTED:
                war[k] = [list(toks) for _ in war[k]]
            else:
                war[k] = list(toks)

    pa_done = snap("st_f0", "st_f1", "st_xone0", "st_xone1", "st_s0", "st_s1")
    wq_ready = load_weights_attn(0, pa_done)
    w1a_ready = load_weights_w1a(0, pa_done)
    src, dst = dxA, dxB
    barrier = pa_done
    for l in range(DEPTH):
        reset_war(["xbf", "xbsh", "mask", "rope", "xin", "xo"], barrier)
        for j in range(NW):
            attn_tile(l, j, src, dst, [wq_ready] if j == 0 else [])
        src, dst = dst, src
        barrier = snap("st_x0", "st_x1")
        b1_ready = build_b1b(l, barrier)
        w1b_ready = load_weights_w1b(l, barrier)
        reset_war(["xsh", "xin"], barrier)
        for j in range(NW):
            f1a_tile(l, j, src, [w1a_ready, b1_ready] if j == 0 else [])
        barrier = snap("st_x0", "st_x1", "st_m0", "st_m1", "st_xnT0", "st_xnT1")
        w2_ready = load_weights_w2(l, barrier)
        reset_war(["lnTf1b"], barrier)
        for j in range(NW):
            f1b_tile(l, j, [w1b_ready] if j == 0 else [])
        barrier = snap("st_x0", "st_x1", "st_m0", "st_m1", "st_xnT0", "st_xnT1")
        if l + 1 < DEPTH:
            wq_ready = load_weights_attn(l + 1, barrier)
        final = (l == DEPTH - 1)
        reset_war(["mload", "xin"], barrier)
        for j in range(NW):
            f2_tile(l, j, src, dst, [w2_ready] if j == 0 else [], final)
        barrier = snap("st_x0", "st_x1", "st_out0", "st_out1")
        if l + 1 < DEPTH:
            w1a_ready = load_weights_w1a(l + 1, barrier)
        src, dst = dst, src

    allw = [(ch, ch.n) for name, ch in P.chans.items()
            if name.startswith("d_") and ch.n > 0]
    P.op("sp", lambda e: None, waits=allw)
    P.op("gp", lambda e: None, waits=allw)

    with stack:
        P.emit(nc, stack)
    return nc



def kernel(**inputs):
    cfg = CFG()
    nc = build_program(cfg)
    in_maps = [stage_core(inputs, cfg, c // 2, c % 2) for c in range(cfg.NC)]
    res = run_bass_kernel_spmd(nc, in_maps, core_ids=list(range(cfg.NC)))
    out = np.zeros((cfg.NB, cfg.T, cfg.D), dtype=np.float32)
    nh0 = cfg.NSWEEP * 128
    vs1 = (cfg.NSWEEP - cfg.ROT) * 128
    for c in range(cfg.NC):
        b, h = c // 2, c % 2
        o = np.asarray(res.results[c]["out"])
        if h == 0:
            out[b, 0:nh0] = o
        else:
            out[b, nh0:] = o[vs1:]
    return out


# version 02
# speedup vs baseline: 1.2667x; 1.2667x over previous
import numpy as np
import ml_dtypes
import concourse.bass as bass
import concourse.mybir as mybir
from concourse.bass_utils import run_bass_kernel_spmd
from contextlib import ExitStack

F32 = mybir.dt.float32
BF16 = mybir.dt.bfloat16
I32 = mybir.dt.int32
AF = mybir.ActivationFunctionType
ALU = mybir.AluOpType
AX = mybir.AxisListType

NEG = -1.0e38


class CFG:
    def __init__(self, nslot=64, depth=6, qg=5, mb=4, nbatch=4, ncores=8):
        self.NSLOT = nslot
        self.DEPTH = depth
        self.NSWEEP = (nslot + 6 * depth) // 2
        assert 2 * self.NSWEEP == nslot + 6 * depth
        self.ROT = nslot - self.NSWEEP
        assert self.ROT % 2 == 0 and self.ROT > 0
        self.QG = qg
        assert self.NSWEEP % qg == 0 and qg <= 8
        self.MB = mb
        assert nslot % mb == 0 and mb <= 4
        self.T = nslot * 128
        self.D = 1024
        self.H = 16
        self.DH = 64
        self.FFH = 2730
        self.FF2 = 5460
        self.NB = nbatch
        self.NC = ncores



def _rope_tables(T, DH, WINDOW=256):
    half = np.arange(0, DH, 2, dtype=np.float64)
    inv_freq = 1.0 / 10000.0 ** (half / DH)
    t = np.arange(T, dtype=np.float64)
    freqs = t[:, None] * inv_freq[None, :]
    freqs = np.concatenate([freqs, freqs], -1)
    base = (half + 0.4 * DH) / (1.4 * DH)
    bs = np.concatenate([base, base])
    p = (t - 256.0 * np.floor(t / 256.0)) / (WINDOW // 2)
    scale = bs[None, :] ** p[:, None]
    cos, sin = np.cos(freqs), np.sin(freqs)
    return (np.concatenate([cos * scale, sin * scale, cos / scale, sin / scale],
                           axis=1).astype(np.float32),
            (bs ** 2.0).astype(np.float32))


def _masks(cfg, perm):
    n = cfg.NSWEEP
    out = np.zeros((n, 128, 384), dtype=np.float32)
    q = np.arange(128)[:, None]
    k = np.arange(128)[None, :]
    up = (k < q)
    lo = (k > q)
    for j in range(n):
        a = perm[j]
        m = out[j]
        if a - 2 < 0:
            m[:, 0:128] = NEG
        else:
            m[:, 0:128][up] = NEG
        if a - 1 < 0:
            m[:, 128:256] = NEG
        m[:, 256:384][lo] = NEG
    return out


def _kt(w, k=128):
    K, N = w.shape
    nk = (K + k - 1) // k
    pad = nk * k - K
    if pad:
        w = np.concatenate([w, np.zeros((pad, N), w.dtype)], 0)
    return np.ascontiguousarray(
        w.reshape(nk, k, N).transpose(1, 0, 2).reshape(k, nk * N))


def stage_core(inputs, cfg, b, h):
    c = cfg
    rot = 0 if h == 0 else c.ROT
    perm = (np.arange(c.NSLOT) + rot) % c.NSLOT
    x = np.asarray(inputs["L0_attn_x"][b], dtype=np.float32)
    xp = np.ascontiguousarray(x.reshape(c.NSLOT, 128, c.D)[perm].reshape(c.T, c.D))
    rope, b2 = _rope_tables(c.T, c.DH)
    rope_p = rope.reshape(c.NSLOT, 128, 256)[perm].reshape(c.T, 256)
    d = {
        "x": xp,
        "rope": rope_p,
        "b2col": np.ascontiguousarray(b2.reshape(c.DH, 1)),
        "masks": _masks(c, perm).reshape(c.NSWEEP * 128, 384).astype(ml_dtypes.bfloat16),
        "ident": np.eye(128, dtype=ml_dtypes.bfloat16),
        "wspinT": _kt(np.ascontiguousarray(
            np.asarray(inputs["L0_attn_W_spin"], np.float32).T)),
        "wjT": _kt(np.ascontiguousarray(
            np.asarray(inputs["L0_attn_W_J"], np.float32).T)),
        "wgT": _kt(np.ascontiguousarray(
            np.asarray(inputs["L0_attn_W_global"], np.float32).T)),
        "bspin": np.broadcast_to(
            np.asarray(inputs["L0_attn_b_spin"], np.float32), (128, c.D)).copy(),
        "beta": np.broadcast_to(
            np.asarray(inputs["L0_attn_beta"], np.float32).reshape(1, 1), (128, 1)).copy(),
    }
    for l in range(c.DEPTH):
        d[f"wqkv{l}"] = _kt(np.asarray(inputs["L0_ffn_Wqkv"][l], np.float32))
        d[f"wo{l}"] = _kt(np.asarray(inputs["L0_ffn_Wo"][l], np.float32))
        W1 = np.asarray(inputs["L0_ffn_W1"][l], np.float32)
        d[f"w1a{l}"] = _kt(np.ascontiguousarray(
            np.concatenate([W1[:, 0:1536], W1[:, 2730:4266]], 1)))
        d[f"w1b{l}"] = _kt(np.ascontiguousarray(
            np.concatenate([W1[:, 1536:2730], W1[:, 4266:5460]], 1)))
        w2 = np.asarray(inputs["L0_ffn_W2"][l], np.float32)
        b2r = np.asarray(inputs["L0_ffn_b2"][l], np.float32)[None, :]
        d[f"w2{l}"] = _kt(np.concatenate([w2, b2r], 0))
        d[f"b1_{l}"] = np.ascontiguousarray(
            np.asarray(inputs["L0_ffn_b1"][l], np.float32)[None, :])
        d[f"lng{l}"] = np.ascontiguousarray(
            np.asarray(inputs["L0_ffn_ln_g"][l], np.float32).reshape(8, 128).T)
        d[f"lnb{l}"] = np.ascontiguousarray(
            np.asarray(inputs["L0_ffn_ln_b"][l], np.float32).reshape(8, 128).T)
        qsl = np.asarray(inputs["L0_ffn_q_scale"][l], np.float32)
        ksl = np.asarray(inputs["L0_ffn_k_scale"][l], np.float32)
        d[f"qsc{l}"] = np.broadcast_to(qsl, (128, c.DH)).copy()
        d[f"ksc{l}"] = np.broadcast_to(ksl, (128, c.DH)).copy()
        d[f"qscr{l}"] = np.broadcast_to(np.roll(qsl, -32), (128, c.DH)).copy()
        d[f"kscr{l}"] = np.broadcast_to(np.roll(ksl, -32), (128, c.DH)).copy()
    return d



class Chan:
    __slots__ = ("name", "n", "step", "sem")

    def __init__(self, name, step=1):
        self.name, self.n, self.step, self.sem = name, 0, step, None

    def push(self):
        self.n += self.step
        return self.n


class Planner:
    ENGINES = ("pe", "dve", "act", "sp", "gp")

    def __init__(self):
        self.streams = {e: [] for e in self.ENGINES}
        self.chans = {}

    def chan(self, name, step=1):
        if name not in self.chans:
            self.chans[name] = Chan(name, step)
        return self.chans[name]

    def op(self, eng, fn, waits=(), inc=None, drain=False):
        v = inc.push() if inc is not None else None
        ws = [(ch, val) for (ch, val) in waits if val > 0]
        self.streams[eng].append((fn, ws, inc, drain))
        return (inc, v) if inc is not None else None

    def emit(self, nc, stack):
        for ch in self.chans.values():
            ch.sem = stack.enter_context(nc.semaphore(ch.name))
        block = stack.enter_context(nc.Block())

        def run(ename):
            def body(e):
                for fn, waits, inc, drain in self.streams[ename]:
                    for (ch, val) in waits:
                        e.wait_ge(ch.sem, val)
                    if drain:
                        e.drain()
                    ins = fn(e)
                    if inc is not None:
                        if ins is None:
                            ins = e.nop()
                        ins.then_inc(inc.sem, inc.step)
            return body

        block.tensor(run("pe"))
        block.vector(run("dve"))
        block.scalar(run("act"))
        block.sync(run("sp"))
        block.gpsimd(run("gp"))


def brd(ap, reps, inner):
    return ap.rearrange("p (a d) -> p a d", a=1).broadcast_to([128, reps, inner])



def build_program(cfg):
    c = cfg
    D, H, DH, FFH, FF2 = c.D, c.H, c.DH, c.FFH, c.FF2
    NS, NW, QG, MB, DEPTH = c.NSLOT, c.NSWEEP, c.QG, c.MB, c.DEPTH
    QW, MW = QG * 128, MB * 128

    nc = bass.Bass("TRN2", target_bir_lowering=False)

    dx = nc.dram_tensor("x", [c.T, D], F32, kind="ExternalInput").ap()
    drope = nc.dram_tensor("rope", [c.T, 256], F32, kind="ExternalInput").ap()
    db2col = nc.dram_tensor("b2col", [DH, 1], F32, kind="ExternalInput").ap()
    dmask = nc.dram_tensor("masks", [NW * 128, 384], BF16, kind="ExternalInput").ap()
    dident = nc.dram_tensor("ident", [128, 128], BF16, kind="ExternalInput").ap()
    dwspinT = nc.dram_tensor("wspinT", [128, 8 * D], F32, kind="ExternalInput").ap()
    dwjT = nc.dram_tensor("wjT", [128, 8 * D], F32, kind="ExternalInput").ap()
    dwgT = nc.dram_tensor("wgT", [128, 8 * D], F32, kind="ExternalInput").ap()
    dbspin = nc.dram_tensor("bspin", [128, D], F32, kind="ExternalInput").ap()
    dbeta = nc.dram_tensor("beta", [128, 1], F32, kind="ExternalInput").ap()
    dwqkv, dwo, dw1a, dw1b, dw2 = [], [], [], [], []
    db1, dlng, dlnb, dqsc, dksc = [], [], [], [], []
    dqscr, dkscr = [], []
    for l in range(DEPTH):
        dwqkv.append(nc.dram_tensor(f"wqkv{l}", [128, 8 * 3 * D], F32, kind="ExternalInput").ap())
        dwo.append(nc.dram_tensor(f"wo{l}", [128, 8 * D], F32, kind="ExternalInput").ap())
        dw1a.append(nc.dram_tensor(f"w1a{l}", [128, 8 * 3072], F32, kind="ExternalInput").ap())
        dw1b.append(nc.dram_tensor(f"w1b{l}", [128, 8 * 2388], F32, kind="ExternalInput").ap())
        dw2.append(nc.dram_tensor(f"w2{l}", [128, 22 * D], F32, kind="ExternalInput").ap())
        db1.append(nc.dram_tensor(f"b1_{l}", [1, FF2], F32, kind="ExternalInput").ap())
        dlng.append(nc.dram_tensor(f"lng{l}", [128, 8], F32, kind="ExternalInput").ap())
        dlnb.append(nc.dram_tensor(f"lnb{l}", [128, 8], F32, kind="ExternalInput").ap())
        dqsc.append(nc.dram_tensor(f"qsc{l}", [128, DH], F32, kind="ExternalInput").ap())
        dksc.append(nc.dram_tensor(f"ksc{l}", [128, DH], F32, kind="ExternalInput").ap())
        dqscr.append(nc.dram_tensor(f"qscr{l}", [128, DH], F32, kind="ExternalInput").ap())
        dkscr.append(nc.dram_tensor(f"kscr{l}", [128, DH], F32, kind="ExternalInput").ap())
    kd = "ExternalOutput" if getattr(cfg, "dbg", False) else "Internal"
    dxA = nc.dram_tensor("xA", [(1 + NS) * 128, D], F32, kind=kd).ap()
    dxB = nc.dram_tensor("xB", [(1 + NS) * 128, D], F32, kind=kd).ap()
    dx1 = nc.dram_tensor("x1s", [NW * 128, D], F32, kind=kd).ap()
    ds = nc.dram_tensor("s_hbm", [NS * 128, D], BF16, kind=kd).ap()
    dxnT = nc.dram_tensor("xnT", [NW * 128, D], BF16, kind=kd).ap()
    dm = nc.dram_tensor("mscr", [NW * 128, FFH], BF16, kind=kd).ap()
    dout = nc.dram_tensor("out", [NW * 128, D], F32, kind="ExternalOutput").ap()
    datt = (nc.dram_tensor("attdbg", [NW * 128, D], BF16, kind="ExternalOutput").ap()
            if getattr(cfg, "dbg", False) else None)

    P = Planner()
    cPE = P.chan("cPE")
    cDVE = P.chan("cDVE")
    cACT = P.chan("cACT")
    cLDs = P.chan("cLDs", 16)
    cGP = P.chan("cGP", 16)

    def pe(fn, waits=(), drain=False):
        return P.op("pe", fn, waits, cPE, drain)

    def dve(fn, waits=(), drain=False):
        return P.op("dve", fn, waits, cDVE, drain)

    def act(fn, waits=(), drain=False):
        return P.op("act", fn, waits, cACT, drain)

    def sp(fn, ch="misc_s", waits=()):
        return P.op("sp", fn, waits, P.chan("d_" + ch, 16))

    def gp(fn, ch="misc_g", waits=()):
        return P.op("gp", fn, waits, P.chan("d_" + ch, 16))

    def tokl(x):
        if x is None:
            return []
        if isinstance(x, list):
            return list(x)
        return [x]

    def snap(*names):
        out = []
        for n in names:
            ch = P.chan("d_" + n, 16)
            if ch.n > 0:
                out.append((ch, ch.n))
        return out

    stack = ExitStack()

    def sb(name, p, f, dt):
        return stack.enter_context(nc.sbuf_tensor("sb_" + name, [p, f], dt))

    def ps(name, p, f, dt=F32):
        return stack.enter_context(nc.psum_tensor("ps_" + name, [p, f], dt))

    AR = sb("arena", 128, 57344, BF16)
    AR2 = sb("arena2", 128, 20480, BF16)

    def a_(off, n):
        return AR.ap()[:, off:off + n]

    def a32(off, n):
        return AR.ap().bitcast(F32)[:, off // 2: off // 2 + n]

    def b_(off, n, p=128):
        return AR2.ap()[0:p, off:off + n]

    def b32(off, n, p=128):
        return AR2.ap().bitcast(F32)[0:p, off // 2: off // 2 + n]

    wspin = a_(0, 8192)
    wj = a_(8192, 8192)
    wg = a_(16384, 8192)
    bspin = a32(49152, D)
    sqT = a_(16384, 8 * QW)
    ET = a_(24576, 8 * QW)
    O_acc = a32(32768, QG * D)
    SST = 49152
    sstr = [a_(SST, MB * D), a_(SST + MB * D, MB * D)]
    wqkv = a_(0, 24576)
    w1b = a_(0, 8 * 2388)
    w1a = a_(24576, 8 * 3072)
    w2 = a_(24576, 22 * D)
    wo = a_(49152, 8192)
    b1b = a_(49152, FF2)

    NLD = 2
    xbf = [b_(0 + 1024 * i, 1024) for i in range(2)]
    xbsh = [b_(2048 + 512 * i, 512) for i in range(2)]
    ropet = [b32(3072 + 512 * i, 256) for i in range(2)]
    maskt = [b_(4096 + 384 * i, 384) for i in range(2)]
    lnT = b_(4864, 1024)
    q_sb = b_(5888, 1024)
    k_sb = b_(6912, 1024)
    rotb = b_(7936, 1024)
    qp = b_(8960, 1024)
    kp = b_(9984, 1024)
    qpT = b_(11008, 2048, p=64)
    qppT = b_(13056, 2048, p=64)
    P_sb = b_(15104, 1536)
    PT_sb = b_(16640, 1536)
    at_sb = b_(18176, 1024)
    aT_sb = b_(19200, 1024)
    xshf = [b32(0 + 1024 * i, 512) for i in range(2)]
    gl_sb = b_(6912, 512)
    aa_sb = b_(7424, 512)
    m_sb = b_(8960, FFH)
    mload = b_(0, FFH)
    mT_sb = b_(11008, 22 * 128)
    Pa_sb = b_(0, MW)
    PT_a = b_(512, MW)
    mrow = b32(1536, 2 * QG)
    lrow = b32(1600, 2 * QG)
    s_q = b_(4864, 1024)
    stile = b_(6912, 1024)
    lnTT = b_(5888, 1024)
    mf_sb = AR2.ap().bitcast(F32)[0:1, 1024:2048]
    gi_sb = AR2.ap()[0:1, 4096:5120]
    mfb = b_(5220, 8)

    ident = sb("ident", 128, 128, BF16)
    onesc = sb("onesc", 128, 1, BF16)
    onesr = sb("onesr", 1, 128, BF16)
    b2col = sb("b2col", 64, 1, F32)
    beta = sb("beta", 128, 1, F32)
    gb = sb("gb", 128, D, BF16)
    lng = sb("lng", 128, 8 * DEPTH, F32)
    lnb = sb("lnb", 128, 8 * DEPTH, F32)
    qsc = sb("qsc", 128, DH * DEPTH, F32)
    ksc = sb("ksc", 128, DH * DEPTH, F32)
    qscr = sb("qscr", 128, DH * DEPTH, F32)
    kscr = sb("kscr", 128, DH * DEPTH, F32)
    xin = sb("xin", 128, NLD * D, F32)
    xo_sb = sb("xo_sb", 128, NLD * D, F32)
    scrA = sb("scrA", 128, D, F32)
    scrB = sb("scrB", 128, D, F32)
    scrD = sb("scrD", 128, D, BF16)
    scrC = sb("scrC", 128, 256, F32)
    small = sb("small", 128, 128, F32)
    ismall = sb("ismall", 128, 16, I32)
    kTr = sb("kTr", 64, 3 * H * 128, BF16)
    vring = sb("vring", 128, 3 * D, BF16)
    lsum = sb("lsum", 128, H, F32)
    rinvb = sb("rinvb", 128, H, F32)
    rq8 = sb("rq8", 128, H, F32)

    pA = ps("pA", 128, 512)
    pB = ps("pB", 128, 512)
    pS = ps("pS", 128, 2048)
    pD = ps("pD", 128, 1024)
    pT = pS.ap().bitcast(BF16)[:, 2048:3072]
    pMF = pD.ap()[0:1, 0:1024]

    guard = {"pT": None, "b0": None, "b1": None}
    pTb = [pS.ap().bitcast(BF16)[:, 2048:3072],
           pS.ap().bitcast(BF16)[:, 3072:4096]]

    def tbatch(srcs, dstf, rows=128, pin=128, pewaits=(), group=4,
               scale2=None):
        ev = None
        gidx = 0
        first = True
        for i0 in range(0, len(srcs), group):
            n = min(group, len(srcs) - i0)
            b = gidx % 2
            gidx += 1
            w = list(pewaits) if first else []
            first = False
            if guard["b%d" % b] is not None:
                w.append(guard["b%d" % b])
            for k in range(n):
                tk = pe(lambda e, i=i0 + k, k=k, b=b, rows=rows, pin=pin:
                        e.transpose(pTb[b][0:rows, pin * k:pin * (k + 1)],
                                    srcs[i](), ident[0:pin, 0:pin]),
                        waits=w if k == 0 else [])
            psrc = (lambda b=b, rows=rows, n=n, pin=pin:
                    pTb[b][0:rows, 0:n * pin].rearrange("p (a w) -> p a w", a=n))
            if scale2 is None:
                ev = dve(lambda e, i0=i0, n=n, psrc=psrc: e.tensor_copy(
                    dstf(i0, n), psrc()), waits=[tk])
            else:
                s1f, s2f = scale2
                ev = dve(lambda e, i0=i0, n=n, psrc=psrc, s1f=s1f: e.tensor_tensor(
                    dstf(i0, n), psrc(),
                    s1f(i0, n).rearrange("p (a u) -> p a u", u=1)
                    .broadcast_to([128, n, 128]), ALU.mult), waits=[tk])
                ev = dve(lambda e, i0=i0, n=n, s2f=s2f: e.tensor_tensor(
                    dstf(i0, n), dstf(i0, n),
                    s2f(i0, n).rearrange("p (a u) -> p a u", u=1)
                    .broadcast_to([128, n, 128]), ALU.add), drain=True)
            guard["b%d" % b] = ev
        return ev


    def newton_rsqrt(dst, src, width=1):
        iv = ismall[:, 0:width]
        t2 = small[:, 72:72 + width]
        u0 = small[:, 88:88 + width]
        dve(lambda e: e.tensor_scalar(iv, src.bitcast(I32), 1, None,
                                      ALU.logical_shift_right), drain=True)
        dve(lambda e: e.tensor_scalar(iv, iv, -1, None, ALU.mult), drain=True)
        dve(lambda e: e.tensor_scalar(iv, iv, 0x5f3759df, None, ALU.add), drain=True)
        dve(lambda e: e.tensor_copy(dst, iv.bitcast(F32)), drain=True)
        dve(lambda e: e.tensor_scalar(t2, src, 0.5, None, ALU.mult), drain=True)
        for _ in range(4):
            dve(lambda e: e.tensor_tensor(u0, dst, dst, ALU.mult), drain=True)
            dve(lambda e: e.tensor_tensor(u0, u0, t2, ALU.mult), drain=True)
            dve(lambda e: e.tensor_scalar(u0, u0, -1.0, 1.5, ALU.mult, ALU.add), drain=True)
            dve(lambda e: e.tensor_tensor(dst, dst, u0, ALU.mult), drain=True)

    def ln_full(xa, act_waits):
        dve(lambda e: e.reduce_sum(small[:, 0:1], xa, AX.X), waits=act_waits,
            drain=True)
        a1 = act(lambda e: e.activation(scrD[:, :], xa, AF.Square,
                                        accum_out=small[:, 1:2]), waits=act_waits)
        dve(lambda e: e.tensor_scalar(small[:, 2:3], small[:, 0:1], 1.0 / D,
                                      None, ALU.mult), waits=[a1], drain=True)
        dve(lambda e: e.tensor_tensor(small[:, 3:4], small[:, 2:3],
                                      small[:, 2:3], ALU.mult), drain=True)
        dve(lambda e: e.tensor_scalar(small[:, 3:4], small[:, 3:4], -1.0, None,
                                      ALU.mult), drain=True)
        dve(lambda e: e.scalar_tensor_tensor(small[:, 3:4], small[:, 1:2],
                                             1.0 / D, small[:, 3:4], ALU.mult,
                                             ALU.add), drain=True)
        dve(lambda e: e.tensor_scalar(small[:, 3:4], small[:, 3:4], 1e-5, None,
                                      ALU.add), drain=True)
        newton_rsqrt(small[:, 8:9], small[:, 3:4])
        dve(lambda e: e.tensor_tensor(small[:, 9:10], small[:, 2:3],
                                      small[:, 8:9], ALU.mult), drain=True)
        return dve(lambda e: e.tensor_scalar(small[:, 9:10], small[:, 9:10],
                                             -1.0, None, ALU.mult), drain=True)

    def transposes(blocks, pewaits=()):
        ev = None
        first = True
        for blk in blocks:
            (srcf, dstf, rows) = blk[:3]
            pin = blk[3] if len(blk) > 3 else 128
            w = list(pewaits) if first else []
            if guard["pT"] is not None:
                w.append(guard["pT"])
            first = False
            tk = pe(lambda e, srcf=srcf, rows=rows, pin=pin: e.transpose(
                pT[0:rows, 0:pin], srcf(), ident[0:pin, 0:pin]), waits=w)
            ev = dve(lambda e, dstf=dstf, rows=rows, pin=pin: e.tensor_copy(
                dstf(), pT[0:rows, 0:pin]), waits=[tk])
            guard["pT"] = ev
        return ev

    def transposes_sc(blocks, pewaits=()):
        ev = None
        first = True
        for (srcf, dstf, s1f, s2f) in blocks:
            w = list(pewaits) if first else []
            if guard["pT"] is not None:
                w.append(guard["pT"])
            first = False
            tk = pe(lambda e, srcf=srcf: e.transpose(
                pT[:, 0:128], srcf(), ident[:, :]), waits=w)
            ev = dve(lambda e, dstf=dstf, s1f=s1f, s2f=s2f: e.tensor_scalar(
                dstf(), pT[:, 0:128], s1f(), s2f(), ALU.mult, ALU.add),
                waits=[tk])
            guard["pT"] = ev
        return ev

    for f in [lambda e: e.dma_start(ident[:, :], dident),
              lambda e: e.dma_start(b2col[:, :], db2col),
              lambda e: e.dma_start(beta[:, :], dbeta)]:
        sp(f, "const")
    for l in range(DEPTH):
        sp(lambda e, l=l: e.dma_start(lng[:, 8 * l:8 * l + 8], dlng[l]), "const")
        sp(lambda e, l=l: e.dma_start(lnb[:, 8 * l:8 * l + 8], dlnb[l]), "const")
        sp(lambda e, l=l: e.dma_start(qsc[:, DH * l:DH * l + DH], dqsc[l]), "const")
        sp(lambda e, l=l: e.dma_start(ksc[:, DH * l:DH * l + DH], dksc[l]), "const")
        sp(lambda e, l=l: e.dma_start(qscr[:, DH * l:DH * l + DH], dqscr[l]), "const")
        sp(lambda e, l=l: e.dma_start(kscr[:, DH * l:DH * l + DH], dkscr[l]), "const")
    tok_const = (P.chan("d_const", 16), P.chan("d_const", 16).n)
    gp(lambda e: e.dma_start(wspin, dwspinT), "w0")
    gp(lambda e: e.dma_start(wj, dwjT), "w0")
    gp(lambda e: e.dma_start(wg, dwgT), "w0")
    gp(lambda e: e.dma_start(bspin, dbspin), "w0")
    tok_w0 = (P.chan("d_w0", 16), P.chan("d_w0", 16).n)

    d0 = dve(lambda e: e.memset(onesc[:, :], 1.0), waits=[tok_const, tok_w0])
    dve(lambda e: e.memset(onesr[:, :], 1.0))
    dve(lambda e: e.memset(kTr[:, :], 0.0))
    dve(lambda e: e.memset(vring[:, :], 0.0))
    dve(lambda e: e.memset(O_acc, 0.0))
    z = dve(lambda e: e.memset(scrA[:, :], 0.0))
    gp(lambda e: e.dma_start(dxA[0:128, :], scrA[:, :]), "pad", waits=[z])
    zz = gp(lambda e: e.dma_start(dxB[0:128, :], scrA[:, :]), "pad")
    act(lambda e: e.activation(small[:, 120:121], beta[:, 0:1], AF.Copy),
        waits=[tok_const, zz])

    war_xin = [None, None]
    s_done = None
    st_s_prev = None
    for j in range(NS):
        sl = j % NLD
        lw = [war_xin[sl]] if war_xin[sl] else []
        ld = sp(lambda e, j=j, sl=sl: e.dma_start(
            xin[:, sl * D:(sl + 1) * D], dx[j * 128:(j + 1) * 128, :]),
            "xin%d" % sl, waits=lw)
        xa = xin[:, sl * D:(sl + 1) * D]
        nm = ln_full(xa, [ld])
        lnt = act(lambda e, xa=xa: e.activation(lnT, xa, AF.Identity,
                                                bias=small[:, 9:10],
                                                scale=small[:, 8:9]), waits=[nm])
        war_xin[sl] = lnt
        tv = tbatch([lambda i=i: lnT[:, 128 * i:128 * (i + 1)] for i in range(8)],
                    lambda i0, n: lnTT[:, 128 * i0:128 * (i0 + n)]
                    .rearrange("p (a w) -> p a w", a=n), pewaits=[lnt])
        sv = None
        for ch in range(2):
            pp = pA if ch == 0 else pB
            for k in range(8):
                mmi = pe(lambda e, pp=pp, k=k, ch=ch: e.matmul(
                    pp[:, :], lnTT[:, 128 * k:128 * (k + 1)],
                    wspin[:, D * k + 512 * ch: D * k + 512 * (ch + 1)],
                    start=(k == 0), stop=(k == 7)),
                    waits=[tv] if k == 0 else [])
            sv = dve(lambda e, pp=pp, ch=ch: e.tensor_tensor(
                stile[:, 512 * ch:512 * (ch + 1)], pp[:, :],
                bspin[:, 512 * ch:512 * (ch + 1)], ALU.add),
                waits=[mmi] + ([st_s_prev] if (ch == 0 and st_s_prev) else []),
                drain=True)
            s_done = pe(lambda e, ch=ch, j=j: e.matmul(
                pMF[:, 512 * ch:512 * (ch + 1)], onesc[:, :],
                stile[:, 512 * ch:512 * (ch + 1)],
                start=(j == 0), stop=(j == NS - 1), skip_group_check=True),
                waits=[sv])
        st_s_prev = gp(lambda e, j=j: e.dma_start(
            ds[j * 128:(j + 1) * 128, :], stile[:, :]), "st_s%d" % (j % 2), waits=[s_done])
    tok_slast = None

    mfev = dve(lambda e: e.tensor_scalar(mf_sb, pMF[0:1, :], 1.0 / c.T, None,
                                         ALU.mult), waits=[s_done])
    cast = dve(lambda e: e.tensor_copy(gi_sb, mf_sb), waits=[mfev], drain=True)
    tv = transposes(
        [(lambda k=k: gi_sb[0:1, 128 * k:128 * (k + 1)],
          lambda k=k: mfb[:, k:k + 1], 128, 1) for k in range(8)],
        pewaits=[cast])
    gv = None
    for ch in range(2):
        pp = pA if ch == 0 else pB
        for k in range(8):
            gmm = pe(lambda e, pp=pp, k=k, ch=ch: e.matmul(
                pp[0:1, :], mfb[:, k:k + 1],
                wg[:, D * k + 512 * ch: D * k + 512 * (ch + 1)],
                start=(k == 0), stop=(k == 7)),
                waits=[tv] if (ch == 0 and k == 0) else [])
        gv = dve(lambda e, pp=pp, ch=ch: e.tensor_copy(
            gi_sb[0:1, 512 * ch:512 * (ch + 1)], pp[0:1, :]),
            waits=[gmm, cast], drain=True)
    gbv = None
    for ch in range(2):
        pp = pB if ch == 0 else pA
        bmm = pe(lambda e, pp=pp, ch=ch: e.matmul(
            pp[:, :], onesr[0:1, :], gi_sb[0:1, 512 * ch:512 * (ch + 1)],
            start=True, stop=True), waits=[gv] if ch == 0 else [])
        gbv = dve(lambda e, pp=pp, ch=ch: e.tensor_copy(
            gb[:, 512 * ch:512 * (ch + 1)], pp[:, :]), waits=[bmm])

    INV32 = 1.0 / 32.0
    epi_store = None
    war_scrB = gbv
    war_sst = [None, None]
    war_sq = None
    war_sqT = None
    for g in range(NW // QG):
        for qt in range(QG):
            sl = qt % 2
            lw = [war_sst[sl]] if war_sst[sl] else []
            if g == 0 and qt < 2:
                lw.extend(snap("st_s0", "st_s1"))
            ldq = sp(lambda e, g=g, qt=qt, sl=sl: e.dma_start(
                sstr[sl][:, 0:D], ds[(g * QG + qt) * 128:(g * QG + qt + 1) * 128, :]),
                "sst%d" % sl, waits=lw)
            sqT3 = sqT.rearrange("p (i w) -> p i w", i=8)
            ev = tbatch([lambda i=i, sl=sl: sstr[sl][:, 128 * i:128 * (i + 1)]
                         for i in range(8)],
                        lambda i0, n, qt=qt, sqT3=sqT3: sqT3[:, i0:i0 + n,
                                                            128 * qt:128 * (qt + 1)],
                        pewaits=[ldq] + ([war_sqT] if war_sqT else []))
            war_sst[sl] = ev
        nch = (QW + 511) // 512
        eev = None
        for o in range(8):
            for ch in range(nch):
                w0 = 512 * ch
                w1c = min(512, QW - w0)
                pp = pA if (o * nch + ch) % 2 == 0 else pB
                for k in range(8):
                    emm = pe(lambda e, pp=pp, o=o, k=k, w0=w0, w1c=w1c: e.matmul(
                        pp[:, 0:w1c], wj[:, D * k + 128 * o: D * k + 128 * (o + 1)],
                        sqT[:, QW * k + w0: QW * k + w0 + w1c],
                        start=(k == 0), stop=(k == 7)),
                        waits=[ev] if (o == 0 and ch == 0 and k == 0) else [])
                eev = dve(lambda e, pp=pp, o=o, w0=w0, w1c=w1c: e.tensor_copy(
                    ET[:, QW * o + w0: QW * o + w0 + w1c], pp[:, 0:w1c]),
                    waits=[emm])
                pe(lambda e: e.nop(), waits=[eev])
        war_sqT = eev
        dve(lambda e: e.memset(mrow[:, 0:QG], NEG), waits=[eev])
        im = dve(lambda e: e.memset(lrow[:, 0:QG], 0.0))
        prev_pact = {0: None, 1: None}
        prev_ptk = None
        prev_oupd = None
        step = 0
        for mb in range(NS // MB):
            par = mb % 2
            mold = mrow[:, QG * par: QG * (par + 1)]
            mnew = mrow[:, QG * (1 - par): QG * (2 - par)]
            lold = lrow[:, QG * par: QG * (par + 1)]
            lnew = lrow[:, QG * (1 - par): QG * (2 - par)]
            ssl = mb % 2
            lw = [war_sst[ssl]] if war_sst[ssl] else []
            ldb = sp(lambda e, mb=mb, ssl=ssl: e.dma_start(
                sstr[ssl].rearrange("p (b d) -> p b d", b=MB),
                ds[MB * mb * 128:(MB * mb + MB) * 128, :].rearrange(
                    "(b p) d -> p b d", p=128)), "sst%d" % ssl, waits=lw)
            sTmv = a_(0, 8 * MW)
            sTm3 = sTmv.rearrange("p (i w) -> p i w", i=8)
            stv = None
            srcs = []
            for sl4 in range(MB):
                for i in range(8):
                    srcs.append(lambda ssl=ssl, sl4=sl4, i=i:
                                sstr[ssl][:, D * sl4 + 128 * i: D * sl4 + 128 * (i + 1)])
            for sl4 in range(MB):
                stv = tbatch(srcs[8 * sl4:8 * (sl4 + 1)],
                             lambda i0, n, sl4=sl4, sTm3=sTm3:
                             sTm3[:, i0:i0 + n, 128 * sl4:128 * (sl4 + 1)],
                             pewaits=([ldb] + ([im] if mb == 0 else []))
                             if sl4 == 0 else [ldb])
            war_sst[ssl] = stv
            for qt in range(QG):
                par2 = step % 2
                pp = pA if par2 == 0 else pB
                sw = [stv] if qt == 0 else []
                if prev_pact[par2] is not None:
                    sw.append(prev_pact[par2])
                for k in range(8):
                    smm = pe(lambda e, pp=pp, k=k, qt=qt: e.matmul(
                        pp[:, 0:MW], ET[:, QW * k + 128 * qt: QW * k + 128 * (qt + 1)],
                        sTmv[:, MW * k: MW * (k + 1)],
                        start=(k == 0), stop=(k == 7)), waits=sw if k == 0 else [])
                tmx = small[:, 16:17]
                dve(lambda e, pp=pp: e.reduce_max(tmx, pp[:, 0:MW], AX.X),
                    waits=[smm], drain=True)
                dve(lambda e, qt=qt, mold=mold, mnew=mnew: e.tensor_tensor(
                    mnew[:, qt:qt + 1], mold[:, qt:qt + 1], tmx, ALU.max),
                    drain=True)
                ng = dve(lambda e, mnew=mnew, qt=qt: e.tensor_scalar(
                    small[:, 17:18], mnew[:, qt:qt + 1], -INV32, None, ALU.mult),
                    drain=True)
                aw = [ng]
                if prev_ptk is not None:
                    aw.append(prev_ptk)
                pact = act(lambda e, pp=pp: e.activation(
                    Pa_sb, pp[:, 0:MW], AF.Exp, bias=small[:, 17:18],
                    scale=INV32, accum_out=small[:, 18:19]), waits=aw)
                prev_pact[par2] = pact
                aact = act(lambda e, qt=qt, mold=mold: e.activation(
                    small[:, 19:20], mold[:, qt:qt + 1], AF.Exp,
                    bias=small[:, 17:18], scale=INV32))
                lup = dve(lambda e, qt=qt, lold=lold, lnew=lnew: e.scalar_tensor_tensor(
                    lnew[:, qt:qt + 1], lold[:, qt:qt + 1], small[:, 19:20],
                    small[:, 18:19], ALU.mult, ALU.add), waits=[pact, aact],
                    drain=True)
                ptk = tbatch([lambda ck=ck: Pa_sb[:, 128 * ck:128 * (ck + 1)]
                              for ck in range(MB)],
                             lambda i0, n: PT_a[:, 128 * i0:128 * (i0 + n)]
                             .rearrange("p (a w) -> p a w", a=n),
                             pewaits=[pact], group=MB)
                prev_ptk = ptk
                pvw = [ptk]
                if prev_oupd is not None:
                    pvw.append(prev_oupd)
                for ck in range(MB):
                    for ch2 in range(2):
                        pvm = pe(lambda e, ck=ck, ch2=ch2, ssl=ssl: e.matmul(
                            pD[:, 512 * ch2:512 * (ch2 + 1)],
                            PT_a[:, 128 * ck:128 * (ck + 1)],
                            sstr[ssl][:, D * ck + 512 * ch2: D * ck + 512 * (ch2 + 1)],
                            start=(ck == 0), stop=(ck == MB - 1)),
                            waits=pvw if (ck == 0 and ch2 == 0) else [])
                prev_oupd = dve(lambda e, qt=qt: e.scalar_tensor_tensor(
                    O_acc[:, D * qt: D * (qt + 1)], O_acc[:, D * qt: D * (qt + 1)],
                    small[:, 19:20], pD[:, :], ALU.mult, ALU.add),
                    waits=[pvm, lup], drain=True)
                step += 1
            war_sst[ssl] = pvm
        fpar = (NS // MB) % 2
        lfin = lrow[:, QG * fpar: QG * (fpar + 1)]
        for qt in range(QG):
            q = g * QG + qt
            sl = q % NLD
            lw = [war_xin[sl]] if war_xin[sl] else []
            ldx = sp(lambda e, q=q, sl=sl: e.dma_start(
                xin[:, sl * D:(sl + 1) * D], dx[q * 128:(q + 1) * 128, :]),
                "xin%d" % sl, waits=lw)
            lws = [war_sq] if war_sq else []
            ldsq = sp(lambda e, q=q: e.dma_start(
                s_q[:, :], ds[q * 128:(q + 1) * 128, :]), "sq", waits=lws)
            rv = small[:, 20:21]
            dve(lambda e, qt=qt, lfin=lfin: e.reciprocal(rv, lfin[:, qt:qt + 1]),
                waits=[prev_oupd] if qt == 0 else [], drain=True)
            dve(lambda e, qt=qt: e.tensor_scalar(
                scrA[:, :], O_acc[:, D * qt: D * (qt + 1)], rv, None, ALU.mult),
                drain=True)
            war_sq = dve(lambda e: e.tensor_tensor(scrA[:, :], scrA[:, :],
                                                   s_q[:, :], ALU.add),
                         waits=[ldsq], drain=True)
            tt = dve(lambda e: e.tensor_tensor(scrA[:, :], scrA[:, :], gb[:, :],
                                               ALU.add), drain=True)
            taw = [tt]
            if war_scrB is not None:
                taw.append(war_scrB)
            ta = act(lambda e: e.activation(scrB[:, :], scrA[:, :], AF.Tanh,
                                            scale=beta[:, 0:1]), waits=taw)
            lnd = ln_full(scrB[:, :], [ta])
            lno = act(lambda e: e.activation(scrA[:, :], scrB[:, :], AF.Identity,
                                             bias=small[:, 9:10],
                                             scale=small[:, 8:9]), waits=[lnd])
            x1v = dve(lambda e, sl=sl: e.tensor_tensor(
                xo_sb[:, 0:D], scrA[:, :], xin[:, sl * D:(sl + 1) * D], ALU.add),
                waits=[lno, ldx], drain=True)
            war_xin[sl] = x1v
            st1 = gp(lambda e, q=q: e.dma_start(
                dx1[q * 128:(q + 1) * 128, :], xo_sb[:, 0:D]), "st_xone%d" % (q % 2), waits=[x1v])
            lnd2 = ln_full(xo_sb[:, 0:D], [st1])
            fv = act(lambda e: e.activation(scrB[:, :], xo_sb[:, 0:D], AF.Identity,
                                            bias=small[:, 9:10],
                                            scale=small[:, 8:9]), waits=[lnd2])
            st2 = gp(lambda e, q=q: e.dma_start(
                dxA[(1 + q) * 128:(2 + q) * 128, :], scrB[:, :]), "st_f%d" % (q % 2), waits=[fv])
            war_scrB = st2
            epi_store = st2

    def load_weights_attn(l, waits):
        for ch in range(8):
            t = gp(lambda e, l=l, ch=ch: e.dma_start(
                wqkv[:, 3 * D * ch:3 * D * (ch + 1)],
                dwqkv[l][:, 3 * D * ch:3 * D * (ch + 1)]),
                "wq", waits=waits if ch == 0 else [])
        for ch in range(2):
            t = gp(lambda e, l=l, ch=ch: e.dma_start(
                wo[:, 4 * D * ch:4 * D * (ch + 1)],
                dwo[l][:, 4 * D * ch:4 * D * (ch + 1)]), "wq")
        return t

    def load_weights_w1a(l, waits):
        for ch in range(8):
            t = gp(lambda e, l=l, ch=ch: e.dma_start(
                w1a[:, 3072 * ch:3072 * (ch + 1)],
                dw1a[l][:, 3072 * ch:3072 * (ch + 1)]),
                "w1a", waits=waits if ch == 0 else [])
        return t

    def load_weights_w1b(l, waits):
        for ch in range(8):
            t = gp(lambda e, l=l, ch=ch: e.dma_start(
                w1b[:, 2388 * ch:2388 * (ch + 1)],
                dw1b[l][:, 2388 * ch:2388 * (ch + 1)]),
                "w1b", waits=waits if ch == 0 else [])
        return t

    def load_weights_w2(l, waits):
        for ch in range(11):
            t = gp(lambda e, l=l, ch=ch: e.dma_start(
                w2[:, 2 * D * ch:2 * D * (ch + 1)],
                dw2[l][:, 2 * D * ch:2 * D * (ch + 1)]),
                "w2", waits=waits if ch == 0 else [])
        return t

    def build_b1b(l, waits):
        cb = gp(lambda e, l=l: e.dma_start(b1b[0:1, 0:FF2], db1[l]), "b1", waits=waits)
        ev = None
        for ch in range((FF2 + 511) // 512):
            w0 = 512 * ch
            w1c = min(512, FF2 - w0)
            pp = pA if ch % 2 == 0 else pB
            bm = pe(lambda e, pp=pp, w0=w0, w1c=w1c: e.matmul(
                pp[:, 0:w1c], onesr[0:1, :], b1b[0:1, w0:w0 + w1c],
                start=True, stop=True), waits=[cb] if ch == 0 else [])
            ev = dve(lambda e, pp=pp, w0=w0, w1c=w1c: e.tensor_copy(
                b1b[:, w0:w0 + w1c], pp[:, 0:w1c]), waits=[bm])
            pe(lambda e: e.nop(), waits=[ev])
        return ev

    war = {"xin": war_xin, "xbf": [None, None], "xbsh": [None, None],
           "mask": [None, None], "rope": [None, None],
           "xo": [None, None], "m_sb": None, "mload": None, "xsh": [None, None],
           "diag": None, "lnTf1b": None, "lnTst": None, "rq8": None,
           "lsum": [None, None, None, None]}

    def attn_tile(l, j, src, dst, extra_w):
        sl = j % NLD
        r0 = (1 + j) * 128

        def lw(key):
            return tokl(war[key][sl])

        ld1 = sp(lambda e, sl=sl, r0=r0, src=src: e.dma_start(
            xin[:, sl * D:(sl + 1) * D], src[r0:r0 + 128, :]),
            "xin%d" % sl, waits=lw("xin"))
        ld2 = gp(lambda e, sl=sl, r0=r0, src=src: e.dma_start(
            xbf[sl], src[r0:r0 + 128, :]), "xbf%d" % sl, waits=lw("xbf"))
        ld3 = gp(lambda e, sl=sl, r0=r0, src=src: e.dma_start(
            xbsh[sl], src[r0 - 1:r0 + 127, 512:1024]),
            "xbsh%d" % sl, waits=lw("xbsh"))
        ld4 = sp(lambda e, sl=sl, j=j: e.dma_start(
            maskt[sl], dmask[j * 128:(j + 1) * 128, :]),
            "mask%d" % sl, waits=lw("mask"))
        ld5 = sp(lambda e, sl=sl, j=j: e.dma_start(
            ropet[sl], drope[j * 128:(j + 1) * 128, :]),
            "rope%d" % sl, waits=lw("rope"))

        def xs_blk(i, sl=sl):
            if i < 4:
                return xbf[sl][:, 128 * i:128 * (i + 1)]
            return xbsh[sl][:, 128 * (i - 4):128 * (i - 3)]
        tv = tbatch([lambda i=i: xs_blk(i) for i in range(8)],
                    lambda i0, n: lnT[:, 128 * i0:128 * (i0 + n)]
                    .rearrange("p (a w) -> p a w", a=n),
                    pewaits=[ld2, ld3] + extra_w)
        war["xbf"][sl] = tv
        war["xbsh"][sl] = tv
        evs = []
        for ch in range(6):
            pp = pA if ch % 2 == 0 else pB
            for k in range(8):
                mm = pe(lambda e, pp=pp, ch=ch, k=k: e.matmul(
                    pp[:, :], lnT[:, 128 * k:128 * (k + 1)],
                    wqkv[:, 3 * D * k + 512 * ch: 3 * D * k + 512 * (ch + 1)],
                    start=(k == 0), stop=(k == 7)),
                    waits=[tv] if (ch == 0 and k == 0) else [])
            if ch < 2:
                ev = dve(lambda e, pp=pp, ch=ch: e.tensor_copy(
                    q_sb[:, 512 * ch:512 * (ch + 1)], pp[:, :]), waits=[mm])
            elif ch < 4:
                ev = dve(lambda e, pp=pp, ch=ch: e.tensor_copy(
                    k_sb[:, 512 * (ch - 2):512 * (ch - 1)], pp[:, :]), waits=[mm])
            else:
                ev = dve(lambda e, pp=pp, ch=ch, j=j: e.tensor_copy(
                    vring[:, D * (j % 3) + 512 * (ch - 4): D * (j % 3) + 512 * (ch - 3)],
                    pp[:, :]), waits=[mm])
            evs.append(ev)
            pe(lambda e: e.nop(), waits=[ev])
        qev, kev = evs[1], evs[3]
        ro = ropet[sl]
        sqa = act(lambda e: e.activation(scrD[:, :], q_sb[:, :], AF.Square),
                  waits=[qev])
        qred = dve(lambda e: e.tensor_reduce(
            small[:, 24:24 + H], scrD.ap().rearrange("p (h d) -> p h d", h=H),
            AX.X, ALU.add), waits=[sqa], drain=True)
        dve(lambda e: e.tensor_scalar(small[:, 24:24 + H], small[:, 24:24 + H],
                                      1e-24, None, ALU.add), drain=True)
        newton_rsqrt(small[:, 56:56 + H], small[:, 24:24 + H], width=H)
        rq = small[:, 56:56 + H]
        qr8 = dve(lambda e: e.tensor_scalar(rq8[:, :], rq, 8.0, None, ALU.mult),
                  waits=[war["rq8"]] if war["rq8"] else [], drain=True)
        ska = act(lambda e: e.activation(scrD[:, :], k_sb[:, :], AF.Square),
                  waits=[kev, qred])
        dve(lambda e: e.tensor_reduce(
            small[:, 24:24 + H], scrD.ap().rearrange("p (h d) -> p h d", h=H),
            AX.X, ALU.add), waits=[ska], drain=True)
        dve(lambda e: e.tensor_scalar(small[:, 24:24 + H], small[:, 24:24 + H],
                                      1e-24, None, ALU.add), drain=True)
        newton_rsqrt(small[:, 40:40 + H], small[:, 24:24 + H], width=H)
        rk = small[:, 40:40 + H]
        qsl = qsc[:, DH * l:DH * (l + 1)]
        ksl = ksc[:, DH * l:DH * (l + 1)]
        qslr = qscr[:, DH * l:DH * (l + 1)]
        kslr = kscr[:, DH * l:DH * (l + 1)]
        t1 = dve(lambda e: e.tensor_tensor(scrC[:, 0:64], qsl, ro[:, 0:64],
                                           ALU.mult), waits=[ld5], drain=True)
        t2 = dve(lambda e: e.tensor_tensor(scrC[:, 64:128], qslr, ro[:, 64:128],
                                           ALU.mult), drain=True)
        t3 = dve(lambda e: e.tensor_tensor(scrC[:, 128:192], ksl, ro[:, 128:192],
                                           ALU.mult), drain=True)
        t4 = dve(lambda e: e.tensor_tensor(scrC[:, 192:256], kslr, ro[:, 192:256],
                                           ALU.mult), drain=True)
        war["rope"][sl] = t4

        def reh(x):
            return x.rearrange("p (h d) -> p h d", h=H)

        def reh2(x):
            return x.rearrange("p (h t d) -> p h t d", h=H, t=2)

        def reA(x):
            return x.rearrange("p (h d) -> p h d", h=H)
        dve(lambda e: e.tensor_scalar(reh2(rotb)[:, :, 0, :],
                                      reh2(q_sb)[:, :, 1, :], -1.0, None,
                                      ALU.mult), drain=True)
        dve(lambda e: e.tensor_copy(reh2(rotb)[:, :, 1, :],
                                    reh2(q_sb)[:, :, 0, :]), drain=True)
        dve(lambda e: e.tensor_tensor(reA(scrA.ap()), reh(q_sb),
                                      brd(scrC[:, 0:64], H, 64), ALU.mult),
            waits=[t1], drain=True)
        dve(lambda e: e.tensor_tensor(reA(scrB.ap()), reh(rotb),
                                      brd(scrC[:, 64:128], H, 64), ALU.mult),
            waits=[t2], drain=True)
        qpv = dve(lambda e: e.tensor_tensor(qp[:, :], scrA[:, :], scrB[:, :],
                                            ALU.add), drain=True)
        dve(lambda e: e.tensor_scalar(reh2(rotb)[:, :, 0, :],
                                      reh2(k_sb)[:, :, 1, :], -1.0, None,
                                      ALU.mult), drain=True)
        dve(lambda e: e.tensor_copy(reh2(rotb)[:, :, 1, :],
                                    reh2(k_sb)[:, :, 0, :]), drain=True)
        dve(lambda e: e.tensor_tensor(reA(scrA.ap()), reh(k_sb),
                                      brd(scrC[:, 128:192], H, 64), ALU.mult),
            waits=[t3], drain=True)
        dve(lambda e: e.tensor_tensor(reA(scrB.ap()), reh(rotb),
                                      brd(scrC[:, 192:256], H, 64), ALU.mult),
            waits=[t4], drain=True)
        dve(lambda e: e.tensor_tensor(scrA[:, :], scrA[:, :], scrB[:, :],
                                      ALU.add), drain=True)
        kpv = dve(lambda e: e.tensor_tensor(
            reA(kp), reA(scrA.ap()),
            rk.rearrange("p (h d) -> p h d", d=1).broadcast_to([128, H, DH]),
            ALU.mult), drain=True)
        tbatch([lambda hh=hh: qp[:, 64 * hh:64 * (hh + 1)] for hh in range(H)],
               lambda i0, n: qpT[0:64, 128 * i0:128 * (i0 + n)]
               .rearrange("p (a w) -> p a w", a=n), rows=64, pewaits=[qpv])
        tq0 = tbatch([lambda hh=hh: kp[:, 64 * hh:64 * (hh + 1)] for hh in range(H)],
                     lambda i0, n, j=j: kTr[0:64, 2048 * (j % 3) + 128 * i0:
                                            2048 * (j % 3) + 128 * (i0 + n)]
                     .rearrange("p (a w) -> p a w", a=n), rows=64, pewaits=[kpv])
        tq = dve(lambda e: e.tensor_scalar(qppT[0:64, :], qpT[0:64, :],
                                           b2col[:, 0:1], None, ALU.mult),
                 waits=[tq0], drain=True)
        even = (j % 2 == 0)
        mk = maskt[sl]
        prev_pacts = None
        prev_ptm = None
        pvm = None
        for grp in range(4):
            sw0 = [tq] if grp == 0 else list(prev_pacts)
            smm = None
            for ck in range(3):
                slot = (j - 2 + ck) % 3
                usepp = (ck < 2) if even else (ck < 1)
                src_qT = qppT if usepp else qpT
                for hh in range(4):
                    h0 = grp * 4 + hh
                    smm = pe(lambda e, hh=hh, h0=h0, ck=ck, slot=slot, src_qT=src_qT: e.matmul(
                        pS[:, 512 * hh + 128 * ck: 512 * hh + 128 * (ck + 1)],
                        src_qT[0:64, 128 * h0:128 * (h0 + 1)],
                        kTr[0:64, 2048 * slot + 128 * h0: 2048 * slot + 128 * (h0 + 1)],
                        start=True, stop=True),
                        waits=sw0 if (ck == 0 and hh == 0) else [])

            def rps():
                return pS.ap().rearrange("p (h w) -> p h w", h=4)
            dve(lambda e, mk=mk: e.tensor_tensor(
                rps()[:, :, 0:256], rps()[:, :, 0:256],
                brd(mk[:, 0:256], 4, 256), ALU.add), waits=[smm, ld4], drain=True)
            ms2 = dve(lambda e, mk=mk: e.tensor_tensor(
                rps()[:, :, 256:384], rps()[:, :, 256:384],
                brd(mk[:, 256:384], 4, 128), ALU.add), drain=True)
            war["mask"][sl] = ms2
            mx4 = small[:, 104:108]
            dve(lambda e: e.reduce_max(mx4, rps()[:, :, 0:384], AX.X),
                waits=[ms2], drain=True)
            bw = [qr8]
            if prev_pacts is not None:
                bw.append(prev_pacts[-1])
            bi4 = dve(lambda e, grp=grp: e.scalar_tensor_tensor(
                small[:, 108:112], mx4, -8.0, rq[:, 4 * grp:4 * (grp + 1)],
                ALU.mult, ALU.mult), waits=bw, drain=True)
            pacts = []
            for hh in range(4):
                h0 = grp * 4 + hh
                aw = [bi4] if hh == 0 else []
                if hh == 0 and prev_ptm is not None:
                    aw.append(prev_ptm)
                if hh == 0 and war["lsum"][grp] is not None:
                    aw.append(war["lsum"][grp])
                pacts.append(act(lambda e, hh=hh, h0=h0: e.activation(
                    P_sb[:, 384 * hh:384 * (hh + 1)],
                    pS[:, 512 * hh:512 * hh + 384], AF.Exp,
                    bias=small[:, 108 + hh:109 + hh], scale=rq8[:, h0:h0 + 1],
                    accum_out=lsum[:, h0:h0 + 1]), waits=aw))
            prev_pacts = pacts
            war["rq8"] = pacts[-1]
            rcp = dve(lambda e, grp=grp: e.reciprocal(
                rinvb[:, 4 * grp:4 * (grp + 1)], lsum[:, 4 * grp:4 * (grp + 1)]),
                waits=list(pacts), drain=True)
            war["lsum"][grp] = rcp
            pw = [pacts[3]] + tokl(war["diag"])
            for hh in range(4):
                ppt = (pA if hh < 2 else pB).ap().bitcast(BF16)
                base = 384 * (hh % 2)
                for ck in range(3):
                    ptm = pe(lambda e, ppt=ppt, base=base, hh=hh, ck=ck: e.transpose(
                        ppt[:, base + 128 * ck: base + 128 * (ck + 1)],
                        P_sb[:, 384 * hh + 128 * ck: 384 * hh + 128 * (ck + 1)],
                        ident[:, :]),
                        waits=pw if (hh == 0 and ck == 0) else [])
            prev_ptm = ptm
            pev0 = dve(lambda e: e.tensor_copy(
                PT_sb[:, 0:768], pA.ap().bitcast(BF16)[:, 0:768]), waits=[ptm])
            pev1 = dve(lambda e: e.tensor_copy(
                PT_sb[:, 768:1536], pB.ap().bitcast(BF16)[:, 0:768]))
            war["diag"] = [pev0, pev1]
            for hh in range(4):
                h0 = grp * 4 + hh
                pcol = 64 * (8 * (h0 % 2) + h0 // 2)
                for ck in range(3):
                    slot = (j - 2 + ck) % 3
                    pvm = pe(lambda e, hh=hh, h0=h0, ck=ck, slot=slot, pcol=pcol: e.matmul(
                        pD[:, pcol:pcol + 64],
                        PT_sb[:, 384 * hh + 128 * ck: 384 * hh + 128 * (ck + 1)],
                        vring[:, D * slot + 64 * h0: D * slot + 64 * (h0 + 1)],
                        start=(ck == 0), stop=(ck == 2)),
                        waits=[pev0, pev1] if (hh == 0 and ck == 0) else [])
        rinv4d = rinvb.ap().rearrange("p (a b) -> p a b", b=2) \
            .rearrange("p a (b u) -> p a b u", u=1).broadcast_to([128, 8, 2, DH])
        atv = dve(lambda e: e.tensor_tensor(
            at_sb.rearrange("p (a b d) -> p a b d", a=8, b=2),
            pD.ap().rearrange("p (b a d) -> p a b d", b=2, a=8),
            rinv4d, ALU.mult), waits=[pvm], drain=True)
        tv2 = tbatch([lambda i=i: at_sb[:, 128 * i:128 * (i + 1)] for i in range(8)],
                     lambda i0, n: aT_sb[:, 128 * i0:128 * (i0 + n)]
                     .rearrange("p (a w) -> p a w", a=n), pewaits=[atv])
        for ch in range(2):
            for k in range(8):
                wm = pe(lambda e, ch=ch, k=k: e.matmul(
                    pD[:, 512 * ch:512 * (ch + 1)], aT_sb[:, 128 * k:128 * (k + 1)],
                    wo[:, D * k + 512 * ch: D * k + 512 * (ch + 1)],
                    start=(k == 0), stop=(k == 7)),
                    waits=[tv2, atv] if (ch == 0 and k == 0) else [])
        xo = xo_sb[:, sl * D:(sl + 1) * D]
        xov = dve(lambda e, xo=xo, sl=sl: e.tensor_tensor(
            xo, pD[:, :], xin[:, sl * D:(sl + 1) * D], ALU.add),
            waits=[wm, ld1] + lw("xo"), drain=True)
        war["xin"][sl] = xov
        if datt is not None and l == 0:
            gp(lambda e, j=j: e.dma_start(datt[j * 128:(j + 1) * 128, :],
                                          at_sb[:, :]), "st_att", waits=[atv])
        stv = gp(lambda e, xo=xo, r0=r0, dst=dst: e.dma_start(
            dst[r0:r0 + 128, :], xo), "st_x%d" % sl, waits=[xov])
        war["xo"][sl] = stv
        return stv

    def f1a_tile(l, j, src, extra_w):
        sl = j % NLD
        r0 = (1 + j) * 128
        ld1 = sp(lambda e, sl=sl, r0=r0, src=src: e.dma_start(
            xin[:, sl * D:(sl + 1) * D], src[r0:r0 + 128, :]),
            "xin%d" % sl, waits=tokl(war["xin"][sl]))
        ld2 = sp(lambda e, sl=sl, r0=r0, src=src: e.dma_start(
            xshf[sl], src[r0 - 1:r0 + 127, 512:1024]),
            "xsh%d" % sl, waits=tokl(war["xsh"][sl]))
        xa = xin[:, sl * D:(sl + 1) * D]
        xb = xshf[sl]
        dve(lambda e, xa=xa: e.reduce_sum(small[:, 0:1], xa[:, 0:512], AX.X),
            waits=[ld1], drain=True)
        dve(lambda e, xb=xb: e.reduce_sum(small[:, 4:5], xb, AX.X),
            waits=[ld2], drain=True)
        dve(lambda e: e.tensor_tensor(small[:, 0:1], small[:, 0:1],
                                      small[:, 4:5], ALU.add), drain=True)
        act(lambda e, xa=xa: e.activation(scrD[:, 0:512], xa[:, 0:512],
                                          AF.Square, accum_out=small[:, 1:2]),
            waits=[ld1])
        a1 = act(lambda e, xb=xb: e.activation(scrD[:, 512:1024], xb, AF.Square,
                                               accum_out=small[:, 5:6]),
                 waits=[ld2])
        dve(lambda e: e.tensor_tensor(small[:, 1:2], small[:, 1:2],
                                      small[:, 5:6], ALU.add), waits=[a1],
            drain=True)
        dve(lambda e: e.tensor_scalar(small[:, 2:3], small[:, 0:1], 1.0 / D,
                                      None, ALU.mult), drain=True)
        dve(lambda e: e.tensor_tensor(small[:, 3:4], small[:, 2:3],
                                      small[:, 2:3], ALU.mult), drain=True)
        dve(lambda e: e.tensor_scalar(small[:, 3:4], small[:, 3:4], -1.0, None,
                                      ALU.mult), drain=True)
        dve(lambda e: e.scalar_tensor_tensor(small[:, 3:4], small[:, 1:2],
                                             1.0 / D, small[:, 3:4], ALU.mult,
                                             ALU.add), drain=True)
        dve(lambda e: e.tensor_scalar(small[:, 3:4], small[:, 3:4], 1e-5, None,
                                      ALU.add), drain=True)
        newton_rsqrt(small[:, 8:9], small[:, 3:4])
        dve(lambda e: e.tensor_tensor(small[:, 9:10], small[:, 2:3],
                                      small[:, 8:9], ALU.mult), drain=True)
        nm = dve(lambda e: e.tensor_scalar(small[:, 9:10], small[:, 9:10], -1.0,
                                           None, ALU.mult), drain=True)
        na = act(lambda e, xa=xa: e.activation(q_sb[:, 0:512], xa[:, 0:512],
                                               AF.Identity, bias=small[:, 9:10],
                                               scale=small[:, 8:9]), waits=[nm])
        nb = act(lambda e, xb=xb: e.activation(q_sb[:, 512:1024], xb, AF.Identity,
                                               bias=small[:, 9:10],
                                               scale=small[:, 8:9]))
        war["xin"][sl] = nb
        war["xsh"][sl] = nb
        tvv = tbatch([lambda i=i: q_sb[:, 128 * i:128 * (i + 1)] for i in range(8)],
                     lambda i0, n: lnT[:, 128 * i0:128 * (i0 + n)]
                     .rearrange("p (a w) -> p a w", a=n),
                     pewaits=[na, nb] + extra_w + tokl(war["lnTst"]),
                     scale2=(lambda i0, n, l=l: lng[:, 8 * l + i0:8 * l + i0 + n],
                             lambda i0, n, l=l: lnb[:, 8 * l + i0:8 * l + i0 + n]))
        stx = gp(lambda e, j=j: e.dma_start(
            dxnT[j * 128:(j + 1) * 128, :], lnT[:, :]), "st_xnT%d" % (j % 2), waits=[tvv])
        war["lnTst"] = stx
        mv = _f1_chunks(j, sl, [tvv], w1a, 3072, 0, 1536,
                        [(0, 512), (512, 512), (1024, 512)])
        stv = gp(lambda e, sl=sl, j=j: e.dma_start(
            dm[j * 128:(j + 1) * 128, 0:1536], m_sb[:, 0:1536]), "st_m%d" % (j % 2), waits=[mv])
        war["m_sb"] = stv
        return stv

    def _f1_chunks(j, sl, lnw, wbuf, wstride, a_off, g_off, chunks):
        mv = None
        for ci, (cw0, wc) in enumerate(chunks):
            def bank(i):
                if i == 0:
                    return pA[:, :]
                if i == 1:
                    return pB[:, :]
                if i < 4:
                    return pS[:, 512 * (i - 2):512 * (i - 1)]
                return pD[:, 512 * (i - 4):512 * (i - 3)]
            ppa, ppg = bank((2 * ci) % 6), bank((2 * ci + 1) % 6)
            for k in range(8):
                mma = pe(lambda e, ppa=ppa, k=k, cw0=cw0, wc=wc: e.matmul(
                    ppa[:, 0:wc], lnT[:, 128 * k:128 * (k + 1)],
                    wbuf[:, wstride * k + cw0: wstride * k + cw0 + wc],
                    start=(k == 0), stop=(k == 7)),
                    waits=lnw if (ci == 0 and k == 0) else [])
            for k in range(8):
                mmg = pe(lambda e, ppg=ppg, k=k, cw0=cw0, wc=wc: e.matmul(
                    ppg[:, 0:wc], lnT[:, 128 * k:128 * (k + 1)],
                    wbuf[:, wstride * k + g_off + cw0: wstride * k + g_off + cw0 + wc],
                    start=(k == 0), stop=(k == 7)))
            bcol = a_off + cw0
            ga = dve(lambda e, ppg=ppg, bcol=bcol, wc=wc: e.tensor_tensor(
                scrA[:, 0:wc], ppg[:, 0:wc], b1b[:, FFH + bcol: FFH + bcol + wc],
                ALU.add), waits=[mmg], drain=True)
            gl = act(lambda e, wc=wc: e.activation(gl_sb[:, 0:wc], scrA[:, 0:wc],
                                                   AF.Gelu), waits=[ga])
            aa = dve(lambda e, ppa=ppa, bcol=bcol, wc=wc: e.tensor_tensor(
                aa_sb[:, 0:wc], ppa[:, 0:wc], b1b[:, bcol:bcol + wc], ALU.add),
                waits=[mma], drain=True)
            mw = [gl, aa]
            if ci == 0 and war["m_sb"]:
                mw.append(war["m_sb"])
            mv = dve(lambda e, bcol=bcol, wc=wc: e.tensor_tensor(
                m_sb[:, bcol:bcol + wc],
                aa_sb[:, 0:wc], gl_sb[:, 0:wc], ALU.mult), waits=mw, drain=True)
            pe(lambda e: e.nop(), waits=[mv])
        return mv

    def f1b_tile(l, j, extra_w):
        sl = j % NLD
        lnw0 = tokl(war["lnTf1b"])
        ldn = sp(lambda e, j=j: e.dma_start(
            lnT[:, :], dxnT[j * 128:(j + 1) * 128, :]), "lnTf", waits=lnw0)
        mv = _f1_chunks(j, sl, [ldn] + extra_w, w1b, 2388, 1536, 1194,
                        [(0, 512), (512, 512), (1024, 170)])
        war["lnTf1b"] = mv
        stv = gp(lambda e, j=j: e.dma_start(
            dm[j * 128:(j + 1) * 128, 1536:2730], m_sb[:, 1536:2730]),
            "st_m%d" % (j % 2), waits=[mv])
        war["m_sb"] = stv
        return stv

    def f2_tile(l, j, src, dst, extra_w, final):
        sl = j % NLD
        r0 = (1 + j) * 128
        ld1 = sp(lambda e, j=j: e.dma_start(
            mload, dm[j * 128:(j + 1) * 128, :]),
            "mload", waits=tokl(war["mload"]))
        ld2 = sp(lambda e, sl=sl, r0=r0, src=src: e.dma_start(
            xin[:, sl * D:(sl + 1) * D], src[r0:r0 + 128, :]),
            "xin%d" % sl, waits=tokl(war["xin"][sl]))
        msone = dve(lambda e: e.memset(mT_sb[:, 21 * 128:22 * 128], 1.0)) \
            if j == 0 else None
        ev = tbatch([lambda k=k: mload[:, 128 * k:128 * (k + 1)] for k in range(20)],
                    lambda i0, n: mT_sb[0:128, 128 * i0:128 * (i0 + n)]
                    .rearrange("p (a w) -> p a w", a=n),
                    pewaits=[ld1] + extra_w + ([msone] if msone else []))
        for k, rows in [(20, 128), (21, FFH - 21 * 128)]:
            ev = tbatch([lambda k=k, rows=rows: mload[:, 128 * k:128 * k + rows]],
                        lambda i0, n, k=k, rows=rows:
                        mT_sb[0:rows, 128 * k:128 * (k + 1)]
                        .rearrange("p (a w) -> p a w", a=1),
                        rows=rows, group=1)
        war["mload"] = ev
        for ch in range(2):
            for k in range(22):
                rows = 128 if k < 21 else (FFH - 21 * 128 + 1)
                mm = pe(lambda e, ch=ch, k=k, rows=rows: e.matmul(
                    pD[:, 512 * ch:512 * (ch + 1)],
                    mT_sb[0:rows, 128 * k:128 * k + 128],
                    w2[0:rows, D * k + 512 * ch: D * k + 512 * (ch + 1)],
                    start=(k == 0), stop=(k == 21)),
                    waits=[ev] if (ch == 0 and k == 0) else [])
        xo = xo_sb[:, sl * D:(sl + 1) * D]
        if not final:
            xov = dve(lambda e, xo=xo, sl=sl: e.tensor_tensor(
                xo, pD[:, :], xin[:, sl * D:(sl + 1) * D], ALU.add),
                waits=[mm, ld2] + ([war["xo"][sl]] if war["xo"][sl] else []),
                drain=True)
            war["xin"][sl] = xov
            stv = gp(lambda e, xo=xo, r0=r0, dst=dst: e.dma_start(
                dst[r0:r0 + 128, :], xo), "st_x%d" % sl, waits=[xov])
        else:
            x12 = dve(lambda e, sl=sl: e.tensor_tensor(
                scrB[:, :], pD[:, :], xin[:, sl * D:(sl + 1) * D], ALU.add),
                waits=[mm, ld2], drain=True)
            war["xin"][sl] = x12
            ld3 = sp(lambda e, j=j: e.dma_start(
                scrA[:, :], dx1[j * 128:(j + 1) * 128, :]), "x1ld", waits=[x12])
            lnd = ln_full(scrB[:, :], [x12])
            lno = act(lambda e: e.activation(scrB[:, :], scrB[:, :], AF.Identity,
                                             bias=small[:, 9:10],
                                             scale=small[:, 8:9]), waits=[lnd])
            xov = dve(lambda e, xo=xo: e.tensor_tensor(
                xo, scrB[:, :], scrA[:, :], ALU.add),
                waits=[lno, ld3] + ([war["xo"][sl]] if war["xo"][sl] else []),
                drain=True)
            stv = gp(lambda e, xo=xo, j=j: e.dma_start(
                dout[j * 128:(j + 1) * 128, :], xo), "st_out%d" % (j % 2), waits=[xov])
        war["xo"][sl] = stv
        return stv

    SLOTTED = {"xin", "xbf", "xbsh", "mask", "rope", "xsh", "xo", "lsum"}

    def reset_war(keys, toks):
        for k in keys:
            if k in SLOTTED:
                war[k] = [list(toks) for _ in war[k]]
            else:
                war[k] = list(toks)

    pa_done = snap("st_f0", "st_f1", "st_xone0", "st_xone1", "st_s0", "st_s1")
    wq_ready = load_weights_attn(0, pa_done)
    w1a_ready = load_weights_w1a(0, pa_done)
    src, dst = dxA, dxB
    barrier = pa_done
    for l in range(DEPTH):
        reset_war(["xbf", "xbsh", "mask", "rope", "xin", "xo"], barrier)
        for j in range(NW):
            attn_tile(l, j, src, dst, [wq_ready] if j == 0 else [])
        src, dst = dst, src
        barrier = snap("st_x0", "st_x1")
        b1_ready = build_b1b(l, barrier)
        w1b_ready = load_weights_w1b(l, barrier)
        reset_war(["xsh", "xin"], barrier)
        for j in range(NW):
            f1a_tile(l, j, src, [w1a_ready, b1_ready] if j == 0 else [])
        barrier = snap("st_x0", "st_x1", "st_m0", "st_m1", "st_xnT0", "st_xnT1")
        w2_ready = load_weights_w2(l, barrier)
        reset_war(["lnTf1b"], barrier)
        for j in range(NW):
            f1b_tile(l, j, [w1b_ready] if j == 0 else [])
        barrier = snap("st_x0", "st_x1", "st_m0", "st_m1", "st_xnT0", "st_xnT1")
        if l + 1 < DEPTH:
            wq_ready = load_weights_attn(l + 1, barrier)
        final = (l == DEPTH - 1)
        reset_war(["mload", "xin"], barrier)
        for j in range(NW):
            f2_tile(l, j, src, dst, [w2_ready] if j == 0 else [], final)
        barrier = snap("st_x0", "st_x1", "st_out0", "st_out1")
        if l + 1 < DEPTH:
            w1a_ready = load_weights_w1a(l + 1, barrier)
        src, dst = dst, src

    allw = [(ch, ch.n) for name, ch in P.chans.items()
            if name.startswith("d_") and ch.n > 0]
    P.op("sp", lambda e: None, waits=allw)
    P.op("gp", lambda e: None, waits=allw)

    with stack:
        P.emit(nc, stack)
    return nc



def kernel(**inputs):
    cfg = CFG()
    nc = build_program(cfg)
    in_maps = [stage_core(inputs, cfg, c // 2, c % 2) for c in range(cfg.NC)]
    res = run_bass_kernel_spmd(nc, in_maps, core_ids=list(range(cfg.NC)))
    out = np.zeros((cfg.NB, cfg.T, cfg.D), dtype=np.float32)
    nh0 = cfg.NSWEEP * 128
    vs1 = (cfg.NSWEEP - cfg.ROT) * 128
    for c in range(cfg.NC):
        b, h = c // 2, c % 2
        o = np.asarray(res.results[c]["out"])
        if h == 0:
            out[b, 0:nh0] = o
        else:
            out[b, nh0:] = o[vs1:]
    return out


# version 06 (final)
# speedup vs baseline: 1.2709x; 1.0033x over previous
import numpy as np
import ml_dtypes
import concourse.bass as bass
import concourse.mybir as mybir
from concourse.bass_utils import run_bass_kernel_spmd
from contextlib import ExitStack

F32 = mybir.dt.float32
BF16 = mybir.dt.bfloat16
I32 = mybir.dt.int32
AF = mybir.ActivationFunctionType
ALU = mybir.AluOpType
AX = mybir.AxisListType

NEG = -1.0e38


class CFG:
    def __init__(self, nslot=64, depth=6, qg=5, mb=4, nbatch=4, ncores=8):
        self.NSLOT = nslot
        self.DEPTH = depth
        self.NSWEEP = (nslot + 6 * depth) // 2
        assert 2 * self.NSWEEP == nslot + 6 * depth
        self.ROT = nslot - self.NSWEEP
        assert self.ROT % 2 == 0 and self.ROT > 0
        self.QG = qg
        assert self.NSWEEP % qg == 0 and qg <= 8
        self.MB = mb
        assert nslot % mb == 0 and mb <= 4
        self.T = nslot * 128
        self.D = 1024
        self.H = 16
        self.DH = 64
        self.FFH = 2730
        self.FF2 = 5460
        self.NB = nbatch
        self.NC = ncores



def _rope_tables(T, DH, WINDOW=256):
    half = np.arange(0, DH, 2, dtype=np.float64)
    inv_freq = 1.0 / 10000.0 ** (half / DH)
    t = np.arange(T, dtype=np.float64)
    freqs = t[:, None] * inv_freq[None, :]
    freqs = np.concatenate([freqs, freqs], -1)
    base = (half + 0.4 * DH) / (1.4 * DH)
    bs = np.concatenate([base, base])
    p = (t - 256.0 * np.floor(t / 256.0)) / (WINDOW // 2)
    scale = bs[None, :] ** p[:, None]
    cos, sin = np.cos(freqs), np.sin(freqs)
    return (np.concatenate([cos * scale, sin * scale, cos / scale, sin / scale],
                           axis=1).astype(np.float32),
            (bs ** 2.0).astype(np.float32))


def _masks(cfg, perm):
    n = cfg.NSWEEP
    out = np.zeros((n, 128, 384), dtype=np.float32)
    q = np.arange(128)[:, None]
    k = np.arange(128)[None, :]
    up = (k < q)
    lo = (k > q)
    for j in range(n):
        a = perm[j]
        m = out[j]
        if a - 2 < 0:
            m[:, 0:128] = NEG
        else:
            m[:, 0:128][up] = NEG
        if a - 1 < 0:
            m[:, 128:256] = NEG
        m[:, 256:384][lo] = NEG
    return out


def _kt(w, k=128):
    K, N = w.shape
    nk = (K + k - 1) // k
    pad = nk * k - K
    if pad:
        w = np.concatenate([w, np.zeros((pad, N), w.dtype)], 0)
    return np.ascontiguousarray(
        w.reshape(nk, k, N).transpose(1, 0, 2).reshape(k, nk * N))


def stage_core(inputs, cfg, b, h):
    c = cfg
    rot = 0 if h == 0 else c.ROT
    perm = (np.arange(c.NSLOT) + rot) % c.NSLOT
    x = np.asarray(inputs["L0_attn_x"][b], dtype=np.float32)
    xp = np.ascontiguousarray(x.reshape(c.NSLOT, 128, c.D)[perm].reshape(c.T, c.D))
    rope, b2 = _rope_tables(c.T, c.DH)
    rope_p = rope.reshape(c.NSLOT, 128, 256)[perm].reshape(c.T, 256)
    d = {
        "x": xp,
        "rope": rope_p,
        "b2col": np.ascontiguousarray(b2.reshape(c.DH, 1)),
        "masks": _masks(c, perm).reshape(c.NSWEEP * 128, 384).astype(ml_dtypes.bfloat16),
        "ident": np.eye(128, dtype=ml_dtypes.bfloat16),
        "wspinT": _kt(np.ascontiguousarray(
            np.asarray(inputs["L0_attn_W_spin"], np.float32).T)),
        "wjT": _kt(np.ascontiguousarray(
            np.asarray(inputs["L0_attn_W_J"], np.float32).T)),
        "wgT": _kt(np.ascontiguousarray(
            np.asarray(inputs["L0_attn_W_global"], np.float32).T)),
        "bspin": np.broadcast_to(
            np.asarray(inputs["L0_attn_b_spin"], np.float32), (128, c.D)).copy(),
        "beta": np.broadcast_to(
            np.asarray(inputs["L0_attn_beta"], np.float32).reshape(1, 1), (128, 1)).copy(),
    }
    for l in range(c.DEPTH):
        d[f"wqkv{l}"] = _kt(np.asarray(inputs["L0_ffn_Wqkv"][l], np.float32))
        d[f"wo{l}"] = _kt(np.asarray(inputs["L0_ffn_Wo"][l], np.float32))
        W1 = np.asarray(inputs["L0_ffn_W1"][l], np.float32)
        d[f"w1a{l}"] = _kt(np.ascontiguousarray(
            np.concatenate([W1[:, 0:1536], W1[:, 2730:4266]], 1)))
        d[f"w1b{l}"] = _kt(np.ascontiguousarray(
            np.concatenate([W1[:, 1536:2730], W1[:, 4266:5460]], 1)))
        w2 = np.asarray(inputs["L0_ffn_W2"][l], np.float32)
        b2r = np.asarray(inputs["L0_ffn_b2"][l], np.float32)[None, :]
        d[f"w2{l}"] = _kt(np.concatenate([w2, b2r], 0))
        d[f"b1_{l}"] = np.ascontiguousarray(
            np.asarray(inputs["L0_ffn_b1"][l], np.float32)[None, :])
        d[f"lng{l}"] = np.ascontiguousarray(
            np.asarray(inputs["L0_ffn_ln_g"][l], np.float32).reshape(8, 128).T)
        d[f"lnb{l}"] = np.ascontiguousarray(
            np.asarray(inputs["L0_ffn_ln_b"][l], np.float32).reshape(8, 128).T)
        qsl = np.asarray(inputs["L0_ffn_q_scale"][l], np.float32)
        ksl = np.asarray(inputs["L0_ffn_k_scale"][l], np.float32)
        d[f"qsc{l}"] = np.broadcast_to(qsl, (128, c.DH)).copy()
        d[f"ksc{l}"] = np.broadcast_to(ksl, (128, c.DH)).copy()
        d[f"qscr{l}"] = np.broadcast_to(np.roll(qsl, -32), (128, c.DH)).copy()
        d[f"kscr{l}"] = np.broadcast_to(np.roll(ksl, -32), (128, c.DH)).copy()
    return d



class Chan:
    __slots__ = ("name", "n", "step", "sem")

    def __init__(self, name, step=1):
        self.name, self.n, self.step, self.sem = name, 0, step, None

    def push(self):
        self.n += self.step
        return self.n


class Planner:
    ENGINES = ("pe", "dve", "act", "sp", "gp")

    def __init__(self):
        self.streams = {e: [] for e in self.ENGINES}
        self.chans = {}

    def chan(self, name, step=1):
        if name not in self.chans:
            self.chans[name] = Chan(name, step)
        return self.chans[name]

    def op(self, eng, fn, waits=(), inc=None, drain=False):
        v = inc.push() if inc is not None else None
        ws = [(ch, val) for (ch, val) in waits if val > 0]
        self.streams[eng].append((fn, ws, inc, v, drain))
        return (inc, v) if inc is not None else None

    def emit(self, nc, stack):
        waited = {}
        for ops in self.streams.values():
            for (_, ws, _, _, _) in ops:
                for (ch, val) in ws:
                    waited.setdefault(ch.name, set()).add(val)
        remap = {}
        for name, ch in self.chans.items():
            if ch.step != 1:
                continue
            vs = sorted(waited.get(name, ()))
            m = {}
            for i, v in enumerate(vs):
                m[v] = i + 1
            remap[name] = m
        for ch in self.chans.values():
            ch.sem = stack.enter_context(nc.semaphore(ch.name))
        block = stack.enter_context(nc.Block())

        def run(ename):
            def body(e):
                for fn, waits, inc, v, drain in self.streams[ename]:
                    for (ch, val) in waits:
                        m = remap.get(ch.name)
                        if m is not None:
                            val = m[val]
                        e.wait_ge(ch.sem, val)
                    if drain:
                        e.drain()
                    ins = fn(e)
                    if inc is not None:
                        m = remap.get(inc.name)
                        if m is None or v in m:
                            if ins is None:
                                ins = e.nop()
                            ins.then_inc(inc.sem, inc.step)
            return body

        block.tensor(run("pe"))
        block.vector(run("dve"))
        block.scalar(run("act"))
        block.sync(run("sp"))
        block.gpsimd(run("gp"))


def brd(ap, reps, inner):
    return ap.rearrange("p (a d) -> p a d", a=1).broadcast_to([128, reps, inner])



def build_program(cfg):
    c = cfg
    D, H, DH, FFH, FF2 = c.D, c.H, c.DH, c.FFH, c.FF2
    NS, NW, QG, MB, DEPTH = c.NSLOT, c.NSWEEP, c.QG, c.MB, c.DEPTH
    QW, MW = QG * 128, MB * 128

    nc = bass.Bass("TRN2", target_bir_lowering=False)

    dx = nc.dram_tensor("x", [c.T, D], F32, kind="ExternalInput").ap()
    drope = nc.dram_tensor("rope", [c.T, 256], F32, kind="ExternalInput").ap()
    db2col = nc.dram_tensor("b2col", [DH, 1], F32, kind="ExternalInput").ap()
    dmask = nc.dram_tensor("masks", [NW * 128, 384], BF16, kind="ExternalInput").ap()
    dident = nc.dram_tensor("ident", [128, 128], BF16, kind="ExternalInput").ap()
    dwspinT = nc.dram_tensor("wspinT", [128, 8 * D], F32, kind="ExternalInput").ap()
    dwjT = nc.dram_tensor("wjT", [128, 8 * D], F32, kind="ExternalInput").ap()
    dwgT = nc.dram_tensor("wgT", [128, 8 * D], F32, kind="ExternalInput").ap()
    dbspin = nc.dram_tensor("bspin", [128, D], F32, kind="ExternalInput").ap()
    dbeta = nc.dram_tensor("beta", [128, 1], F32, kind="ExternalInput").ap()
    dwqkv, dwo, dw1a, dw1b, dw2 = [], [], [], [], []
    db1, dlng, dlnb, dqsc, dksc = [], [], [], [], []
    dqscr, dkscr = [], []
    for l in range(DEPTH):
        dwqkv.append(nc.dram_tensor(f"wqkv{l}", [128, 8 * 3 * D], F32, kind="ExternalInput").ap())
        dwo.append(nc.dram_tensor(f"wo{l}", [128, 8 * D], F32, kind="ExternalInput").ap())
        dw1a.append(nc.dram_tensor(f"w1a{l}", [128, 8 * 3072], F32, kind="ExternalInput").ap())
        dw1b.append(nc.dram_tensor(f"w1b{l}", [128, 8 * 2388], F32, kind="ExternalInput").ap())
        dw2.append(nc.dram_tensor(f"w2{l}", [128, 22 * D], F32, kind="ExternalInput").ap())
        db1.append(nc.dram_tensor(f"b1_{l}", [1, FF2], F32, kind="ExternalInput").ap())
        dlng.append(nc.dram_tensor(f"lng{l}", [128, 8], F32, kind="ExternalInput").ap())
        dlnb.append(nc.dram_tensor(f"lnb{l}", [128, 8], F32, kind="ExternalInput").ap())
        dqsc.append(nc.dram_tensor(f"qsc{l}", [128, DH], F32, kind="ExternalInput").ap())
        dksc.append(nc.dram_tensor(f"ksc{l}", [128, DH], F32, kind="ExternalInput").ap())
        dqscr.append(nc.dram_tensor(f"qscr{l}", [128, DH], F32, kind="ExternalInput").ap())
        dkscr.append(nc.dram_tensor(f"kscr{l}", [128, DH], F32, kind="ExternalInput").ap())
    kd = "ExternalOutput" if getattr(cfg, "dbg", False) else "Internal"
    dxA = nc.dram_tensor("xA", [(1 + NS) * 128, D], F32, kind=kd).ap()
    dxB = nc.dram_tensor("xB", [(1 + NS) * 128, D], F32, kind=kd).ap()
    dx1 = nc.dram_tensor("x1s", [NW * 128, D], F32, kind=kd).ap()
    ds = nc.dram_tensor("s_hbm", [NS * 128, D], BF16, kind=kd).ap()
    dxnT = nc.dram_tensor("xnT", [NW * 128, D], BF16, kind=kd).ap()
    dm = nc.dram_tensor("mscr", [NW * 128, FFH], BF16, kind=kd).ap()
    dout = nc.dram_tensor("out", [NW * 128, D], F32, kind="ExternalOutput").ap()
    datt = (nc.dram_tensor("attdbg", [NW * 128, D], BF16, kind="ExternalOutput").ap()
            if getattr(cfg, "dbg", False) else None)

    P = Planner()
    cPE = P.chan("cPE")
    cDVE = P.chan("cDVE")
    cACT = P.chan("cACT")
    cLDs = P.chan("cLDs", 16)
    cGP = P.chan("cGP", 16)

    def pe(fn, waits=(), drain=False):
        return P.op("pe", fn, waits, cPE, drain)

    def dve(fn, waits=(), drain=False):
        return P.op("dve", fn, waits, cDVE, drain)

    def act(fn, waits=(), drain=False):
        return P.op("act", fn, waits, cACT, drain)

    def sp(fn, ch="misc_s", waits=()):
        return P.op("sp", fn, waits, P.chan("d_" + ch, 16))

    def gp(fn, ch="misc_g", waits=()):
        return P.op("gp", fn, waits, P.chan("d_" + ch, 16))

    cPOOL = P.chan("cPOOL")

    def pl(fn, waits=(), drain=False):
        return P.op("gp", fn, waits, cPOOL, drain)

    def tokl(x):
        if x is None:
            return []
        if isinstance(x, list):
            return list(x)
        return [x]

    def snap(*names):
        out = []
        for n in names:
            ch = P.chan("d_" + n, 16)
            if ch.n > 0:
                out.append((ch, ch.n))
        return out

    stack = ExitStack()

    def sb(name, p, f, dt):
        return stack.enter_context(nc.sbuf_tensor("sb_" + name, [p, f], dt))

    def ps(name, p, f, dt=F32):
        return stack.enter_context(nc.psum_tensor("ps_" + name, [p, f], dt))

    AR = sb("arena", 128, 57344, BF16)
    AR2 = sb("arena2", 128, 20480, BF16)

    def a_(off, n):
        return AR.ap()[:, off:off + n]

    def a32(off, n):
        return AR.ap().bitcast(F32)[:, off // 2: off // 2 + n]

    def b_(off, n, p=128):
        return AR2.ap()[0:p, off:off + n]

    def b32(off, n, p=128):
        return AR2.ap().bitcast(F32)[0:p, off // 2: off // 2 + n]

    wspin = a_(0, 8192)
    wj = a_(8192, 8192)
    wg = a_(16384, 8192)
    bspin = a32(49152, D)
    sqT = a_(16384, 8 * QW)
    ET = a_(24576, 8 * QW)
    O_acc = a32(32768, QG * D)
    SST = 49152
    sstr = [a_(SST, MB * D), a_(SST + MB * D, MB * D)]
    wqkv = a_(0, 24576)
    w1b = a_(0, 8 * 2388)
    w1a = a_(24576, 8 * 3072)
    w2 = a_(24576, 22 * D)
    wo = a_(49152, 8192)
    b1b = a_(49152, FF2)

    NLD = 2
    xbf = [b_(0 + 1024 * i, 1024) for i in range(2)]
    xbsh = [b_(2048 + 512 * i, 512) for i in range(2)]
    ropet = [b32(3072 + 512 * i, 256) for i in range(2)]
    maskt = [b_(4096 + 384 * i, 384) for i in range(2)]
    lnT = b_(4864, 1024)
    q_sb = b_(5888, 1024)
    k_sb = b_(6912, 1024)
    rotb = b_(7936, 1024)
    qp = b_(8960, 1024)
    kp = b_(9984, 1024)
    qpT = b_(11008, 2048, p=64)
    qppT = b_(13056, 2048, p=64)
    P_sb = b_(15104, 1536)
    PT_sb = b_(16640, 1536)
    at_sb = b_(18176, 1024)
    aT_sb = b_(19200, 1024)
    xshf = [b32(0 + 1024 * i, 512) for i in range(2)]
    gl_sb = b_(6912, 512)
    aa_sb = b_(7424, 512)
    m_sb = b_(8960, FFH)
    mload = b_(0, FFH)
    mT_sb = b_(11008, 22 * 128)
    Pa_sb = b_(0, MW)
    PT_a = b_(512, MW)
    mrow = b32(1536, 2 * QG)
    lrow = b32(1600, 2 * QG)
    s_q = b_(4864, 1024)
    stile = b_(6912, 1024)
    lnTT = b_(5888, 1024)
    mf_sb = AR2.ap().bitcast(F32)[0:1, 1024:2048]
    gi_sb = AR2.ap()[0:1, 4096:5120]
    mfb = b_(5220, 8)

    ident = sb("ident", 128, 128, BF16)
    onesc = sb("onesc", 128, 1, BF16)
    onesr = sb("onesr", 1, 128, BF16)
    b2col = sb("b2col", 64, 1, F32)
    beta = sb("beta", 128, 1, F32)
    gb = sb("gb", 128, D, BF16)
    lng = sb("lng", 128, 8 * DEPTH, F32)
    lnb = sb("lnb", 128, 8 * DEPTH, F32)
    qsc = sb("qsc", 128, DH * DEPTH, F32)
    ksc = sb("ksc", 128, DH * DEPTH, F32)
    qscr = sb("qscr", 128, DH * DEPTH, F32)
    kscr = sb("kscr", 128, DH * DEPTH, F32)
    xin = sb("xin", 128, NLD * D, F32)
    xo_sb = sb("xo_sb", 128, NLD * D, F32)
    scrA = sb("scrA", 128, D, F32)
    scrB = sb("scrB", 128, D, F32)
    scrD = sb("scrD", 128, D, BF16)
    scrC = sb("scrC", 128, 256, BF16)
    small = sb("small", 128, 128, F32)
    ismall = sb("ismall", 128, 16, I32)
    kTr = sb("kTr", 64, 3 * H * 128, BF16)
    vring = sb("vring", 128, 3 * D, BF16)
    lsum = sb("lsum", 128, H, F32)
    rinvb = sb("rinvb", 128, H, F32)
    rq8 = sb("rq8", 128, H, F32)

    pA = ps("pA", 128, 512)
    pB = ps("pB", 128, 512)
    pS = ps("pS", 128, 2048)
    pD = ps("pD", 128, 1024)
    pT = pS.ap().bitcast(BF16)[:, 2048:3072]
    pMF = pD.ap()[0:1, 0:1024]

    guard = {"pT": None, "b0": None, "b1": None}
    pTb = [pS.ap().bitcast(BF16)[:, 2048:3072],
           pS.ap().bitcast(BF16)[:, 3072:4096]]

    def tbatch(srcs, dstf, rows=128, pin=128, pewaits=(), group=4,
               scale2=None, ee="act"):
        ev = None
        gidx = 0
        first = True
        for i0 in range(0, len(srcs), group):
            n = min(group, len(srcs) - i0)
            b = gidx % 2
            gidx += 1
            w = list(pewaits) if first else []
            first = False
            if guard["b%d" % b] is not None:
                w.append(guard["b%d" % b])
            for k in range(n):
                tk = pe(lambda e, i=i0 + k, k=k, b=b, rows=rows, pin=pin:
                        e.transpose(pTb[b][0:rows, pin * k:pin * (k + 1)],
                                    srcs[i](), ident[0:pin, 0:pin]),
                        waits=w if k == 0 else [])
            psrc = (lambda b=b, rows=rows, n=n, pin=pin:
                    pTb[b][0:rows, 0:n * pin].rearrange("p (a w) -> p a w", a=n))
            if scale2 is None:
                if ee == "act":
                    ev = act(lambda e, i0=i0, n=n, psrc=psrc: e.copy(
                        dstf(i0, n), psrc()), waits=[tk])
                else:
                    ev = dve(lambda e, i0=i0, n=n, psrc=psrc: e.tensor_copy(
                        dstf(i0, n), psrc()), waits=[tk])
            else:
                s1f, s2f = scale2
                ev = dve(lambda e, i0=i0, n=n, psrc=psrc, s1f=s1f: e.tensor_tensor(
                    dstf(i0, n), psrc(),
                    s1f(i0, n).rearrange("p (a u) -> p a u", u=1)
                    .broadcast_to([128, n, 128]), ALU.mult), waits=[tk])
                ev = dve(lambda e, i0=i0, n=n, s2f=s2f: e.tensor_tensor(
                    dstf(i0, n), dstf(i0, n),
                    s2f(i0, n).rearrange("p (a u) -> p a u", u=1)
                    .broadcast_to([128, n, 128]), ALU.add), drain=True)
            guard["b%d" % b] = ev
        return ev


    def newton_rsqrt(dst, src, width=1):
        iv = ismall[:, 0:width]
        t2 = small[:, 72:72 + width]
        u0 = small[:, 88:88 + width]
        dve(lambda e: e.tensor_scalar(iv, src.bitcast(I32), 1, None,
                                      ALU.logical_shift_right), drain=True)
        dve(lambda e: e.tensor_scalar(iv, iv, -1, None, ALU.mult), drain=True)
        dve(lambda e: e.tensor_scalar(iv, iv, 0x5f3759df, None, ALU.add), drain=True)
        dve(lambda e: e.tensor_copy(dst, iv.bitcast(F32)), drain=True)
        dve(lambda e: e.tensor_scalar(t2, src, 0.5, None, ALU.mult), drain=True)
        for _ in range(4):
            dve(lambda e: e.tensor_tensor(u0, dst, dst, ALU.mult), drain=True)
            dve(lambda e: e.tensor_tensor(u0, u0, t2, ALU.mult), drain=True)
            dve(lambda e: e.tensor_scalar(u0, u0, -1.0, 1.5, ALU.mult, ALU.add), drain=True)
            dve(lambda e: e.tensor_tensor(dst, dst, u0, ALU.mult), drain=True)

    def ln_full(xa, act_waits):
        dve(lambda e: e.reduce_sum(small[:, 0:1], xa, AX.X), waits=act_waits,
            drain=True)
        a1 = act(lambda e: e.activation(scrD[:, :], xa, AF.Square,
                                        accum_out=small[:, 1:2]), waits=act_waits)
        dve(lambda e: e.tensor_scalar(small[:, 2:3], small[:, 0:1], 1.0 / D,
                                      None, ALU.mult), waits=[a1], drain=True)
        dve(lambda e: e.tensor_tensor(small[:, 3:4], small[:, 2:3],
                                      small[:, 2:3], ALU.mult), drain=True)
        dve(lambda e: e.tensor_scalar(small[:, 3:4], small[:, 3:4], -1.0, None,
                                      ALU.mult), drain=True)
        dve(lambda e: e.scalar_tensor_tensor(small[:, 3:4], small[:, 1:2],
                                             1.0 / D, small[:, 3:4], ALU.mult,
                                             ALU.add), drain=True)
        dve(lambda e: e.tensor_scalar(small[:, 3:4], small[:, 3:4], 1e-5, None,
                                      ALU.add), drain=True)
        newton_rsqrt(small[:, 8:9], small[:, 3:4])
        dve(lambda e: e.tensor_tensor(small[:, 9:10], small[:, 2:3],
                                      small[:, 8:9], ALU.mult), drain=True)
        return dve(lambda e: e.tensor_scalar(small[:, 9:10], small[:, 9:10],
                                             -1.0, None, ALU.mult), drain=True)

    def transposes(blocks, pewaits=()):
        ev = None
        first = True
        for blk in blocks:
            (srcf, dstf, rows) = blk[:3]
            pin = blk[3] if len(blk) > 3 else 128
            w = list(pewaits) if first else []
            if guard["pT"] is not None:
                w.append(guard["pT"])
            first = False
            tk = pe(lambda e, srcf=srcf, rows=rows, pin=pin: e.transpose(
                pT[0:rows, 0:pin], srcf(), ident[0:pin, 0:pin]), waits=w)
            ev = dve(lambda e, dstf=dstf, rows=rows, pin=pin: e.tensor_copy(
                dstf(), pT[0:rows, 0:pin]), waits=[tk])
            guard["pT"] = ev
        return ev

    def transposes_sc(blocks, pewaits=()):
        ev = None
        first = True
        for (srcf, dstf, s1f, s2f) in blocks:
            w = list(pewaits) if first else []
            if guard["pT"] is not None:
                w.append(guard["pT"])
            first = False
            tk = pe(lambda e, srcf=srcf: e.transpose(
                pT[:, 0:128], srcf(), ident[:, :]), waits=w)
            ev = dve(lambda e, dstf=dstf, s1f=s1f, s2f=s2f: e.tensor_scalar(
                dstf(), pT[:, 0:128], s1f(), s2f(), ALU.mult, ALU.add),
                waits=[tk])
            guard["pT"] = ev
        return ev

    for f in [lambda e: e.dma_start(ident[:, :], dident),
              lambda e: e.dma_start(b2col[:, :], db2col),
              lambda e: e.dma_start(beta[:, :], dbeta)]:
        sp(f, "const")
    for l in range(DEPTH):
        sp(lambda e, l=l: e.dma_start(lng[:, 8 * l:8 * l + 8], dlng[l]), "const")
        sp(lambda e, l=l: e.dma_start(lnb[:, 8 * l:8 * l + 8], dlnb[l]), "const")
        sp(lambda e, l=l: e.dma_start(qsc[:, DH * l:DH * l + DH], dqsc[l]), "const")
        sp(lambda e, l=l: e.dma_start(ksc[:, DH * l:DH * l + DH], dksc[l]), "const")
        sp(lambda e, l=l: e.dma_start(qscr[:, DH * l:DH * l + DH], dqscr[l]), "const")
        sp(lambda e, l=l: e.dma_start(kscr[:, DH * l:DH * l + DH], dkscr[l]), "const")
    tok_const = (P.chan("d_const", 16), P.chan("d_const", 16).n)
    gp(lambda e: e.dma_start(wspin, dwspinT), "w0")
    gp(lambda e: e.dma_start(wj, dwjT), "w0")
    gp(lambda e: e.dma_start(wg, dwgT), "w0")
    gp(lambda e: e.dma_start(bspin, dbspin), "w0")
    tok_w0 = (P.chan("d_w0", 16), P.chan("d_w0", 16).n)

    d0 = dve(lambda e: e.memset(onesc[:, :], 1.0), waits=[tok_const, tok_w0])
    dve(lambda e: e.memset(onesr[:, :], 1.0))
    dve(lambda e: e.memset(kTr[:, :], 0.0))
    dve(lambda e: e.memset(vring[:, :], 0.0))
    dve(lambda e: e.memset(O_acc, 0.0))
    z = dve(lambda e: e.memset(scrA[:, :], 0.0))
    gp(lambda e: e.dma_start(dxA[0:128, :], scrA[:, :]), "pad", waits=[z])
    zz = gp(lambda e: e.dma_start(dxB[0:128, :], scrA[:, :]), "pad")
    act(lambda e: e.activation(small[:, 120:121], beta[:, 0:1], AF.Copy),
        waits=[tok_const, zz])

    war_xin = [None, None]
    s_done = None
    st_s_prev = None
    for j in range(NS):
        sl = j % NLD
        lw = [war_xin[sl]] if war_xin[sl] else []
        ld = sp(lambda e, j=j, sl=sl: e.dma_start(
            xin[:, sl * D:(sl + 1) * D], dx[j * 128:(j + 1) * 128, :]),
            "xin%d" % sl, waits=lw)
        xa = xin[:, sl * D:(sl + 1) * D]
        nm = ln_full(xa, [ld])
        lnt = act(lambda e, xa=xa: e.activation(lnT, xa, AF.Identity,
                                                bias=small[:, 9:10],
                                                scale=small[:, 8:9]), waits=[nm])
        war_xin[sl] = lnt
        tv = tbatch([lambda i=i: lnT[:, 128 * i:128 * (i + 1)] for i in range(8)],
                    lambda i0, n: lnTT[:, 128 * i0:128 * (i0 + n)]
                    .rearrange("p (a w) -> p a w", a=n), pewaits=[lnt])
        sv = None
        for ch in range(2):
            pp = pA if ch == 0 else pB
            for k in range(8):
                mmi = pe(lambda e, pp=pp, k=k, ch=ch: e.matmul(
                    pp[:, :], lnTT[:, 128 * k:128 * (k + 1)],
                    wspin[:, D * k + 512 * ch: D * k + 512 * (ch + 1)],
                    start=(k == 0), stop=(k == 7)),
                    waits=[tv] if k == 0 else [])
            sv = dve(lambda e, pp=pp, ch=ch: e.tensor_tensor(
                stile[:, 512 * ch:512 * (ch + 1)], pp[:, :],
                bspin[:, 512 * ch:512 * (ch + 1)], ALU.add),
                waits=[mmi] + ([st_s_prev] if (ch == 0 and st_s_prev) else []),
                drain=True)
            s_done = pe(lambda e, ch=ch, j=j: e.matmul(
                pMF[:, 512 * ch:512 * (ch + 1)], onesc[:, :],
                stile[:, 512 * ch:512 * (ch + 1)],
                start=(j == 0), stop=(j == NS - 1), skip_group_check=True),
                waits=[sv])
        st_s_prev = gp(lambda e, j=j: e.dma_start(
            ds[j * 128:(j + 1) * 128, :], stile[:, :]), "st_s%d" % (j % 2), waits=[s_done])
    tok_slast = None

    mfev = dve(lambda e: e.tensor_scalar(mf_sb, pMF[0:1, :], 1.0 / c.T, None,
                                         ALU.mult), waits=[s_done])
    cast = dve(lambda e: e.tensor_copy(gi_sb, mf_sb), waits=[mfev], drain=True)
    tv = transposes(
        [(lambda k=k: gi_sb[0:1, 128 * k:128 * (k + 1)],
          lambda k=k: mfb[:, k:k + 1], 128, 1) for k in range(8)],
        pewaits=[cast])
    gv = None
    for ch in range(2):
        pp = pA if ch == 0 else pB
        for k in range(8):
            gmm = pe(lambda e, pp=pp, k=k, ch=ch: e.matmul(
                pp[0:1, :], mfb[:, k:k + 1],
                wg[:, D * k + 512 * ch: D * k + 512 * (ch + 1)],
                start=(k == 0), stop=(k == 7)),
                waits=[tv] if (ch == 0 and k == 0) else [])
        gv = dve(lambda e, pp=pp, ch=ch: e.tensor_copy(
            gi_sb[0:1, 512 * ch:512 * (ch + 1)], pp[0:1, :]),
            waits=[gmm, cast], drain=True)
    gbv = None
    for ch in range(2):
        pp = pB if ch == 0 else pA
        bmm = pe(lambda e, pp=pp, ch=ch: e.matmul(
            pp[:, :], onesr[0:1, :], gi_sb[0:1, 512 * ch:512 * (ch + 1)],
            start=True, stop=True), waits=[gv] if ch == 0 else [])
        gbv = dve(lambda e, pp=pp, ch=ch: e.tensor_copy(
            gb[:, 512 * ch:512 * (ch + 1)], pp[:, :]), waits=[bmm])

    INV32 = 1.0 / 32.0
    epi_store = None
    war_scrB = gbv
    war_sst = [None, None]
    war_sq = None
    war_sqT = None
    for g in range(NW // QG):
        for qt in range(QG):
            sl = qt % 2
            lw = [war_sst[sl]] if war_sst[sl] else []
            if g == 0 and qt < 2:
                lw.extend(snap("st_s0", "st_s1"))
            ldq = sp(lambda e, g=g, qt=qt, sl=sl: e.dma_start(
                sstr[sl][:, 0:D], ds[(g * QG + qt) * 128:(g * QG + qt + 1) * 128, :]),
                "sst%d" % sl, waits=lw)
            sqT3 = sqT.rearrange("p (i w) -> p i w", i=8)
            ev = tbatch([lambda i=i, sl=sl: sstr[sl][:, 128 * i:128 * (i + 1)]
                         for i in range(8)],
                        lambda i0, n, qt=qt, sqT3=sqT3: sqT3[:, i0:i0 + n,
                                                            128 * qt:128 * (qt + 1)],
                        pewaits=[ldq] + ([war_sqT] if war_sqT else []))
            war_sst[sl] = ev
        nch = (QW + 511) // 512
        eev = None
        for o in range(8):
            for ch in range(nch):
                w0 = 512 * ch
                w1c = min(512, QW - w0)
                pp = pA if (o * nch + ch) % 2 == 0 else pB
                for k in range(8):
                    emm = pe(lambda e, pp=pp, o=o, k=k, w0=w0, w1c=w1c: e.matmul(
                        pp[:, 0:w1c], wj[:, D * k + 128 * o: D * k + 128 * (o + 1)],
                        sqT[:, QW * k + w0: QW * k + w0 + w1c],
                        start=(k == 0), stop=(k == 7)),
                        waits=([ev] + ([gbv] if g == 0 else []))
                        if (o == 0 and ch == 0 and k == 0) else [])
                eev = dve(lambda e, pp=pp, o=o, w0=w0, w1c=w1c: e.tensor_copy(
                    ET[:, QW * o + w0: QW * o + w0 + w1c], pp[:, 0:w1c]),
                    waits=[emm])
                pe(lambda e: e.nop(), waits=[eev])
        war_sqT = eev
        dve(lambda e: e.memset(mrow[:, 0:QG], NEG), waits=[eev])
        im = dve(lambda e: e.memset(lrow[:, 0:QG], 0.0))
        prev_pact = {0: None, 1: None}
        prev_ptk = None
        prev_oupd = None
        step = 0
        for mb in range(NS // MB):
            par = mb % 2
            mold = mrow[:, QG * par: QG * (par + 1)]
            mnew = mrow[:, QG * (1 - par): QG * (2 - par)]
            lold = lrow[:, QG * par: QG * (par + 1)]
            lnew = lrow[:, QG * (1 - par): QG * (2 - par)]
            ssl = mb % 2
            lw = [war_sst[ssl]] if war_sst[ssl] else []
            ldb = sp(lambda e, mb=mb, ssl=ssl: e.dma_start(
                sstr[ssl].rearrange("p (b d) -> p b d", b=MB),
                ds[MB * mb * 128:(MB * mb + MB) * 128, :].rearrange(
                    "(b p) d -> p b d", p=128)), "sst%d" % ssl, waits=lw)
            sTmv = a_(0, 8 * MW)
            sTm3 = sTmv.rearrange("p (i w) -> p i w", i=8)
            stv = None
            srcs = []
            for sl4 in range(MB):
                for i in range(8):
                    srcs.append(lambda ssl=ssl, sl4=sl4, i=i:
                                sstr[ssl][:, D * sl4 + 128 * i: D * sl4 + 128 * (i + 1)])
            for sl4 in range(MB):
                stv = tbatch(srcs[8 * sl4:8 * (sl4 + 1)],
                             lambda i0, n, sl4=sl4, sTm3=sTm3:
                             sTm3[:, i0:i0 + n, 128 * sl4:128 * (sl4 + 1)],
                             pewaits=([ldb] + ([im] if mb == 0 else []))
                             if sl4 == 0 else [ldb])
            war_sst[ssl] = stv
            for qt in range(QG):
                par2 = step % 2
                pp = pA if par2 == 0 else pB
                sw = [stv] if qt == 0 else []
                if prev_pact[par2] is not None:
                    sw.append(prev_pact[par2])
                for k in range(8):
                    smm = pe(lambda e, pp=pp, k=k, qt=qt: e.matmul(
                        pp[:, 0:MW], ET[:, QW * k + 128 * qt: QW * k + 128 * (qt + 1)],
                        sTmv[:, MW * k: MW * (k + 1)],
                        start=(k == 0), stop=(k == 7)), waits=sw if k == 0 else [])
                tmx = small[:, 16:17]
                dve(lambda e, pp=pp: e.reduce_max(tmx, pp[:, 0:MW], AX.X),
                    waits=[smm], drain=True)
                dve(lambda e, qt=qt, mold=mold, mnew=mnew: e.tensor_tensor(
                    mnew[:, qt:qt + 1], mold[:, qt:qt + 1], tmx, ALU.max),
                    drain=True)
                ng = dve(lambda e, mnew=mnew, qt=qt: e.tensor_scalar(
                    small[:, 17:18], mnew[:, qt:qt + 1], -INV32, None, ALU.mult),
                    drain=True)
                aw = [ng]
                if prev_ptk is not None:
                    aw.append(prev_ptk)
                pact = act(lambda e, pp=pp: e.activation(
                    Pa_sb, pp[:, 0:MW], AF.Exp, bias=small[:, 17:18],
                    scale=INV32, accum_out=small[:, 18:19]), waits=aw)
                prev_pact[par2] = pact
                aact = act(lambda e, qt=qt, mold=mold: e.activation(
                    small[:, 19:20], mold[:, qt:qt + 1], AF.Exp,
                    bias=small[:, 17:18], scale=INV32))
                lup = dve(lambda e, qt=qt, lold=lold, lnew=lnew: e.scalar_tensor_tensor(
                    lnew[:, qt:qt + 1], lold[:, qt:qt + 1], small[:, 19:20],
                    small[:, 18:19], ALU.mult, ALU.add), waits=[pact, aact],
                    drain=True)
                ptk = tbatch([lambda ck=ck: Pa_sb[:, 128 * ck:128 * (ck + 1)]
                              for ck in range(MB)],
                             lambda i0, n: PT_a[:, 128 * i0:128 * (i0 + n)]
                             .rearrange("p (a w) -> p a w", a=n),
                             pewaits=[pact], group=MB)
                prev_ptk = ptk
                pvw = [ptk]
                if prev_oupd is not None:
                    pvw.append(prev_oupd)
                for ck in range(MB):
                    for ch2 in range(2):
                        pvm = pe(lambda e, ck=ck, ch2=ch2, ssl=ssl: e.matmul(
                            pD[:, 512 * ch2:512 * (ch2 + 1)],
                            PT_a[:, 128 * ck:128 * (ck + 1)],
                            sstr[ssl][:, D * ck + 512 * ch2: D * ck + 512 * (ch2 + 1)],
                            start=(ck == 0), stop=(ck == MB - 1)),
                            waits=pvw if (ck == 0 and ch2 == 0) else [])
                prev_oupd = dve(lambda e, qt=qt: e.scalar_tensor_tensor(
                    O_acc[:, D * qt: D * (qt + 1)], O_acc[:, D * qt: D * (qt + 1)],
                    small[:, 19:20], pD[:, :], ALU.mult, ALU.add),
                    waits=[pvm, lup], drain=True)
                step += 1
            war_sst[ssl] = pvm
        fpar = (NS // MB) % 2
        lfin = lrow[:, QG * fpar: QG * (fpar + 1)]
        for qt in range(QG):
            q = g * QG + qt
            sl = q % NLD
            lw = [war_xin[sl]] if war_xin[sl] else []
            ldx = sp(lambda e, q=q, sl=sl: e.dma_start(
                xin[:, sl * D:(sl + 1) * D], dx[q * 128:(q + 1) * 128, :]),
                "xin%d" % sl, waits=lw)
            lws = [war_sq] if war_sq else []
            ldsq = sp(lambda e, q=q: e.dma_start(
                s_q[:, :], ds[q * 128:(q + 1) * 128, :]), "sq", waits=lws)
            rv = small[:, 20:21]
            dve(lambda e, qt=qt, lfin=lfin: e.reciprocal(rv, lfin[:, qt:qt + 1]),
                waits=[prev_oupd] if qt == 0 else [], drain=True)
            dve(lambda e, qt=qt: e.tensor_scalar(
                scrA[:, :], O_acc[:, D * qt: D * (qt + 1)], rv, None, ALU.mult),
                drain=True)
            war_sq = dve(lambda e: e.tensor_tensor(scrA[:, :], scrA[:, :],
                                                   s_q[:, :], ALU.add),
                         waits=[ldsq], drain=True)
            tt = dve(lambda e: e.tensor_tensor(scrA[:, :], scrA[:, :], gb[:, :],
                                               ALU.add), drain=True)
            taw = [tt]
            if war_scrB is not None:
                taw.append(war_scrB)
            ta = act(lambda e: e.activation(scrB[:, :], scrA[:, :], AF.Tanh,
                                            scale=beta[:, 0:1]), waits=taw)
            lnd = ln_full(scrB[:, :], [ta])
            lno = act(lambda e: e.activation(scrA[:, :], scrB[:, :], AF.Identity,
                                             bias=small[:, 9:10],
                                             scale=small[:, 8:9]), waits=[lnd])
            x1v = dve(lambda e, sl=sl: e.tensor_tensor(
                xo_sb[:, 0:D], scrA[:, :], xin[:, sl * D:(sl + 1) * D], ALU.add),
                waits=[lno, ldx], drain=True)
            war_xin[sl] = x1v
            st1 = gp(lambda e, q=q: e.dma_start(
                dx1[q * 128:(q + 1) * 128, :], xo_sb[:, 0:D]), "st_xone%d" % (q % 2), waits=[x1v])
            lnd2 = ln_full(xo_sb[:, 0:D], [st1])
            fv = act(lambda e: e.activation(scrB[:, :], xo_sb[:, 0:D], AF.Identity,
                                            bias=small[:, 9:10],
                                            scale=small[:, 8:9]), waits=[lnd2])
            st2 = gp(lambda e, q=q: e.dma_start(
                dxA[(1 + q) * 128:(2 + q) * 128, :], scrB[:, :]), "st_f%d" % (q % 2), waits=[fv])
            war_scrB = st2
            epi_store = st2

    def load_weights_attn(l, waits):
        for ch in range(8):
            t = gp(lambda e, l=l, ch=ch: e.dma_start(
                wqkv[:, 3 * D * ch:3 * D * (ch + 1)],
                dwqkv[l][:, 3 * D * ch:3 * D * (ch + 1)]),
                "wq", waits=waits if ch == 0 else [])
        for ch in range(2):
            t = gp(lambda e, l=l, ch=ch: e.dma_start(
                wo[:, 4 * D * ch:4 * D * (ch + 1)],
                dwo[l][:, 4 * D * ch:4 * D * (ch + 1)]), "wq")
        return t

    def load_weights_w1a(l, waits):
        for ch in range(8):
            t = gp(lambda e, l=l, ch=ch: e.dma_start(
                w1a[:, 3072 * ch:3072 * (ch + 1)],
                dw1a[l][:, 3072 * ch:3072 * (ch + 1)]),
                "w1a", waits=waits if ch == 0 else [])
        return t

    def load_weights_w1b(l, waits):
        for ch in range(8):
            t = gp(lambda e, l=l, ch=ch: e.dma_start(
                w1b[:, 2388 * ch:2388 * (ch + 1)],
                dw1b[l][:, 2388 * ch:2388 * (ch + 1)]),
                "w1b", waits=waits if ch == 0 else [])
        return t

    def load_weights_w2(l, waits):
        for ch in range(11):
            t = gp(lambda e, l=l, ch=ch: e.dma_start(
                w2[:, 2 * D * ch:2 * D * (ch + 1)],
                dw2[l][:, 2 * D * ch:2 * D * (ch + 1)]),
                "w2", waits=waits if ch == 0 else [])
        return t

    def build_b1b(l, waits):
        cb = gp(lambda e, l=l: e.dma_start(b1b[0:1, 0:FF2], db1[l]), "b1", waits=waits)
        ev = None
        for ch in range((FF2 + 511) // 512):
            w0 = 512 * ch
            w1c = min(512, FF2 - w0)
            pp = pA if ch % 2 == 0 else pB
            bm = pe(lambda e, pp=pp, w0=w0, w1c=w1c: e.matmul(
                pp[:, 0:w1c], onesr[0:1, :], b1b[0:1, w0:w0 + w1c],
                start=True, stop=True), waits=[cb] if ch == 0 else [])
            ev = dve(lambda e, pp=pp, w0=w0, w1c=w1c: e.tensor_copy(
                b1b[:, w0:w0 + w1c], pp[:, 0:w1c]), waits=[bm])
            pe(lambda e: e.nop(), waits=[ev])
        return ev

    war = {"xin": war_xin, "xbf": [None, None], "xbsh": [None, None],
           "mask": [None, None], "rope": [None, None],
           "xo": [None, None], "m_sb": None, "mload": None, "xsh": [None, None],
           "diag": None, "lnTf1b": None, "lnTst": None, "rq8": None,
           "pD2": None, "lsum": [None, None, None, None]}

    def attn_tile(l, j, src, dst, extra_w):
        sl = j % NLD
        r0 = (1 + j) * 128

        def lw(key):
            return tokl(war[key][sl])

        ld1 = sp(lambda e, sl=sl, r0=r0, src=src: e.dma_start(
            xin[:, sl * D:(sl + 1) * D], src[r0:r0 + 128, :]),
            "xin%d" % sl, waits=lw("xin"))
        ld2 = gp(lambda e, sl=sl, r0=r0, src=src: e.dma_start(
            xbf[sl], src[r0:r0 + 128, :]), "xbf%d" % sl, waits=lw("xbf"))
        ld3 = gp(lambda e, sl=sl, r0=r0, src=src: e.dma_start(
            xbsh[sl], src[r0 - 1:r0 + 127, 512:1024]),
            "xbsh%d" % sl, waits=lw("xbsh"))
        ld4 = sp(lambda e, sl=sl, j=j: e.dma_start(
            maskt[sl], dmask[j * 128:(j + 1) * 128, :]),
            "mask%d" % sl, waits=lw("mask"))
        ld5 = sp(lambda e, sl=sl, j=j: e.dma_start(
            ropet[sl], drope[j * 128:(j + 1) * 128, :]),
            "rope%d" % sl, waits=lw("rope"))

        def xs_blk(i, sl=sl):
            if i < 4:
                return xbf[sl][:, 128 * i:128 * (i + 1)]
            return xbsh[sl][:, 128 * (i - 4):128 * (i - 3)]
        tv = tbatch([lambda i=i: xs_blk(i) for i in range(8)],
                    lambda i0, n: lnT[:, 128 * i0:128 * (i0 + n)]
                    .rearrange("p (a w) -> p a w", a=n),
                    pewaits=[ld2, ld3] + extra_w)
        war["xbf"][sl] = tv
        war["xbsh"][sl] = tv
        evs = []
        for ch in range(6):
            pp = pA if ch % 2 == 0 else pB
            for k in range(8):
                mm = pe(lambda e, pp=pp, ch=ch, k=k: e.matmul(
                    pp[:, :], lnT[:, 128 * k:128 * (k + 1)],
                    wqkv[:, 3 * D * k + 512 * ch: 3 * D * k + 512 * (ch + 1)],
                    start=(k == 0), stop=(k == 7)),
                    waits=[tv] if (ch == 0 and k == 0) else [])
            if ch < 2:
                ev = act(lambda e, pp=pp, ch=ch: e.copy(
                    q_sb[:, 512 * ch:512 * (ch + 1)], pp[:, :]), waits=[mm])
            elif ch < 4:
                ev = act(lambda e, pp=pp, ch=ch: e.copy(
                    k_sb[:, 512 * (ch - 2):512 * (ch - 1)], pp[:, :]), waits=[mm])
            else:
                ev = act(lambda e, pp=pp, ch=ch, j=j: e.copy(
                    vring[:, D * (j % 3) + 512 * (ch - 4): D * (j % 3) + 512 * (ch - 3)],
                    pp[:, :]), waits=[mm])
            evs.append(ev)
            pe(lambda e: e.nop(), waits=[ev])
        qev, kev = evs[1], evs[3]
        ro = ropet[sl]
        sqa = act(lambda e: e.activation(scrD[:, :], q_sb[:, :], AF.Square),
                  waits=[qev])
        qred = dve(lambda e: e.tensor_reduce(
            small[:, 24:24 + H], scrD.ap().rearrange("p (h d) -> p h d", h=H),
            AX.X, ALU.add), waits=[sqa], drain=True)
        dve(lambda e: e.tensor_scalar(small[:, 24:24 + H], small[:, 24:24 + H],
                                      1e-24, None, ALU.add), drain=True)
        newton_rsqrt(small[:, 56:56 + H], small[:, 24:24 + H], width=H)
        rq = small[:, 56:56 + H]
        qr8 = dve(lambda e: e.tensor_scalar(rq8[:, :], rq, 8.0, None, ALU.mult),
                  waits=[war["rq8"]] if war["rq8"] else [], drain=True)
        ska = act(lambda e: e.activation(scrD[:, :], k_sb[:, :], AF.Square),
                  waits=[kev, qred])
        dve(lambda e: e.tensor_reduce(
            small[:, 24:24 + H], scrD.ap().rearrange("p (h d) -> p h d", h=H),
            AX.X, ALU.add), waits=[ska], drain=True)
        dve(lambda e: e.tensor_scalar(small[:, 24:24 + H], small[:, 24:24 + H],
                                      1e-24, None, ALU.add), drain=True)
        newton_rsqrt(small[:, 40:40 + H], small[:, 24:24 + H], width=H)
        rk = small[:, 40:40 + H]
        qsl = qsc[:, DH * l:DH * (l + 1)]
        ksl = ksc[:, DH * l:DH * (l + 1)]
        qslr = qscr[:, DH * l:DH * (l + 1)]
        kslr = kscr[:, DH * l:DH * (l + 1)]
        t1 = dve(lambda e: e.tensor_tensor(scrC[:, 0:64], qsl, ro[:, 0:64],
                                           ALU.mult), waits=[ld5], drain=True)
        t2 = dve(lambda e: e.tensor_tensor(scrC[:, 64:128], qslr, ro[:, 64:128],
                                           ALU.mult), drain=True)
        t3 = dve(lambda e: e.tensor_tensor(scrC[:, 128:192], ksl, ro[:, 128:192],
                                           ALU.mult), drain=True)
        t4 = dve(lambda e: e.tensor_tensor(scrC[:, 192:256], kslr, ro[:, 192:256],
                                           ALU.mult), drain=True)
        war["rope"][sl] = t4

        def reh(x):
            return x.rearrange("p (h d) -> p h d", h=H)

        def reh2(x):
            return x.rearrange("p (h t d) -> p h t d", h=H, t=2)

        def reA(x):
            return x.rearrange("p (h d) -> p h d", h=H)
        dve(lambda e: e.tensor_scalar(reh2(rotb)[:, :, 0, :],
                                      reh2(q_sb)[:, :, 1, :], -1.0, None,
                                      ALU.mult), drain=True)
        dve(lambda e: e.tensor_copy(reh2(rotb)[:, :, 1, :],
                                    reh2(q_sb)[:, :, 0, :]), drain=True)
        dve(lambda e: e.tensor_tensor(reh(q_sb), reh(q_sb),
                                      brd(scrC[:, 0:64], H, 64), ALU.mult),
            waits=[t1], drain=True)
        dve(lambda e: e.tensor_tensor(reh(rotb), reh(rotb),
                                      brd(scrC[:, 64:128], H, 64), ALU.mult),
            waits=[t2], drain=True)
        qpv = dve(lambda e: e.tensor_tensor(qp[:, :], q_sb[:, :], rotb[:, :],
                                            ALU.add), drain=True)
        dve(lambda e: e.tensor_scalar(reh2(rotb)[:, :, 0, :],
                                      reh2(k_sb)[:, :, 1, :], -1.0, None,
                                      ALU.mult), drain=True)
        dve(lambda e: e.tensor_copy(reh2(rotb)[:, :, 1, :],
                                    reh2(k_sb)[:, :, 0, :]), drain=True)
        dve(lambda e: e.tensor_tensor(reh(k_sb), reh(k_sb),
                                      brd(scrC[:, 128:192], H, 64), ALU.mult),
            waits=[t3], drain=True)
        dve(lambda e: e.tensor_tensor(reh(rotb), reh(rotb),
                                      brd(scrC[:, 192:256], H, 64), ALU.mult),
            waits=[t4], drain=True)
        dve(lambda e: e.tensor_tensor(k_sb[:, :], k_sb[:, :], rotb[:, :],
                                      ALU.add), drain=True)
        kpv = dve(lambda e: e.tensor_tensor(
            reA(kp), reh(k_sb),
            rk.rearrange("p (h d) -> p h d", d=1).broadcast_to([128, H, DH]),
            ALU.mult), drain=True)
        tbatch([lambda hh=hh: qp[:, 64 * hh:64 * (hh + 1)] for hh in range(H)],
               lambda i0, n: qpT[0:64, 128 * i0:128 * (i0 + n)]
               .rearrange("p (a w) -> p a w", a=n), rows=64, pewaits=[qpv])
        tq0 = tbatch([lambda hh=hh: kp[:, 64 * hh:64 * (hh + 1)] for hh in range(H)],
                     lambda i0, n, j=j: kTr[0:64, 2048 * (j % 3) + 128 * i0:
                                            2048 * (j % 3) + 128 * (i0 + n)]
                     .rearrange("p (a w) -> p a w", a=n), rows=64, pewaits=[kpv])
        tq = dve(lambda e: e.tensor_scalar(qppT[0:64, :], qpT[0:64, :],
                                           b2col[:, 0:1], None, ALU.mult),
                 waits=[tq0], drain=True)
        even = (j % 2 == 0)
        mk = maskt[sl]
        prev_pacts = None
        prev_ptm = None
        pvm = None
        for grp in range(4):
            sw0 = [tq] if grp == 0 else list(prev_pacts)
            smm = None
            for ck in range(3):
                slot = (j - 2 + ck) % 3
                usepp = (ck < 2) if even else (ck < 1)
                src_qT = qppT if usepp else qpT
                for hh in range(4):
                    h0 = grp * 4 + hh
                    smm = pe(lambda e, hh=hh, h0=h0, ck=ck, slot=slot, src_qT=src_qT: e.matmul(
                        pS[:, 512 * hh + 128 * ck: 512 * hh + 128 * (ck + 1)],
                        src_qT[0:64, 128 * h0:128 * (h0 + 1)],
                        kTr[0:64, 2048 * slot + 128 * h0: 2048 * slot + 128 * (h0 + 1)],
                        start=True, stop=True),
                        waits=sw0 if (ck == 0 and hh == 0) else [])

            def rps():
                return pS.ap().rearrange("p (h w) -> p h w", h=4)
            dve(lambda e, mk=mk: e.tensor_tensor(
                rps()[:, :, 0:256], rps()[:, :, 0:256],
                brd(mk[:, 0:256], 4, 256), ALU.add), waits=[smm, ld4], drain=True)
            ms2 = dve(lambda e, mk=mk: e.tensor_tensor(
                rps()[:, :, 256:384], rps()[:, :, 256:384],
                brd(mk[:, 256:384], 4, 128), ALU.add), drain=True)
            war["mask"][sl] = ms2
            mx4 = small[:, 104:108]
            dve(lambda e: e.reduce_max(mx4, rps()[:, :, 0:384], AX.X),
                waits=[ms2], drain=True)
            bw = [qr8]
            if prev_pacts is not None:
                bw.append(prev_pacts[-1])
            bi4 = dve(lambda e, grp=grp: e.scalar_tensor_tensor(
                small[:, 108:112], mx4, -8.0, rq[:, 4 * grp:4 * (grp + 1)],
                ALU.mult, ALU.mult), waits=bw, drain=True)
            pacts = []
            for hh in range(4):
                h0 = grp * 4 + hh
                aw = [bi4] if hh == 0 else []
                if hh == 0 and prev_ptm is not None:
                    aw.append(prev_ptm)
                if hh == 0 and war["lsum"][grp] is not None:
                    aw.append(war["lsum"][grp])
                pacts.append(act(lambda e, hh=hh, h0=h0: e.activation(
                    P_sb[:, 384 * hh:384 * (hh + 1)],
                    pS[:, 512 * hh:512 * hh + 384], AF.Exp,
                    bias=small[:, 108 + hh:109 + hh], scale=rq8[:, h0:h0 + 1],
                    accum_out=lsum[:, h0:h0 + 1]), waits=aw))
            prev_pacts = pacts
            war["rq8"] = pacts[-1]
            rcp = dve(lambda e, grp=grp: e.reciprocal(
                rinvb[:, 4 * grp:4 * (grp + 1)], lsum[:, 4 * grp:4 * (grp + 1)]),
                waits=list(pacts), drain=True)
            war["lsum"][grp] = rcp
            pw = [pacts[3]] + tokl(war["diag"])
            for hh in range(4):
                ppt = (pA if hh < 2 else pB).ap().bitcast(BF16)
                base = 384 * (hh % 2)
                for ck in range(3):
                    ptm = pe(lambda e, ppt=ppt, base=base, hh=hh, ck=ck: e.transpose(
                        ppt[:, base + 128 * ck: base + 128 * (ck + 1)],
                        P_sb[:, 384 * hh + 128 * ck: 384 * hh + 128 * (ck + 1)],
                        ident[:, :]),
                        waits=pw if (hh == 0 and ck == 0) else [])
            prev_ptm = ptm
            pev0 = act(lambda e: e.copy(
                PT_sb[:, 0:768], pA.ap().bitcast(BF16)[:, 0:768]), waits=[ptm])
            pev1 = act(lambda e: e.copy(
                PT_sb[:, 768:1536], pB.ap().bitcast(BF16)[:, 0:768]))
            war["diag"] = [pev0, pev1]
            for hh in range(4):
                h0 = grp * 4 + hh
                pcol = 64 * (8 * (h0 % 2) + h0 // 2)
                for ck in range(3):
                    slot = (j - 2 + ck) % 3
                    pvm = pe(lambda e, hh=hh, h0=h0, ck=ck, slot=slot, pcol=pcol: e.matmul(
                        pD[:, pcol:pcol + 64],
                        PT_sb[:, 384 * hh + 128 * ck: 384 * hh + 128 * (ck + 1)],
                        vring[:, D * slot + 64 * h0: D * slot + 64 * (h0 + 1)],
                        start=(ck == 0), stop=(ck == 2)),
                        waits=[pev0, pev1] if (hh == 0 and ck == 0) else [])
        rinv4d = rinvb.ap().rearrange("p (a b) -> p a b", b=2) \
            .rearrange("p a (b u) -> p a b u", u=1).broadcast_to([128, 8, 2, DH])
        atv = dve(lambda e: e.tensor_tensor(
            at_sb.rearrange("p (a b d) -> p a b d", a=8, b=2),
            pD.ap().rearrange("p (b a d) -> p a b d", b=2, a=8),
            rinv4d, ALU.mult), waits=[pvm], drain=True)
        tv2 = tbatch([lambda i=i: at_sb[:, 128 * i:128 * (i + 1)] for i in range(8)],
                     lambda i0, n: aT_sb[:, 128 * i0:128 * (i0 + n)]
                     .rearrange("p (a w) -> p a w", a=n), pewaits=[atv])
        for ch in range(2):
            for k in range(8):
                wm = pe(lambda e, ch=ch, k=k: e.matmul(
                    pD[:, 512 * ch:512 * (ch + 1)], aT_sb[:, 128 * k:128 * (k + 1)],
                    wo[:, D * k + 512 * ch: D * k + 512 * (ch + 1)],
                    start=(k == 0), stop=(k == 7)),
                    waits=[tv2, atv] if (ch == 0 and k == 0) else [])
        xo = xo_sb[:, sl * D:(sl + 1) * D]
        xov = dve(lambda e, xo=xo, sl=sl: e.tensor_tensor(
            xo, pD[:, :], xin[:, sl * D:(sl + 1) * D], ALU.add),
            waits=[wm, ld1] + lw("xo"), drain=True)
        war["xin"][sl] = xov
        if datt is not None and l == 0:
            gp(lambda e, j=j: e.dma_start(datt[j * 128:(j + 1) * 128, :],
                                          at_sb[:, :]), "st_att", waits=[atv])
        stv = gp(lambda e, xo=xo, r0=r0, dst=dst: e.dma_start(
            dst[r0:r0 + 128, :], xo), "st_x%d" % sl, waits=[xov])
        war["xo"][sl] = stv
        return stv

    def f1a_tile(l, j, src, extra_w):
        sl = j % NLD
        r0 = (1 + j) * 128
        ld1 = sp(lambda e, sl=sl, r0=r0, src=src: e.dma_start(
            xin[:, sl * D:(sl + 1) * D], src[r0:r0 + 128, :]),
            "xin%d" % sl, waits=tokl(war["xin"][sl]))
        ld2 = sp(lambda e, sl=sl, r0=r0, src=src: e.dma_start(
            xshf[sl], src[r0 - 1:r0 + 127, 512:1024]),
            "xsh%d" % sl, waits=tokl(war["xsh"][sl]))
        xa = xin[:, sl * D:(sl + 1) * D]
        xb = xshf[sl]
        dve(lambda e, xa=xa: e.reduce_sum(small[:, 0:1], xa[:, 0:512], AX.X),
            waits=[ld1], drain=True)
        dve(lambda e, xb=xb: e.reduce_sum(small[:, 4:5], xb, AX.X),
            waits=[ld2], drain=True)
        dve(lambda e: e.tensor_tensor(small[:, 0:1], small[:, 0:1],
                                      small[:, 4:5], ALU.add), drain=True)
        act(lambda e, xa=xa: e.activation(scrD[:, 0:512], xa[:, 0:512],
                                          AF.Square, accum_out=small[:, 1:2]),
            waits=[ld1])
        a1 = act(lambda e, xb=xb: e.activation(scrD[:, 512:1024], xb, AF.Square,
                                               accum_out=small[:, 5:6]),
                 waits=[ld2])
        dve(lambda e: e.tensor_tensor(small[:, 1:2], small[:, 1:2],
                                      small[:, 5:6], ALU.add), waits=[a1],
            drain=True)
        dve(lambda e: e.tensor_scalar(small[:, 2:3], small[:, 0:1], 1.0 / D,
                                      None, ALU.mult), drain=True)
        dve(lambda e: e.tensor_tensor(small[:, 3:4], small[:, 2:3],
                                      small[:, 2:3], ALU.mult), drain=True)
        dve(lambda e: e.tensor_scalar(small[:, 3:4], small[:, 3:4], -1.0, None,
                                      ALU.mult), drain=True)
        dve(lambda e: e.scalar_tensor_tensor(small[:, 3:4], small[:, 1:2],
                                             1.0 / D, small[:, 3:4], ALU.mult,
                                             ALU.add), drain=True)
        dve(lambda e: e.tensor_scalar(small[:, 3:4], small[:, 3:4], 1e-5, None,
                                      ALU.add), drain=True)
        newton_rsqrt(small[:, 8:9], small[:, 3:4])
        dve(lambda e: e.tensor_tensor(small[:, 9:10], small[:, 2:3],
                                      small[:, 8:9], ALU.mult), drain=True)
        nm = dve(lambda e: e.tensor_scalar(small[:, 9:10], small[:, 9:10], -1.0,
                                           None, ALU.mult), drain=True)
        na = act(lambda e, xa=xa: e.activation(q_sb[:, 0:512], xa[:, 0:512],
                                               AF.Identity, bias=small[:, 9:10],
                                               scale=small[:, 8:9]), waits=[nm])
        nb = act(lambda e, xb=xb: e.activation(q_sb[:, 512:1024], xb, AF.Identity,
                                               bias=small[:, 9:10],
                                               scale=small[:, 8:9]))
        war["xin"][sl] = nb
        war["xsh"][sl] = nb
        tvv = tbatch([lambda i=i: q_sb[:, 128 * i:128 * (i + 1)] for i in range(8)],
                     lambda i0, n: lnT[:, 128 * i0:128 * (i0 + n)]
                     .rearrange("p (a w) -> p a w", a=n),
                     pewaits=[na, nb] + extra_w + tokl(war["lnTst"]),
                     scale2=(lambda i0, n, l=l: lng[:, 8 * l + i0:8 * l + i0 + n],
                             lambda i0, n, l=l: lnb[:, 8 * l + i0:8 * l + i0 + n]))
        stx = gp(lambda e, j=j: e.dma_start(
            dxnT[j * 128:(j + 1) * 128, :], lnT[:, :]), "st_xnT%d" % (j % 2), waits=[tvv])
        war["lnTst"] = stx
        mv = _f1_chunks(j, sl, [tvv], w1a, 3072, 0, 1536,
                        [(0, 512), (512, 512), (1024, 512)])
        stv = gp(lambda e, sl=sl, j=j: e.dma_start(
            dm[j * 128:(j + 1) * 128, 0:1536], m_sb[:, 0:1536]), "st_m%d" % (j % 2), waits=[mv])
        war["m_sb"] = stv
        return stv

    def _f1_chunks(j, sl, lnw, wbuf, wstride, a_off, g_off, chunks):
        mv = None
        for ci, (cw0, wc) in enumerate(chunks):
            def bank(i):
                if i == 0:
                    return pA[:, :]
                if i == 1:
                    return pB[:, :]
                if i < 4:
                    return pS[:, 512 * (i - 2):512 * (i - 1)]
                return pD[:, 512 * (i - 4):512 * (i - 3)]
            ppa, ppg = bank((2 * ci) % 6), bank((2 * ci + 1) % 6)
            for k in range(8):
                mma = pe(lambda e, ppa=ppa, k=k, cw0=cw0, wc=wc: e.matmul(
                    ppa[:, 0:wc], lnT[:, 128 * k:128 * (k + 1)],
                    wbuf[:, wstride * k + cw0: wstride * k + cw0 + wc],
                    start=(k == 0), stop=(k == 7)),
                    waits=lnw if (ci == 0 and k == 0) else [])
            for k in range(8):
                mmg = pe(lambda e, ppg=ppg, k=k, cw0=cw0, wc=wc: e.matmul(
                    ppg[:, 0:wc], lnT[:, 128 * k:128 * (k + 1)],
                    wbuf[:, wstride * k + g_off + cw0: wstride * k + g_off + cw0 + wc],
                    start=(k == 0), stop=(k == 7)))
            bcol = a_off + cw0
            ga = dve(lambda e, ppg=ppg, bcol=bcol, wc=wc: e.tensor_tensor(
                scrA[:, 0:wc], ppg[:, 0:wc], b1b[:, FFH + bcol: FFH + bcol + wc],
                ALU.add), waits=[mmg], drain=True)
            gl = act(lambda e, wc=wc: e.activation(gl_sb[:, 0:wc], scrA[:, 0:wc],
                                                   AF.Gelu), waits=[ga])
            aa = dve(lambda e, ppa=ppa, bcol=bcol, wc=wc: e.tensor_tensor(
                aa_sb[:, 0:wc], ppa[:, 0:wc], b1b[:, bcol:bcol + wc], ALU.add),
                waits=[mma], drain=True)
            mw = [gl, aa]
            if ci == 0 and war["m_sb"]:
                mw.append(war["m_sb"])
            mv = dve(lambda e, bcol=bcol, wc=wc: e.tensor_tensor(
                m_sb[:, bcol:bcol + wc],
                aa_sb[:, 0:wc], gl_sb[:, 0:wc], ALU.mult), waits=mw, drain=True)
            pe(lambda e: e.nop(), waits=[mv])
        return mv

    def f1b_tile(l, j, extra_w):
        sl = j % NLD
        lnw0 = tokl(war["lnTf1b"])
        ldn = sp(lambda e, j=j: e.dma_start(
            lnT[:, :], dxnT[j * 128:(j + 1) * 128, :]), "lnTf", waits=lnw0)
        mv = _f1_chunks(j, sl, [ldn] + extra_w, w1b, 2388, 1536, 1194,
                        [(0, 512), (512, 512), (1024, 170)])
        war["lnTf1b"] = mv
        stv = gp(lambda e, j=j: e.dma_start(
            dm[j * 128:(j + 1) * 128, 1536:2730], m_sb[:, 1536:2730]),
            "st_m%d" % (j % 2), waits=[mv])
        war["m_sb"] = stv
        return stv

    def f2_tile(l, j, src, dst, extra_w, final):
        sl = j % NLD
        r0 = (1 + j) * 128
        ld1 = sp(lambda e, j=j: e.dma_start(
            mload, dm[j * 128:(j + 1) * 128, :]),
            "mload", waits=tokl(war["mload"]))
        ld2 = sp(lambda e, sl=sl, r0=r0, src=src: e.dma_start(
            xin[:, sl * D:(sl + 1) * D], src[r0:r0 + 128, :]),
            "xin%d" % sl, waits=tokl(war["xin"][sl]))
        msone = dve(lambda e: e.memset(mT_sb[:, 21 * 128:22 * 128], 1.0)) \
            if j == 0 else None
        ev = tbatch([lambda k=k: mload[:, 128 * k:128 * (k + 1)] for k in range(20)],
                    lambda i0, n: mT_sb[0:128, 128 * i0:128 * (i0 + n)]
                    .rearrange("p (a w) -> p a w", a=n),
                    pewaits=[ld1] + extra_w + ([msone] if msone else []))
        for k, rows in [(20, 128), (21, FFH - 21 * 128)]:
            ev = tbatch([lambda k=k, rows=rows: mload[:, 128 * k:128 * k + rows]],
                        lambda i0, n, k=k, rows=rows:
                        mT_sb[0:rows, 128 * k:128 * (k + 1)]
                        .rearrange("p (a w) -> p a w", a=1),
                        rows=rows, group=1)
        war["mload"] = ev
        for ch in range(2):
            for k in range(22):
                rows = 128 if k < 21 else (FFH - 21 * 128 + 1)
                mm = pe(lambda e, ch=ch, k=k, rows=rows: e.matmul(
                    pD[:, 512 * ch:512 * (ch + 1)],
                    mT_sb[0:rows, 128 * k:128 * k + 128],
                    w2[0:rows, D * k + 512 * ch: D * k + 512 * (ch + 1)],
                    start=(k == 0), stop=(k == 21)),
                    waits=[ev] + tokl(war["pD2"]) if (ch == 0 and k == 0) else [])
        xo = xo_sb[:, sl * D:(sl + 1) * D]
        if not final:
            xov = dve(lambda e, xo=xo, sl=sl: e.tensor_tensor(
                xo, pD[:, :], xin[:, sl * D:(sl + 1) * D], ALU.add),
                waits=[mm, ld2] + ([war["xo"][sl]] if war["xo"][sl] else []),
                drain=True)
            war["xin"][sl] = xov
            war["pD2"] = xov
            stv = gp(lambda e, xo=xo, r0=r0, dst=dst: e.dma_start(
                dst[r0:r0 + 128, :], xo), "st_x%d" % sl, waits=[xov])
        else:
            x12 = dve(lambda e, sl=sl: e.tensor_tensor(
                scrB[:, :], pD[:, :], xin[:, sl * D:(sl + 1) * D], ALU.add),
                waits=[mm, ld2], drain=True)
            war["xin"][sl] = x12
            war["pD2"] = x12
            ld3 = sp(lambda e, j=j: e.dma_start(
                scrA[:, :], dx1[j * 128:(j + 1) * 128, :]), "x1ld", waits=[x12])
            lnd = ln_full(scrB[:, :], [x12])
            lno = act(lambda e: e.activation(scrB[:, :], scrB[:, :], AF.Identity,
                                             bias=small[:, 9:10],
                                             scale=small[:, 8:9]), waits=[lnd])
            xov = dve(lambda e, xo=xo: e.tensor_tensor(
                xo, scrB[:, :], scrA[:, :], ALU.add),
                waits=[lno, ld3] + ([war["xo"][sl]] if war["xo"][sl] else []),
                drain=True)
            stv = gp(lambda e, xo=xo, j=j: e.dma_start(
                dout[j * 128:(j + 1) * 128, :], xo), "st_out%d" % (j % 2), waits=[xov])
        war["xo"][sl] = stv
        return stv

    SLOTTED = {"xin", "xbf", "xbsh", "mask", "rope", "xsh", "xo", "lsum"}

    def reset_war(keys, toks):
        for k in keys:
            if k in SLOTTED:
                war[k] = [list(toks) for _ in war[k]]
            else:
                war[k] = list(toks)

    pa_done = snap("st_f0", "st_f1", "st_xone0", "st_xone1", "st_s0", "st_s1")
    if getattr(cfg, "only_pa", False):
        for j in range(NW):
            sl = j % NLD
            ldx1 = sp(lambda e, j=j, sl=sl: e.dma_start(
                xin[:, sl * D:(sl + 1) * D], dx1[j * 128:(j + 1) * 128, :]),
                "xin%d" % sl, waits=tokl(war["xin"][sl]) + (pa_done if j < 2 else []))
            cp = dve(lambda e, sl=sl: e.tensor_copy(xo_sb[:, sl * D:(sl + 1) * D],
                                                    xin[:, sl * D:(sl + 1) * D]),
                     waits=[ldx1] + tokl(war["xo"][sl]), drain=True)
            war["xin"][sl] = cp
            stv = gp(lambda e, j=j, sl=sl: e.dma_start(
                dout[j * 128:(j + 1) * 128, :], xo_sb[:, sl * D:(sl + 1) * D]),
                "st_out%d" % (j % 2), waits=[cp])
            war["xo"][sl] = stv
        allw = [(ch, ch.n) for name, ch in P.chans.items()
                if name.startswith("d_") and ch.n > 0]
        P.op("sp", lambda e: None, waits=allw)
        P.op("gp", lambda e: None, waits=allw)
        with stack:
            P.emit(nc, stack)
        return nc
    wq_ready = load_weights_attn(0, pa_done)
    w1a_ready = load_weights_w1a(0, pa_done)
    src, dst = dxA, dxB
    barrier = pa_done
    SWEEPS = getattr(cfg, "sweeps", ("attn", "f1a", "f1b", "f2"))
    for l in range(DEPTH):
        if "attn" in SWEEPS:
            reset_war(["xbf", "xbsh", "mask", "rope", "xin", "xo"], barrier)
            for j in range(NW):
                attn_tile(l, j, src, dst, [wq_ready] if j == 0 else [])
            barrier = snap("st_x0", "st_x1")
        src, dst = dst, src
        b1_ready = build_b1b(l, barrier)
        w1b_ready = load_weights_w1b(l, barrier)
        if "f1a" in SWEEPS:
            reset_war(["xsh", "xin"], barrier)
            for j in range(NW):
                f1a_tile(l, j, src, [w1a_ready, b1_ready] if j == 0 else [])
            barrier = snap("st_x0", "st_x1", "st_m0", "st_m1", "st_xnT0", "st_xnT1")
        w2_ready = load_weights_w2(l, barrier)
        if "f1b" in SWEEPS:
            reset_war(["lnTf1b"], barrier)
            for j in range(NW):
                f1b_tile(l, j, [w1b_ready] if j == 0 else [])
            barrier = snap("st_x0", "st_x1", "st_m0", "st_m1", "st_xnT0", "st_xnT1")
        if l + 1 < DEPTH:
            wq_ready = load_weights_attn(l + 1, barrier)
        final = (l == DEPTH - 1)
        if "f2" in SWEEPS:
            reset_war(["mload", "xin"], barrier)
            for j in range(NW):
                f2_tile(l, j, src, dst, [w2_ready] if j == 0 else [], final)
            barrier = snap("st_x0", "st_x1", "st_out0", "st_out1")
        if l + 1 < DEPTH:
            w1a_ready = load_weights_w1a(l + 1, barrier)
        src, dst = dst, src
    if "f2" not in SWEEPS or "attn" not in SWEEPS:
        z2 = dve(lambda e: e.memset(xo_sb[:, 0:D], 0.0), waits=barrier)
        for j in range(NW):
            gp(lambda e, j=j: e.dma_start(dout[j * 128:(j + 1) * 128, :],
                                          xo_sb[:, 0:D]), "st_out%d" % (j % 2),
               waits=[z2])

    allw = [(ch, ch.n) for name, ch in P.chans.items()
            if name.startswith("d_") and ch.n > 0]
    P.op("sp", lambda e: None, waits=allw)
    P.op("gp", lambda e: None, waits=allw)

    with stack:
        P.emit(nc, stack)
    return nc



def kernel(**inputs):
    cfg = CFG()
    nc = build_program(cfg)
    in_maps = [stage_core(inputs, cfg, c // 2, c % 2) for c in range(cfg.NC)]
    res = run_bass_kernel_spmd(nc, in_maps, core_ids=list(range(cfg.NC)))
    out = np.zeros((cfg.NB, cfg.T, cfg.D), dtype=np.float32)
    nh0 = cfg.NSWEEP * 128
    vs1 = (cfg.NSWEEP - cfg.ROT) * 128
    for c in range(cfg.NC):
        b, h = c // 2, c % 2
        o = np.asarray(res.results[c]["out"])
        if h == 0:
            out[b, 0:nh0] = o
        else:
            out[b, nh0:] = o[vs1:]
    return out
